# version 09
# baseline (speedup 1.0000x reference)
import numpy as np
import concourse.bass as bass
import concourse.mybir as mybir
from concourse.bass_utils import run_bass_kernel_spmd

F32 = mybir.dt.float32
BF16 = mybir.dt.bfloat16
AF = mybir.ActivationFunctionType
ALU = mybir.AluOpType
AX = mybir.AxisListType

NCORE = 8
B = 2
T = 8192
D = 1024
SH = 2048
KSS = 4
NDT = D // 128
EPS_RMS = 1e-6
EPS_LN = 1e-5

_DEBUG = False


class Prog:

    def __init__(self):
        self.nc = bass.Bass("TRN2", target_bir_lowering=False)
        self._ctx = []
        self._sems = {}
        self._bank_req = {}
        self._scopes = []
        self._part = ""
        self._nbar = 0

    def ent(self, cm):
        if self._scopes:
            self._scopes[-1].append(cm)
        else:
            self._ctx.append(cm)
        return cm.__enter__()

    def sb(self, name, shape, dt=F32):
        return self.ent(self.nc.sbuf_tensor(self._part + name, list(shape), dt))

    def psum(self, name, parts=128):
        return self.ent(self.nc.psum_tensor(self._part + name, [parts, 512], F32))

    def push_scope(self, part):
        self._part = part + "_"
        self._scopes.append([])

    def pop_scope(self):
        for cm in reversed(self._scopes.pop()):
            cm.__exit__(None, None, None)
        self._part = ""

    def barrier(self):
        self._nbar += 1
        pb = self.sem("pbar")
        target = 5 * self._nbar
        blk = self.block
        for section in (blk.tensor, blk.vector, blk.scalar, blk.gpsimd, blk.sync):
            def b(e):
                self.inc(e.nop(), "pbar", 1)
                e.wait_ge(pb[0], target)
            section(b)
        pool = getattr(self, "_sem_pool", None)
        if pool is None:
            pool = self._sem_pool = []
        for nm in [n for n in self._sems if n != "pbar"]:
            h, cnt = self._sems.pop(nm)
            pool.append((h, cnt))
        self._bank_req = {}

    def begin(self):
        self.block = self.ent(self.nc.Block())

    def close(self):
        for c in reversed(self._ctx):
            c.__exit__(None, None, None)
        return self.nc

    def sem(self, name):
        if name not in self._sems:
            if getattr(self, "_sem_pool", None):
                h, base = self._sem_pool.pop()
            else:
                cm = self.nc.semaphore(name)
                self._ctx.append(cm)
                h = cm.__enter__()
                base = 0
            self._sems[name] = [h, base]
        return self._sems[name]

    def inc(self, instr, name, k=1):
        s = self.sem(name)
        instr.then_inc(s[0], k)
        s[1] += k
        return instr

    def val(self, name):
        return self.sem(name)[1]

    def wait(self, eng, name, value=None):
        s = self.sem(name)
        v = s[1] if value is None else value
        if v > 0:
            eng.wait_ge(s[0], v)

    def bank_acquire(self, eng, bank):
        req = self._bank_req.get(bank)
        if req is not None:
            name, cnt = req
            s = self.sem(name)
            eng.wait_ge(s[0], cnt)
            self._bank_req[bank] = None

    def edge(self, eng, instr, name="vsy"):
        self.inc(instr, name, 1)
        self.wait(eng, name)

    def bank_release(self, bank, name):
        self._bank_req[bank] = (name, self.val(name))



def emit_A(p, xT, w_gate, w_stin, w_stout, w_z36, w_z2, b_z2, w_qkvg, wgb_col,
           conv_w, gate_b, logit_bc, o_signs, o_glr, o_gmr, o_qkvg, o_zgpc,
           o_long, o_finT):
    nc = p.nc
    SLAB = 256
    NSL = SH // SLAB
    HCO = SLAB + 3

    sgw = p.sb("sgw", [128, NDT * D], BF16)
    sstin = p.sb("sstin", [128, NDT * KSS * D], BF16)
    sstout = p.sb("sstout", [128, 32 * D], BF16)
    sz36 = p.sb("sz36", [128, NDT * 36], BF16)
    sz2 = p.sb("sz2", [128, NDT * 2], BF16)
    sqkvg = p.sb("sqkvg", [128, NDT * 97], BF16)
    sconv = p.sb("sconv", [128, NDT * 4], F32)
    sgateb = p.sb("sgateb", [128, NDT], F32)
    sbz2 = p.sb("sbz2", [2, 1], F32)
    swgb = p.sb("swgb", [128, 1], F32)
    slogit = p.sb("slogit", [128, KSS], F32)
    sdecay = p.sb("sdecay", [128, KSS], F32)

    sx = p.sb("sx", [128, NDT * HCO], F32)
    sx2 = p.sb("sx2", [128, NDT * HCO], BF16)
    sxn = p.sb("sxn", [128, NDT * HCO], BF16)
    srstd = p.sb("srstd", [1, HCO], F32)
    sh_ = p.sb("sh_", [128, NDT * SLAB], BF16)
    ssig = p.sb("ssig", [128, SLAB], BF16)
    ssil = p.sb("ssil", [128, SLAB], BF16)
    su = p.sb("su", [128, NDT * SLAB], BF16)
    ssgn = p.sb("ssgn", [36, SLAB], F32)
    sglgm = p.sb("sglgm", [2, SLAB], F32)
    sqk = p.sb("sqk", [128, 2 * 97], F32)
    sinp = p.sb("sinp", [128, 2 * SLAB], F32)
    ssc = p.sb("ssc", [128, 2 * SLAB], F32)
    ss = p.sb("ss", [128, 32 * SLAB], BF16)
    scar = p.sb("scar", [128, 32], F32)
    szgpc = p.sb("szgpc", [128, 2 * NSL], F32)
    sfint = p.sb("sfint", [32, 128], F32)
    seyef = p.sb("seyef", [128, 128], F32)
    slong = p.sb("slong", [128, NDT * SLAB], F32)
    sones = p.sb("sones", [128, 1], BF16)
    sonesf = p.sb("sonesf", [1, 128], F32)

    pa = p.psum("pa")
    pb = p.psum("pb")
    pc = p.psum("pc")
    pd = p.psum("pd")
    pl = [p.psum(f"pl{i}") for i in range(4)]

    def plong(i):
        return pl[i // 2][:, (i % 2) * 256:(i % 2) * 256 + 256]

    block = p.block

    def loads(g):
        p.inc(g.dma_start(out=sgw.rearrange("p (d n) -> p d n", d=NDT),
                          in_=w_gate.rearrange("(d p) n -> p d n", p=128)), "wload", 16)
        p.inc(g.dma_start(out=sstin.rearrange("p (d n) -> p d n", d=NDT),
                          in_=w_stin.rearrange("(d p) n -> p d n", p=128)), "wload", 16)
        p.inc(g.dma_start(out=sstout.rearrange("p (c n) -> p c n", c=32),
                          in_=w_stout.rearrange("(c p) n -> p c n", p=128)), "wload", 16)
        p.inc(g.dma_start(out=sz36.rearrange("p (d n) -> p d n", d=NDT),
                          in_=w_z36.rearrange("(d p) n -> p d n", p=128)), "wload", 16)
        p.inc(g.dma_start(out=sz2.rearrange("p (d n) -> p d n", d=NDT),
                          in_=w_z2.rearrange("(d p) n -> p d n", p=128)), "wload", 16)
        p.inc(g.dma_start(out=sqkvg.rearrange("p (d n) -> p d n", d=NDT),
                          in_=w_qkvg.rearrange("(d p) n -> p d n", p=128)), "wload", 16)
        p.inc(g.dma_start(out=sconv[:, :], in_=conv_w), "wload", 16)
        p.inc(g.dma_start(out=sgateb[:, :], in_=gate_b), "wload", 16)
        p.inc(g.dma_start(out=sbz2[:, :], in_=b_z2), "wload", 16)
        p.inc(g.dma_start(out=swgb[:, :], in_=wgb_col), "wload", 16)
        p.inc(g.dma_start(out=slogit[:, :], in_=logit_bc), "wload", 16)

    block.gpsimd(loads)

    def init_g(g):
        p.inc(g.iota(seyef[:, :], pattern=[[1, 128]], base=0, channel_multiplier=-1,
                     allow_small_or_imprecise_dtypes=True), "iog", 1)

    block.gpsimd(init_g)

    def init_v(v):
        v.memset(sones[:, :], 1.0)
        v.memset(sonesf[:, :], 1.0)
        p.wait(v, "iog")
        v.tensor_single_scalar(seyef[:, :], seyef[:, :], 0.0, ALU.is_equal)
        p.inc(v.memset(scar[:, :], 0.0), "init", 1)

    block.vector(init_v)

    def init_s(s):
        p.wait(s, "wload")
        p.inc(s.activation(sdecay[:, :], slogit[:, :], AF.Sigmoid), "init", 1)

    block.scalar(init_s)

    for sl in range(NSL):
        col0 = sl * SLAB

        def ld_x(g, sl=sl, col0=col0):
            if sl > 0:
                p.wait(g, "xnrd")
            p.inc(g.dma_start(
                out=sx.rearrange("p (d n) -> p d n", d=NDT),
                in_=xT[:, col0:col0 + HCO].rearrange("(d p) n -> p d n", p=128)), "xload", 16)

        block.sync(ld_x)

        def sq(s, sl=sl):
            p.wait(s, "xload")
            p.inc(s.activation(sx2[:, :], sx[:, :], AF.Square), "sq", 1)

        block.scalar(sq)

        def statmm(t, sl=sl):
            p.wait(t, "sq")
            p.wait(t, "init", 2)
            p.bank_acquire(t, "pa")
            for i in range(NDT):
                ins = t.matmul(pa[0:1, 0:HCO], sones[:, :],
                               sx2[:, i * HCO:(i + 1) * HCO],
                               start=(i == 0), stop=(i == NDT - 1))
            p.inc(ins, "stat", 1)

        block.tensor(statmm)

        def rstd_row(v):
            p.wait(v, "stat")
            p.inc(v.tensor_scalar(srstd[:, :], pa[0:1, 0:HCO], 1.0 / D, EPS_RMS,
                                  ALU.mult, ALU.add), "statrd", 1)
        block.vector(rstd_row)
        p.bank_release("pa", "statrd")

        def rstd_sqrt(s):
            p.wait(s, "statrd")
            p.edge(s, s.activation(srstd[:, :], srstd[:, :], AF.Ln), "ssy")
            p.inc(s.activation(srstd[:, :], srstd[:, :], AF.Exp, scale=-0.5),
                  "rrec", 1)

        block.scalar(rstd_sqrt)

        def bcmm(t):
            p.wait(t, "rrec")
            p.bank_acquire(t, "pb")
            p.inc(t.matmul(pb[:, 0:HCO], sonesf[:, :], srstd[:, :],
                           start=True, stop=True), "bc", 1)

        block.tensor(bcmm)

        def mk_xn(v):
            p.wait(v, "bc")
            for i in range(NDT):
                ins = v.tensor_mul(sxn[:, i * HCO:(i + 1) * HCO],
                                   sx[:, i * HCO:(i + 1) * HCO],
                                   pb[:, 0:HCO])
            p.inc(ins, "xnrd", 1)

        block.vector(mk_xn)
        p.bank_release("pb", "xnrd")

        def mk_conv(v):
            for i in range(NDT):
                hseg = sh_[:, i * SLAB:(i + 1) * SLAB]
                xnseg = sxn[:, i * HCO:(i + 1) * HCO]
                v.tensor_scalar(hseg, xnseg[:, 0:SLAB], sconv[:, 4 * i:4 * i + 1],
                                None, ALU.mult)
                for j in range(1, 4):
                    ins = v.scalar_tensor_tensor(hseg, xnseg[:, j:SLAB + j],
                                                 sconv[:, 4 * i + j:4 * i + j + 1],
                                                 hseg, ALU.mult, ALU.add)
            p.inc(ins, "conv", 1)

        block.vector(mk_conv)

        for i in range(NDT):
            bank = ["pc", "pd"][i % 2]
            pgb = {"pc": pc, "pd": pd}[bank]

            def gate_mm(t, i=i, bank=bank, pgb=pgb):
                if i == 0:
                    p.wait(t, "xnrd")
                    p.wait(t, "wload")
                p.bank_acquire(t, bank)
                for d in range(NDT):
                    ins = t.matmul(pgb[:, 0:SLAB],
                                   sgw[:, d * D + i * 128:d * D + (i + 1) * 128],
                                   sxn[:, d * HCO + 3:d * HCO + HCO],
                                   start=(d == 0), stop=(d == NDT - 1))
                p.inc(ins, "gmm", 1)

            block.tensor(gate_mm)

            def gate_ep_s(s, i=i, pgb=pgb):
                p.wait(s, "gmm")
                if i == 0:
                    p.wait(s, "conv")
                p.wait(s, "u", p.val("u"))
                s.activation(ssig[:, :], pgb[:, 0:SLAB],
                             AF.Sigmoid, bias=sgateb[:, i:i + 1])
                p.inc(s.activation(ssil[:, :], sh_[:, i * SLAB:(i + 1) * SLAB],
                                   AF.Sigmoid), "gsig", 1)

            block.scalar(gate_ep_s)
            p.bank_release(bank, "gsig")

            def gate_u(v, i=i):
                p.wait(v, "gsig")
                p.edge(v, v.tensor_mul(ssil[:, :], ssil[:, :],
                                       sh_[:, i * SLAB:(i + 1) * SLAB]))
                p.inc(v.tensor_mul(su[:, i * SLAB:(i + 1) * SLAB], ssig[:, :], ssil[:, :]),
                      "u", 1)

            block.vector(gate_u)

        def zmm(t):
            p.wait(t, "u")
            p.bank_acquire(t, "pa")
            p.bank_acquire(t, "pb")
            for d in range(NDT):
                t.matmul(pa[0:36, 0:SLAB], sz36[:, d * 36:(d + 1) * 36],
                         su[:, d * SLAB:(d + 1) * SLAB],
                         start=(d == 0), stop=(d == NDT - 1))
            for d in range(NDT):
                ins = t.matmul(pb[0:2, 0:SLAB], sz2[:, d * 2:(d + 1) * 2],
                               su[:, d * SLAB:(d + 1) * SLAB],
                               start=(d == 0), stop=(d == NDT - 1))
            p.inc(ins, "zmm", 1)

        block.tensor(zmm)

        def z_ep(v):
            p.wait(v, "zmm")
            v.tensor_single_scalar(ssgn[:, :], pa[0:36, 0:SLAB], 0.0, ALU.is_gt)
            p.inc(v.tensor_scalar(ssgn[:, :], ssgn[:, :], 2.0, -1.0, ALU.mult, ALU.add),
                  "sgn", 1)

        block.vector(z_ep)
        p.bank_release("pa", "sgn")

        def z2_ep(s):
            p.wait(s, "zmm")
            p.inc(s.activation(sglgm[:, :], pb[0:2, 0:SLAB], AF.Sigmoid,
                               bias=sbz2[:, :]), "glgm", 1)

        block.scalar(z2_ep)
        p.bank_release("pb", "glgm")

        def z_store(g, col0=col0):
            p.wait(g, "sgn")
            p.wait(g, "glgm")
            p.inc(g.dma_start(out=o_signs[:, col0:col0 + SLAB], in_=ssgn[:, :]),
                  "zst", 16)
            p.inc(g.dma_start(out=o_glr[:, col0:col0 + SLAB], in_=sglgm[0:1, :]),
                  "zst", 16)
            p.inc(g.dma_start(out=o_gmr[:, col0:col0 + SLAB], in_=sglgm[1:2, :]),
                  "zst", 16)

        block.gpsimd(z_store)

        for tt in range(SLAB // 128):
            bank = ["pc", "pd"][tt % 2]
            pgb = {"pc": pc, "pd": pd}[bank]

            def qkv_mm(t, tt=tt, bank=bank, pgb=pgb):
                p.bank_acquire(t, bank)
                for d in range(NDT):
                    ins = t.matmul(pgb[:, 0:97],
                                   su[:, d * SLAB + tt * 128:d * SLAB + tt * 128 + 128],
                                   sqkvg[:, d * 97:(d + 1) * 97],
                                   start=(d == 0), stop=(d == NDT - 1))
                p.inc(ins, "qmm", 1)

            block.tensor(qkv_mm)

            def qkv_ep(v, tt=tt, sl=sl, pgb=pgb):
                p.wait(v, "qmm")
                p.edge(v, v.tensor_copy(sqk[:, tt * 97:(tt + 1) * 97], pgb[:, 0:97]))
                p.edge(v, v.tensor_scalar(sqk[:, tt * 97 + 96:tt * 97 + 97],
                                          sqk[:, tt * 97 + 96:tt * 97 + 97],
                                          swgb[:, :], None, ALU.add))
                p.inc(v.tensor_copy(szgpc[:, sl * 2 + tt:sl * 2 + tt + 1],
                                    sqk[:, tt * 97 + 96:tt * 97 + 97]), "qep", 1)

            block.vector(qkv_ep)
            p.bank_release(bank, "qep")

            def qkv_store(g, tt=tt, col0=col0):
                p.wait(g, "qep")
                p.inc(g.dma_start(out=o_qkvg[col0 + tt * 128:col0 + (tt + 1) * 128, :],
                                  in_=sqk[:, tt * 97:(tt + 1) * 97]), "qst", 16)

            block.gpsimd(qkv_store)

        for c in range(32):
            ib = c % 2
            bank = ["pc", "pd"][c % 2]
            pgb = {"pc": pc, "pd": pd}[bank]

            def inp_mm(t, c=c, bank=bank, pgb=pgb):
                p.bank_acquire(t, bank)
                for d in range(NDT):
                    ins = t.matmul(pgb[:, 0:SLAB],
                                   sstin[:, d * KSS * D + c * 128:d * KSS * D + (c + 1) * 128],
                                   su[:, d * SLAB:(d + 1) * SLAB],
                                   start=(d == 0), stop=(d == NDT - 1))
                p.inc(ins, "imm", 1)

            block.tensor(inp_mm)

            def scan_v(v, c=c, ib=ib, pgb=pgb):
                p.wait(v, "imm")
                p.inc(v.tensor_copy(sinp[:, ib * SLAB:(ib + 1) * SLAB],
                                    pgb[:, 0:SLAB]), "ipev", 1)
                k = c // 8
                p.edge(v, v.tensor_tensor_scan(ssc[:, ib * SLAB:(ib + 1) * SLAB],
                                     sdecay[:, k:k + 1].broadcast_to((128, SLAB)),
                                     sinp[:, ib * SLAB:(ib + 1) * SLAB],
                                     scar[:, c:c + 1], ALU.mult, ALU.add))
                v.tensor_copy(ss[:, c * SLAB:(c + 1) * SLAB],
                              ssc[:, ib * SLAB:(ib + 1) * SLAB])
                p.inc(v.tensor_copy(scar[:, c:c + 1],
                                    ssc[:, (ib + 1) * SLAB - 1:(ib + 1) * SLAB]),
                      "scan", 1)

            block.vector(scan_v)
            p.bank_release(bank, "ipev")

        def long_mm(t, sl=sl):
            p.wait(t, "scan")
            p.bank_acquire(t, "pl")
            for i in range(NDT):
                for c in range(32):
                    ins = t.matmul(plong(i),
                                   sstout[:, c * D + i * 128:c * D + (i + 1) * 128],
                                   ss[:, c * SLAB:(c + 1) * SLAB],
                                   start=(c == 0), stop=(c == 31))
            p.inc(ins, "lmm", 1)

        block.tensor(long_mm)

        def long_ev(v):
            p.wait(v, "lmm")
            for i in range(NDT):
                ins = v.tensor_copy(slong[:, i * SLAB:(i + 1) * SLAB], plong(i))
            p.inc(ins, "lev", 1)

        block.vector(long_ev)
        p.bank_release("pl", "lev")

        def long_store(g, col0=col0):
            p.wait(g, "lev")
            p.inc(g.dma_start(
                out=o_long[:, col0:col0 + SLAB].rearrange("(d p) n -> p d n", p=128),
                in_=slong.rearrange("p (d n) -> p d n", d=NDT)), "lst", 16)

        block.gpsimd(long_store)

        def slab_fence(g):
            p.wait(g, "lst")

        if sl == NSL - 1:
            block.gpsimd(slab_fence)
        else:
            def fence2(v, sl=sl):
                pass
            def store_fence(t, sl=sl):
                p.wait(t, "zst")
                p.wait(t, "qst")
                p.wait(t, "lst")
            block.tensor(store_fence)

    def fin_tr(t):
        p.wait(t, "scan")
        p.bank_acquire(t, "pa")
        p.inc(t.transpose(pa[0:32, 0:128], scar[:, :], seyef[:, :]), "ftr", 1)

    block.tensor(fin_tr)

    def fin_ev(v):
        p.wait(v, "ftr")
        p.inc(v.tensor_copy(sfint[:, :], pa[0:32, 0:128]), "fev", 1)

    block.vector(fin_ev)
    p.bank_release("pa", "fev")

    def fin_store(g):
        p.wait(g, "fev")
        p.inc(g.dma_start(out=o_finT, in_=sfint[:, :]), "fst", 16)
        p.inc(g.dma_start(out=o_zgpc, in_=szgpc[:, :]), "fst", 16)
        p.wait(g, "fst")
        p.wait(g, "zst")
        p.wait(g, "qst")
        p.wait(g, "lst")

    block.gpsimd(fin_store)


def build_launch_A():
    p = Prog()
    nc = p.nc
    aps = dict(
        xT=nc.dram_tensor("xT", [D, SH + 3], F32, kind="ExternalInput").ap(),
        w_gate=nc.dram_tensor("w_gate", [D, D], F32, kind="ExternalInput").ap(),
        w_stin=nc.dram_tensor("w_stin", [D, KSS * D], F32, kind="ExternalInput").ap(),
        w_stout=nc.dram_tensor("w_stout", [KSS * D, D], F32, kind="ExternalInput").ap(),
        w_z36=nc.dram_tensor("w_z36", [D, 36], F32, kind="ExternalInput").ap(),
        w_z2=nc.dram_tensor("w_z2", [D, 2], F32, kind="ExternalInput").ap(),
        b_z2=nc.dram_tensor("b_z2", [2, 1], F32, kind="ExternalInput").ap(),
        w_qkvg=nc.dram_tensor("w_qkvg", [D, 97], F32, kind="ExternalInput").ap(),
        wgb_col=nc.dram_tensor("wgb_col", [128, 1], F32, kind="ExternalInput").ap(),
        conv_w=nc.dram_tensor("conv_w", [128, NDT * 4], F32, kind="ExternalInput").ap(),
        gate_b=nc.dram_tensor("gate_b", [128, NDT], F32, kind="ExternalInput").ap(),
        logit_bc=nc.dram_tensor("logit_bc", [128, KSS], F32, kind="ExternalInput").ap(),
        o_signs=nc.dram_tensor("o_signs", [36, SH], F32, kind="ExternalOutput").ap(),
        o_glr=nc.dram_tensor("o_glr", [1, SH], F32, kind="ExternalOutput").ap(),
        o_gmr=nc.dram_tensor("o_gmr", [1, SH], F32, kind="ExternalOutput").ap(),
        o_qkvg=nc.dram_tensor("o_qkvg", [SH, 97], F32, kind="ExternalOutput").ap(),
        o_zgpc=nc.dram_tensor("o_zgpc", [128, 16], F32, kind="ExternalOutput").ap(),
        o_long=nc.dram_tensor("o_long", [D, SH], F32, kind="ExternalOutput").ap(),
        o_finT=nc.dram_tensor("o_finT", [32, 128], F32, kind="ExternalOutput").ap(),
    )
    p.begin()
    emit_A(p, **aps)
    p.close()
    return p.nc


def host_inputs_A(inputs, core):
    b, pos = core // 4, core % 4
    x = np.asarray(inputs["L0_attn_x"])[b]
    t0 = pos * SH
    xsh = np.zeros((D, SH + 3), np.float32)
    xsh[:, 3:] = np.ascontiguousarray(x[t0:t0 + SH].T)
    if pos > 0:
        xsh[:, 0:3] = x[t0 - 3:t0].T
    z36 = np.concatenate([inputs["L0_attn_read_bits_w"], inputs["L0_attn_write_bits_w"]], 0)
    qkvg = np.concatenate([inputs["L0_attn_qkey_w"], inputs["L0_attn_wkey_w"],
                           inputs["L0_attn_value_w"], inputs["L0_attn_write_gate_w"]], 0)
    z2 = np.concatenate([inputs["L0_attn_gate_long_w"], inputs["L0_attn_gate_mem_w"]], 0)
    f32 = np.float32
    return {
        "xT": xsh,
        "w_gate": np.ascontiguousarray(np.asarray(inputs["L0_attn_gate_w"]).T, dtype=f32),
        "w_stin": np.ascontiguousarray(np.asarray(inputs["L0_attn_state_in_w"]).T, dtype=f32),
        "w_stout": np.ascontiguousarray(np.asarray(inputs["L0_attn_state_out_w"]).T, dtype=f32),
        "w_z36": np.ascontiguousarray(z36.T, dtype=f32),
        "w_z2": np.ascontiguousarray(z2.T, dtype=f32),
        "b_z2": np.stack([np.asarray(inputs["L0_attn_gate_long_b"]),
                          np.asarray(inputs["L0_attn_gate_mem_b"])]).reshape(2, 1).astype(f32),
        "w_qkvg": np.ascontiguousarray(qkvg.T, dtype=f32),
        "wgb_col": np.broadcast_to(np.asarray(inputs["L0_attn_write_gate_b"]).reshape(1, 1),
                                   (128, 1)).astype(f32).copy(),
        "conv_w": np.ascontiguousarray(np.asarray(inputs["L0_attn_conv_w"]).reshape(NDT, 128, 4).transpose(1, 0, 2).reshape(128, NDT * 4), dtype=f32),
        "gate_b": np.ascontiguousarray(np.asarray(inputs["L0_attn_gate_b"]).reshape(NDT, 128).T, dtype=f32),
        "logit_bc": np.broadcast_to(np.asarray(inputs["L0_attn_state_decay_logit"]).reshape(1, KSS),
                                    (128, KSS)).astype(f32).copy(),
    }



QLVL = 80.0
LGCLAMP = 30.0
MSCALE = float(1.0 / np.sqrt(32.0))
EM80 = float(np.exp(-QLVL))


def emit_B(p, sgath, qgath, zgath, codesF, suF, suiF, thrF, altF, bndF, eyeF,
           o_readT, nch_limit=None):
    import os
    BSTAGE = int(os.environ.get("BSTAGE", "99"))
    nc = p.nc
    C = 128
    NCH = T // C
    NCH_RUN = NCH if nch_limit is None else nch_limit
    _IGNORE = """
    codesF = nc.dram_tensor("codesF", [9, 512], F32, kind="ExternalInput").ap()
    suF = nc.dram_tensor("suF", [128, 128], F32, kind="ExternalInput").ap()
    suiF = nc.dram_tensor("suiF", [128, 128], F32, kind="ExternalInput").ap()
    thrF = nc.dram_tensor("thrF", [128, 128], F32, kind="ExternalInput").ap()
    altF = nc.dram_tensor("altF", [128, 128], F32, kind="ExternalInput").ap()
    bndF = nc.dram_tensor("bndF", [128, 132], F32, kind="ExternalInput").ap()
    eyeFX = 0 """
    import os as _os
    BDBG = _os.environ.get("BDBG", "") == "1"
    if BDBG:
        dbg_names = ["d_mtmw", "d_w1h", "d_r1ht", "d_cnt", "d_sms", "d_lpack",
                     "d_lv", "d_rfr", "d_d4", "d_dt", "d_lhsr", "d_aA", "d_aE",
                     "d_read", "d_gath", "d_upd", "d_tbl", "d_cont", "d_att"]
        dbg_shapes = {"d_mtmw": [128, 256], "d_w1h": [128, 512], "d_r1ht": [128, 512],
                      "d_cnt": [128, 1], "d_sms": [128, 4], "d_lpack": [128, 6],
                      "d_lv": [128, 4], "d_rfr": [128, 4], "d_d4": [128, 128],
                      "d_dt": [32, 512], "d_lhsr": [128, 256], "d_aA": [128, 2],
                      "d_aE": [128, 2], "d_read": [128, 128], "d_gath": [128, 128],
                      "d_upd": [128, 528], "d_tbl": [128, 1024], "d_cont": [128, 256],
                      "d_att": [128, 4]}
        dbg = {nm: nc.dram_tensor(nm, dbg_shapes[nm], F32, kind="ExternalOutput").ap()
               for nm in dbg_names}

    s_rs0 = p.sb("s_rs0", [9, T], BF16)
    s_rs1 = p.sb("s_rs1", [9, T], BF16)
    s_ws0 = p.sb("s_ws0", [9, T], BF16)
    s_ws1 = p.sb("s_ws1", [9, T], BF16)
    s_kv = p.sb("s_kv", [128, NCH * 64], BF16)
    s_q = p.sb("s_q", [128, NCH * 32], F32)
    s_zg = p.sb("s_zg", [128, NCH], F32)
    s_g = p.sb("s_g", [128, NCH], F32)
    s_lg = p.sb("s_lg", [128, NCH], F32)
    s_codes = p.sb("s_codes", [9, 512], BF16)
    s_su = p.sb("s_su", [128, 128], BF16)
    s_sui = p.sb("s_sui", [128, 128], BF16)
    s_thr = p.sb("s_thr", [128, 128], F32)
    s_alt = p.sb("s_alt", [128, 128], F32)
    s_bnd = p.sb("s_bnd", [128, 132], F32)
    s_eye = p.sb("s_eye", [128, 128], BF16)
    s_eyef = p.sb("s_eyef", [128, 128], F32)
    s_rot = p.sb("s_rot", [32, 128], F32)
    s_ones1 = p.sb("s_ones1", [128, 1], BF16)
    s_tmp = p.sb("s_tmp", [128, 64], F32)

    s_table = p.sb("s_table", [128, 2 * 4 * 128], F32)
    s_tblb = p.sb("s_tblb", [128, 2 * 4 * 128], BF16)
    s_uexp = p.sb("s_uexp", [128, 8], F32)
    s_newt = p.sb("s_newt", [128, 512], F32)
    s_swp = p.sb("s_swp", [128, 256], F32)

    s_mtmw = p.sb("s_mtmw", [128, 256], BF16)
    s_w1h = p.sb("s_w1h", [128, 512], BF16)
    s_r1ht = p.sb("s_r1ht", [128, 512], BF16)
    s_mskD = p.sb("s_mskD", [128, 128], BF16)
    s_mskA = p.sb("s_mskA", [128, 128], BF16)
    s_mskB = p.sb("s_mskB", [128, 128], BF16)
    s_sms = p.sb("s_sms", [128, 4], BF16)
    s_par = p.sb("s_par", [128, 1], F32)
    s_cdbg = p.sb("s_cdbg", [128, 4], F32)
    s_cmp = p.sb("s_cmp", [128, 128], F32)
    s_rhs2 = p.sb("s_rhs2", [128, 2], BF16)
    s_lpack = p.sb("s_lpack", [128, 6], F32)
    s_neg4 = p.sb("s_neg4", [128, 4], F32)
    s_cum = p.sb("s_cum", [128, 4 * 33], F32)
    s_lv = p.sb("s_lv", [128, 4], F32)
    s_rfr = p.sb("s_rfr", [128, 4], F32)
    s_d4 = p.sb("s_d4", [128, 128], BF16)
    s_dT = p.sb("s_dT", [32, 256], BF16)
    s_dT1 = p.sb("s_dT1", [32, 256], BF16)
    s_exps = p.sb("s_exps", [128, 2], F32)
    s_g1 = p.sb("s_g1", [128, 2], F32)
    s_ecar = p.sb("s_ecar", [128, 2], F32)
    s_aA = p.sb("s_aA", [128, 2], F32)
    s_aE = p.sb("s_aE", [128, 2], F32)
    s_dE = p.sb("s_dE", [128, 2], F32)
    s_rhsr = p.sb("s_rhsr", [128, 128], BF16)
    s_rhsu = p.sb("s_rhsu", [128, 132], BF16)
    s_lhsr = p.sb("s_lhsr", [128, 256], BF16)
    s_upd = p.sb("s_upd", [128, 4 * 132], F32)
    s_cont = p.sb("s_cont", [128, 2 * 128], F32)
    s_contv = p.sb("s_contv", [128, 256], F32)
    s_sc4 = p.sb("s_sc4", [128, 4], F32)
    s_att = p.sb("s_att", [128, 4], F32)
    s_mx = p.sb("s_mx", [128, 1], F32)
    s_den = p.sb("s_den", [128, 1], F32)
    s_ro = p.sb("s_ro", [128, 32], F32)

    pb0 = p.psum("pb0")
    pb1 = p.psum("pb1")
    pb2 = p.psum("pb2")
    pb3 = p.psum("pb3")
    pb4 = p.psum("pb4")
    pb5 = p.psum("pb5")
    pbt = p.ent(nc.psum_tensor(p._part + "pbt", [128, 1024], BF16))
    pbrt = p.psum("pbrt")

    block = p.block

    def loads(g):
        for r in range(4):
            for dst, r0 in [(s_rs0, 0), (s_rs1, 9), (s_ws0, 18), (s_ws1, 27)]:
                p.inc(g.dma_start(out=dst[:, r * SH:(r + 1) * SH],
                                  in_=sgath[r * 36 + r0:r * 36 + r0 + 9, :]), "ld", 16)
            p.inc(g.dma_start(out=s_zg[:, r * 16:(r + 1) * 16],
                              in_=zgath[r * 128:(r + 1) * 128, :]), "ld", 16)
        p.inc(g.dma_start(out=s_kv.rearrange("p (c f) -> p c f", c=NCH),
                          in_=qgath.rearrange("(c p) f -> p c f", p=128)[:, :, 32:96]),
              "ld", 16)
        p.inc(g.dma_start(out=s_q.rearrange("p (c f) -> p c f", c=NCH),
                          in_=qgath.rearrange("(c p) f -> p c f", p=128)[:, :, 0:32]),
              "ld", 16)
        p.inc(g.dma_start(out=s_codes[:, :], in_=codesF), "ld", 16)
        p.inc(g.dma_start(out=s_su[:, :], in_=suF), "ld", 16)
        p.inc(g.dma_start(out=s_sui[:, :], in_=suiF), "ld", 16)
        p.inc(g.dma_start(out=s_thr[:, :], in_=thrF), "ld", 16)
        p.inc(g.dma_start(out=s_alt[:, :], in_=altF), "ld", 16)
        p.inc(g.dma_start(out=s_bnd[:, :], in_=bndF), "ld", 16)
        p.inc(g.dma_start(out=s_eye[:, :], in_=eyeF), "ld", 16)
        p.inc(g.dma_start(out=s_eyef[:, :], in_=eyeF), "ld", 16)

    block.gpsimd(loads)

    def init_g(s):
        p.wait(s, "ld")
        s.activation(s_g[:, :], s_zg[:, :], AF.Sigmoid)
        p.edge(s, s.activation(s_tmp[:, :], s_zg[:, :], AF.Abs), "ssy")
        p.edge(s, s.activation(s_tmp[:, :], s_tmp[:, :], AF.Exp, scale=-1.0), "ssy")
        p.edge(s, s.activation(s_tmp[:, :], s_tmp[:, :], AF.Ln, bias=1.0), "ssy")
        p.inc(s.activation(s_lg[:, :], s_zg[:, :], AF.Relu), "ginit", 1)

    block.scalar(init_g)

    def init_g2(v):
        p.wait(v, "ginit")
        p.edge(v, v.tensor_add(s_lg[:, :], s_lg[:, :], s_tmp[:, :]))
        p.edge(v, v.tensor_scalar(s_lg[:, :], s_lg[:, :], -1.0, None, ALU.mult))
        p.edge(v, v.tensor_scalar(s_lg[:, :], s_lg[:, :], -LGCLAMP, None, ALU.max))
        v.memset(s_ones1[:, :], 1.0)
        v.memset(s_table[:, :], 0.0)
        p.inc(v.memset(s_tblb[:, :], 0.0), "ginit2", 1)

    block.vector(init_g2)

    for c in range(NCH_RUN):
        cc = slice(c * C, (c + 1) * C)
        for h in range(2):
            rs = (s_rs0 if h == 0 else s_rs1)[:, cc]
            ws = (s_ws0 if h == 0 else s_ws1)[:, cc]
            tbl = s_table[:, h * 512:(h + 1) * 512]

            def mm_signs(t, rs=rs, ws=ws, c=c, h=h):
                if c == 0 and h == 0:
                    p.wait(t, "ginit2")
                p.bank_acquire(t, "b0")
                t.matmul(pb0[:, 0:128], ws, rs, start=True, stop=True)
                t.matmul(pb0[:, 128:256], ws, ws, start=True, stop=True)
                p.bank_acquire(t, "b1")
                t.matmul(pb1[:, 0:512], ws, s_codes[:, :], start=True, stop=True)
                p.bank_acquire(t, "b2")
                for rt in range(2):
                    t.matmul(pb2[:, rt * 128:(rt + 1) * 128],
                             s_codes[:, rt * 128:(rt + 1) * 128], rs,
                             start=True, stop=True)
                p.bank_acquire(t, "b3")
                for rt in range(2, 4):
                    ins = t.matmul(pb3[:, (rt - 2) * 128:(rt - 1) * 128],
                                   s_codes[:, rt * 128:(rt + 1) * 128], rs,
                                   start=True, stop=True)
                p.inc(ins, "smm", 1)

            block.tensor(mm_signs)

            def ev_signs(v, c=c, h=h):
                p.wait(v, "smm")
                v.tensor_single_scalar(s_mtmw[:, :], pb0[:, 0:256], 8.5, ALU.is_ge)
                v.tensor_single_scalar(s_w1h[:, :], pb1[:, 0:512], 8.5, ALU.is_ge)
                v.tensor_single_scalar(s_r1ht[:, 0:256], pb2[:, 0:256], 8.5, ALU.is_ge)
                p.inc(v.tensor_single_scalar(s_r1ht[:, 256:512], pb3[:, 0:256], 8.5, ALU.is_ge),
                      "sev", 1)
                p.wait(v, "sev")
                v.tensor_mul(s_mskD[:, :], s_mtmw[:, 0:128], s_su[:, :])
                v.tensor_mul(s_mskA[:, :], s_mtmw[:, 128:256], s_su[:, :])
                p.inc(v.tensor_mul(s_mskB[:, :], s_mtmw[:, 128:256], s_sui[:, :]),
                      "msk", 1)

            block.vector(ev_signs)
            p.bank_release("b0", "sev")
            p.bank_release("b1", "sev")
            p.bank_release("b2", "sev")
            p.bank_release("b3", "sev")

            if BSTAGE < 2:
                continue
            def mm_cnt(t, h=h):
                p.wait(t, "msk")
                p.bank_acquire(t, "b0")
                p.inc(t.matmul(pb0[:, 0:1], s_mskA[:, :], s_ones1[:, :],
                               start=True, stop=True), "cmm", 1)

            block.tensor(mm_cnt)

            def slot_par(v, c=c, h=h):
                p.wait(v, "cmm")
                v.tensor_scalar(s_cmp[:, :], s_thr[:, :], pb0[:, 0:1], None, ALU.is_le)
                p.edge(v, v.tensor_mul(s_cmp[:, :], s_cmp[:, :], s_alt[:, :]))
                p.inc(v.reduce_sum(s_par[:, :], s_cmp[:, :], axis=AX.X), "parred", 1)
                p.wait(v, "parred")
                v.tensor_copy(s_sms[:, 1:2], s_par[:, :])
                v.tensor_scalar(s_sms[:, 0:1], s_par[:, :], -1.0, 1.0, ALU.mult, ALU.add)
                p.edge(v, v.tensor_scalar(s_sms[:, 2:3], s_par[:, :], -2.0, 1.0, ALU.mult, ALU.add))
                p.inc(v.tensor_scalar(s_rhs2[:, 0:2], s_sms[:, 0:2],
                                      s_lg[:, c:c + 1], None, ALU.mult), "sms", 1)

            block.vector(slot_par)
            p.bank_release("b0", "parred")

            if BSTAGE < 3:
                continue
            def mm_L(t, h=h):
                p.wait(t, "sms")
                p.bank_acquire(t, "b0")
                t.matmul(pb0[:, 0:2], s_mskB[:, :], s_rhs2[:, 0:2], start=True, stop=True)
                t.matmul(pb0[:, 2:4], s_mtmw[:, 128:256], s_rhs2[:, 0:2], start=True, stop=True)
                p.inc(t.matmul(pb0[:, 4:6], s_mskD[:, :], s_rhs2[:, 0:2],
                               start=True, stop=True), "lmm2", 1)

            block.tensor(mm_L)

            def levels(v, c=c, h=h):
                p.wait(v, "lmm2")
                p.inc(v.tensor_copy(s_lpack[:, :], pb0[:, 0:6]), "lev2", 1)
                p.wait(v, "lev2")
                v.tensor_scalar(s_neg4[:, 0:2], s_lpack[:, 0:2], -1.0, None, ALU.mult)
                p.edge(v, v.tensor_scalar(s_neg4[:, 2:4], s_lpack[:, 4:6], -1.0, None, ALU.mult))
                for j in range(4):
                    v.tensor_scalar(s_cum[:, j * 33:(j + 1) * 33], s_bnd[:, 0:33],
                                    s_neg4[:, j:j + 1], None, ALU.is_le)
                p.edge(v, v.engine_nop())
                p.inc(v.reduce_sum(s_lv[:, :],
                                   s_cum.rearrange("p (g n) -> p g n", g=4)[:, :, 1:33],
                                   axis=AX.X), "vsy", 1)
                p.wait(v, "vsy")
                v.scalar_tensor_tensor(s_rfr[:, :], s_lv[:, :], -QLVL, s_neg4[:, :],
                                       ALU.mult, ALU.add)
                p.inc(v.tensor_sub(s_d4.rearrange("p (g n) -> p g n", g=4),
                                   s_cum.rearrange("p (g n) -> p g n", g=4)[:, :, 0:32],
                                   s_cum.rearrange("p (g n) -> p g n", g=4)[:, :, 1:33]),
                      "lvd", 1)
                p.wait(v, "lvd")
                p.inc(v.scalar_tensor_tensor(
                    s_d4.rearrange("p (g n) -> p g n", g=4)[:, 2:4, 0:31],
                    s_d4.rearrange("p (g n) -> p g n", g=4)[:, 2:4, 1:32],
                    EM80,
                    s_d4.rearrange("p (g n) -> p g n", g=4)[:, 2:4, 0:31],
                    ALU.mult, ALU.add), "laug", 1)

            block.vector(levels)
            p.bank_release("b0", "lev2")

            if BSTAGE < 4:
                continue
            def exps_s(s, c=c, h=h):
                p.wait(s, "laug")
                s.activation(s_exps[:, :], s_rfr[:, 0:2], AF.Exp)
                s.activation(s_g1[:, :], s_rfr[:, 2:4], AF.Exp, scale=-1.0)
                p.inc(s.activation(s_ecar[:, :], s_lpack[:, 4:6], AF.Exp), "exps", 1)

            block.scalar(exps_s)

            def mk_a(v, c=c, h=h):
                p.wait(v, "exps")
                p.edge(v, v.tensor_scalar(s_aA[:, :], s_exps[:, :], s_g[:, c:c + 1], None, ALU.mult))
                v.tensor_mul(s_aA[:, :], s_aA[:, :], s_sms[:, 0:2])
                v.tensor_sub(s_dE[:, :], s_lpack[:, 2:4], s_lpack[:, 0:2])
                p.inc(v.engine_nop(), "mka", 1)

            block.vector(mk_a)

            def exps_e(s):
                p.wait(s, "mka")
                p.inc(s.activation(s_aE[:, :], s_dE[:, :], AF.Exp), "expe", 1)

            block.scalar(exps_e)

            def mk_rhs(v, c=c, h=h):
                p.wait(v, "expe")
                p.edge(v, v.tensor_scalar(s_aE[:, :], s_aE[:, :], s_g[:, c:c + 1], None, ALU.mult))
                p.edge(v, v.tensor_mul(s_aE[:, :], s_aE[:, :], s_sms[:, 0:2]))
                for si in range(2):
                    v.tensor_scalar(s_rhsr[:, si * 64:(si + 1) * 64],
                                    s_kv[:, c * 64:(c + 1) * 64],
                                    s_aA[:, si:si + 1], None, ALU.mult)
                    v.tensor_scalar(s_rhsu[:, si * 64:(si + 1) * 64],
                                    s_kv[:, c * 64:(c + 1) * 64],
                                    s_aE[:, si:si + 1], None, ALU.mult)
                v.tensor_copy(s_rhsu[:, 128:130], s_rhs2[:, 0:2])
                p.inc(v.tensor_copy(s_rhsu[:, 130:131], s_sms[:, 2:3]), "rhs", 1)

            block.vector(mk_rhs)

            if BSTAGE < 5:
                continue
            def mm_tr(t, h=h):
                p.wait(t, "laug")
                p.bank_acquire(t, "bt")
                t.transpose(pbt[0:64, 0:128], s_d4[:, 0:64], s_eye[:, :])
                p.inc(t.transpose(pbt[0:64, 128:256], s_d4[:, 64:128], s_eye[:, :]),
                      "trmm", 1)

            block.tensor(mm_tr)

            def ev_tr(v):
                p.wait(v, "trmm")
                v.tensor_copy(s_dT[:, :], pbt[0:32, 0:256])
                p.inc(v.tensor_copy(s_dT1[:, :], pbt[32:64, 0:256]), "trev", 1)

            block.vector(ev_tr)
            p.bank_release("bt", "trev")

            if BSTAGE < 42:
                continue
            def mm_E(t, h=h):
                p.wait(t, "trev")
                p.bank_acquire(t, "b0")
                t.matmul(pb0[:, 0:128], s_dT[0:32, 0:128], s_dT[0:32, 128:256],
                         start=True, stop=True)
                p.inc(t.matmul(pb0[:, 128:256], s_dT1[0:32, 0:128], s_dT1[0:32, 128:256],
                               start=True, stop=True), "emm", 1)

            block.tensor(mm_E)

            if BSTAGE < 43:
                continue
            def mk_lhsr(v, h=h):
                p.wait(v, "emm")
                v.tensor_mul(s_lhsr[:, 0:128], s_mskD[:, :], pb0[:, 0:128])
                p.inc(v.tensor_mul(s_lhsr[:, 128:256], s_mskD[:, :], pb0[:, 128:256]),
                      "lhsr", 1)

            block.vector(mk_lhsr)
            p.bank_release("b0", "lhsr")

            if BSTAGE < 6:
                continue
            def mm_read(t, h=h, tbl=tbl):
                p.wait(t, "lhsr")
                p.wait(t, "rhs")
                p.wait(t, "tupd")
                p.bank_acquire(t, "b4")
                t.matmul(pb4[:, 0:64], s_lhsr[:, 0:128], s_rhsr[:, 0:64],
                         start=True, stop=True)
                t.matmul(pb4[:, 64:128], s_lhsr[:, 128:256], s_rhsr[:, 64:128],
                         start=True, stop=True)
                for rt in range(4):
                    ins = t.matmul(pb4[:, 128:256],
                                   s_r1ht[:, rt * 128:(rt + 1) * 128],
                                   s_tblb[:, h * 512 + rt * 128:h * 512 + (rt + 1) * 128],
                                   start=(rt == 0), stop=(rt == 3))
                p.inc(ins, "rmm", 1)
                p.bank_acquire(t, "b5")
                for rt in range(2):
                    t.matmul(pb5[:, rt * 256:rt * 256 + 131],
                             s_w1h[:, rt * 128:(rt + 1) * 128], s_rhsu[:, 0:131],
                             start=True, stop=True)
                p.bank_acquire(t, "b1")
                for rt in range(2, 4):
                    ins = t.matmul(pb1[:, (rt - 2) * 256:(rt - 2) * 256 + 131],
                                   s_w1h[:, rt * 128:(rt + 1) * 128], s_rhsu[:, 0:131],
                                   start=True, stop=True)
                p.inc(ins, "umm", 1)

            block.tensor(mm_read)

            if BSTAGE < 7:
                continue
            def upd_ev(v, h=h):
                p.wait(v, "umm")
                v.tensor_copy(s_upd[:, 0:264].rearrange("p (g n) -> p g n", g=2)[:, :, 0:131],
                              pb5.rearrange("p (g n) -> p g n", g=2)[:, :, 0:131])
                p.inc(v.tensor_copy(s_upd[:, 264:528].rearrange("p (g n) -> p g n", g=2)[:, :, 0:131],
                                    pb1.rearrange("p (g n) -> p g n", g=2)[:, :, 0:131]),
                      "uev", 1)

            block.vector(upd_ev)
            p.bank_release("b5", "uev")
            p.bank_release("b1", "uev")

            def upd_exp(s, h=h):
                p.wait(s, "uev")
                p.inc(s.activation(s_uexp.rearrange("p (g n) -> p g n", g=4),
                                   s_upd.rearrange("p (g n) -> p g n", g=4)[:, :, 128:130],
                                   AF.Exp), "uexp", 1)

            block.scalar(upd_exp)

            def upd_v(v, h=h, tbl=tbl):
                p.wait(v, "uexp")
                tb3 = tbl.rearrange("p (g n) -> p g n", g=4)
                up3 = s_upd.rearrange("p (g n) -> p g n", g=4)
                ue3 = s_uexp.rearrange("p (g n) -> p g n", g=4)
                nt3 = s_newt.rearrange("p (g n) -> p g n", g=4)
                v.tensor_mul(nt3[:, :, 0:64], tb3[:, :, 0:64],
                             ue3[:, :, 0:1].broadcast_to((128, 4, 64)))
                v.tensor_add(nt3[:, :, 0:64], nt3[:, :, 0:64], up3[:, :, 0:64])
                v.tensor_mul(nt3[:, :, 64:128], tb3[:, :, 64:128],
                             ue3[:, :, 1:2].broadcast_to((128, 4, 64)))
                v.tensor_add(nt3[:, :, 64:128], nt3[:, :, 64:128], up3[:, :, 64:128])
                v.tensor_sub(s_swp.rearrange("p (g n) -> p g n", g=4),
                             nt3[:, :, 64:128], nt3[:, :, 0:64])
                v.tensor_mul(s_swp.rearrange("p (g n) -> p g n", g=4),
                             s_swp.rearrange("p (g n) -> p g n", g=4),
                             up3[:, :, 130:131].broadcast_to((128, 4, 64)))
                v.tensor_add(tb3[:, :, 0:64], nt3[:, :, 0:64],
                             s_swp.rearrange("p (g n) -> p g n", g=4))
                v.tensor_sub(tb3[:, :, 64:128], nt3[:, :, 64:128],
                             s_swp.rearrange("p (g n) -> p g n", g=4))
                p.inc(v.tensor_copy(s_tblb[:, h * 512:(h + 1) * 512], tbl), "tupd", 1)

            block.vector(upd_v)

            if BSTAGE < 8:
                continue
            def cont_v(v, h=h, c=c):
                p.wait(v, "rmm")
                for si in range(2):
                    v.tensor_scalar(s_cont[:, si * 64:(si + 1) * 64],
                                    pb4[:, 128 + si * 64:128 + (si + 1) * 64],
                                    s_ecar[:, si:si + 1], None, ALU.mult)
                p.edge(v, v.engine_nop())
                for si in range(2):
                    v.scalar_tensor_tensor(s_contv[:, h * 128 + si * 64:h * 128 + (si + 1) * 64],
                                           pb4[:, si * 64:(si + 1) * 64],
                                           s_g1[:, si:si + 1],
                                           s_cont[:, si * 64:(si + 1) * 64],
                                           ALU.mult, ALU.add)
                p.inc(v.engine_nop(), "cont", 1)

            block.vector(cont_v)
            p.bank_release("b4", "cont")

        if BDBG and c == 0:
            def dbg_dump(g, c=c):
                p.wait(g, "tupd")
                p.wait(g, "cont")
                p.inc(g.dma_start(out=dbg["d_cnt"], in_=s_cdbg[:, 0:1]), "dbgst", 16)
                for nm, src_ in [("d_mtmw", s_mtmw), ("d_w1h", s_w1h), ("d_r1ht", s_r1ht),
                                 ("d_sms", s_sms), ("d_lpack", s_lpack),
                                 ("d_lv", s_lv), ("d_rfr", s_rfr),
                                 ("d_aA", s_aA), ("d_aE", s_aE),
                                 ("d_upd", s_upd), ("d_tbl", s_table), ("d_cont", s_contv)]:
                    p.inc(g.dma_start(out=dbg[nm], in_=src_[:, :]), "dbgst", 16)
                p.inc(g.dma_start(out=dbg["d_d4"], in_=s_d4[:, :]), "dbgst", 16)
                p.inc(g.dma_start(out=dbg["d_dt"][:, 0:256], in_=s_dT[:, :]), "dbgst", 16)
                p.inc(g.dma_start(out=dbg["d_dt"][:, 256:512], in_=s_dT1[:, :]), "dbgst", 16)
                p.inc(g.dma_start(out=dbg["d_lhsr"], in_=s_lhsr[:, :]), "dbgst", 16)
                p.inc(g.dma_start(out=dbg["d_att"], in_=s_neg4[:, :]), "dbgst", 16)
                p.inc(g.dma_start(out=dbg["d_read"][:, 0:2], in_=s_dE[:, :]), "dbgst", 16)
            block.gpsimd(dbg_dump)
        if BSTAGE < 9:
            continue
        def attn_v(v, c=c):
            p.wait(v, "cont")
            v.tensor_mul(s_cmp[:, 0:128].rearrange("p (g n) -> p g n", g=4),
                         s_contv.rearrange("p (g n) -> p g n", g=4)[:, :, 0:32],
                         s_q[:, c * 32:(c + 1) * 32].unsqueeze(1).broadcast_to((128, 4, 32)))
            p.edge(v, v.engine_nop())
            p.inc(v.reduce_sum(s_sc4[:, :], s_cmp[:, 0:128].rearrange("p (g n) -> p g n", g=4),
                               axis=AX.X), "vsy", 1)
            p.wait(v, "vsy")
            p.inc(v.reduce_max(s_mx[:, :], s_sc4[:, :], axis=AX.X), "vsy", 1)
            p.wait(v, "vsy")
            p.inc(v.tensor_scalar(s_mx[:, :], s_mx[:, :], -MSCALE, None, ALU.mult), "att1", 1)

        block.vector(attn_v)

        def attn_s(s, c=c):
            p.wait(s, "att1")
            p.inc(s.activation(s_att[:, :], s_sc4[:, :], AF.Exp,
                               bias=s_mx[:, :], scale=MSCALE), "att2", 1)

        block.scalar(attn_s)

        def attn_v2(v, c=c):
            p.wait(v, "att2")
            p.inc(v.reduce_sum(s_den[:, :], s_att[:, :], axis=AX.X), "vsy", 1)
            p.wait(v, "vsy")
            p.inc(v.reciprocal(s_den[:, :], s_den[:, :]), "vsy", 1)
            p.wait(v, "vsy")
            p.edge(v, v.tensor_scalar(s_att[:, :], s_att[:, :], s_den[:, :], None, ALU.mult))
            p.edge(v, v.tensor_scalar(s_ro[:, :], s_contv[:, 32:64], s_att[:, 0:1], None, ALU.mult))
            for hs in range(1, 4):
                p.edge(v, v.scalar_tensor_tensor(s_ro[:, :],
                                                 s_contv[:, hs * 64 + 32:hs * 64 + 64],
                                                 s_att[:, hs:hs + 1], s_ro[:, :],
                                                 ALU.mult, ALU.add))
            p.inc(v.engine_nop(), "ro", 1)

        block.vector(attn_v2)

        def ro_tr(t, c=c):
            p.wait(t, "ro")
            p.bank_acquire(t, "brt")
            p.inc(t.transpose(pbrt[0:32, 0:128], s_ro[:, :], s_eyef[:, :]), "rtr2", 1)

        block.tensor(ro_tr)

        def ro_ev(v, c=c):
            p.wait(v, "rtr2")
            p.inc(v.tensor_copy(s_rot[:, :], pbrt[0:32, 0:128]), "rev2", 1)

        block.vector(ro_ev)
        p.bank_release("brt", "rev2")

        def ro_store(g, c=c):
            p.wait(g, "rev2")
            p.inc(g.dma_start(out=o_readT[:, c * C:(c + 1) * C], in_=s_rot[:, :]), "rst", 16)

        block.gpsimd(ro_store)

        def ro_fence(v, c=c):
            p.wait(v, "rst")

        block.vector(ro_fence)

    def done(g):
        p.wait(g, "rst")

    block.gpsimd(done)


def build_launch_B(nch_limit=None):
    p = Prog()
    nc = p.nc
    aps = dict(
        sgath=nc.dram_tensor("sgath", [144, SH], F32, kind="ExternalInput").ap(),
        qgath=nc.dram_tensor("qgath", [T, 97], F32, kind="ExternalInput").ap(),
        zgath=nc.dram_tensor("zgath", [512, 16], F32, kind="ExternalInput").ap(),
        codesF=nc.dram_tensor("codesF", [9, 512], F32, kind="ExternalInput").ap(),
        suF=nc.dram_tensor("suF", [128, 128], F32, kind="ExternalInput").ap(),
        suiF=nc.dram_tensor("suiF", [128, 128], F32, kind="ExternalInput").ap(),
        thrF=nc.dram_tensor("thrF", [128, 128], F32, kind="ExternalInput").ap(),
        altF=nc.dram_tensor("altF", [128, 128], F32, kind="ExternalInput").ap(),
        bndF=nc.dram_tensor("bndF", [128, 132], F32, kind="ExternalInput").ap(),
        eyeF=nc.dram_tensor("eyeF", [128, 128], F32, kind="ExternalInput").ap(),
        o_readT=nc.dram_tensor("o_readT", [32, T], F32, kind="ExternalOutput").ap(),
    )
    p.begin()
    emit_B(p, nch_limit=nch_limit, **aps)
    p.close()
    return p.nc



def emit_C(p, xT, longT, finAG, gl_row, gm_row, readTF, owns_bc, w_stout, w_mo,
           w_up, w_dn, b_up, b_dn, logit_row, ecp_row, mask44, iotap1, o_x1):
    import os as _os
    CST = int(_os.environ.get("CST", "99"))
    nc = p.nc
    SLAB = 256
    NSL = SH // SLAB

    swgt = p.sb("swgt", [128, 32 * D], BF16)
    smo = p.sb("smo", [32, D], BF16)
    sbup = p.sb("sbup", [128, 32], F32)
    sbdn = p.sb("sbdn", [1, D], BF16)
    sfin = p.sb("sfin", [4, KSS * D], BF16)
    sglr = p.sb("sglr", [1, SH], F32)
    sgmr = p.sb("sgmr", [1, SH], F32)
    srdt = p.sb("srdt", [32, SH], BF16)
    slogit = p.sb("slogit", [1, KSS], F32)
    slnd = p.sb("slnd", [1, KSS], F32)
    secp = p.sb("secp", [1, 4], F32)
    smask = p.sb("smask", [4, 4], F32)
    swmat = p.sb("swmat", [4, 4], BF16)
    srdf = p.sb("srdf", [32, T], BF16)
    sowns = p.sb("sowns", [32, 64], F32)
    siop = p.sb("siop", [1, SH], F32)
    sq = p.sb("sq", [1, KSS * D], BF16)
    sdec = p.sb("sdec", [1, KSS * SLAB], BF16)
    scarb = p.sb("scarb", [128, 32], BF16)

    sx = p.sb("sx", [128, NDT * SLAB], F32)
    slg = p.sb("slg", [128, NDT * SLAB], F32)
    sy = p.sb("sy", [128, NDT * SLAB], F32)
    sw_ = p.sb("sw_", [128, NDT * SLAB], F32)
    syn = p.sb("syn", [128, NDT * SLAB], BF16)
    sh_ = p.sb("sh_", [128, 16 * SLAB], BF16)
    swb = p.sb("swb", [128, NDT * SLAB], BF16)
    swb2 = p.sb("swb2", [128, NDT * SLAB], BF16)
    st1 = p.sb("st1", [128, SLAB], F32)
    st2 = p.sb("st2", [128, SLAB], F32)
    sct4 = p.sb("sct4", [128, 4 * SLAB], BF16)
    scc4 = p.sb("scc4", [128, 4 * SLAB], BF16)
    sglbc = p.sb("sglbc", [128, SLAB], F32)
    sgmbc = p.sb("sgmbc", [128, SLAB], F32)
    srow = p.sb("srow", [1, 3 * SLAB], F32)
    sones = p.sb("sones", [128, 1], BF16)
    sonesr = p.sb("sonesr", [1, SLAB], BF16)
    sonesrf = p.sb("sonesrf", [1, 128], F32)
    sx1 = p.sb("sx1", [128, NDT * SLAB], F32)

    pa = p.psum("pa")
    pb = p.psum("pb")
    pc = p.psum("pc")
    pd = p.psum("pd")
    pe = p.psum("pe")
    pf = p.psum("pf")

    block = p.block

    def loads(g):
        p.inc(g.dma_start(out=swgt.rearrange("p (c n) -> p c n", c=32),
                          in_=w_stout.rearrange("(c p) n -> p c n", p=128)), "wl", 16)
        p.inc(g.dma_start(out=smo[:, :], in_=w_mo), "wl", 16)
        p.inc(g.dma_start(out=sbup[:, :], in_=b_up), "wl", 16)
        p.inc(g.dma_start(out=sbdn[:, :], in_=b_dn), "wl", 16)
        for r in range(4):
            p.inc(g.dma_start(
                out=sfin[r:r + 1, :],
                in_=finAG[r * 32:(r + 1) * 32, :].rearrange("c pp -> (c pp)").unsqueeze(0)), "wl", 16)
        p.inc(g.dma_start(out=sglr[:, :], in_=gl_row), "wl", 16)
        p.inc(g.dma_start(out=sgmr[:, :], in_=gm_row), "wl", 16)
        p.inc(g.dma_start(out=srdf[:, :], in_=readTF), "wl", 16)
        p.inc(g.dma_start(out=sowns[:, :], in_=owns_bc), "wl", 16)
        p.inc(g.dma_start(out=slogit[:, :], in_=logit_row), "wl", 16)
        p.inc(g.dma_start(out=secp[:, :], in_=ecp_row), "wl", 16)
        p.inc(g.dma_start(out=smask[:, :], in_=mask44), "wl", 16)
        p.inc(g.dma_start(out=siop[:, :], in_=iotap1), "wl", 16)

    block.gpsimd(loads)

    def init_v(v):
        v.memset(sones[:, :], 1.0)
        v.memset(sonesr[:, :], 1.0)
        p.inc(v.memset(sonesrf[:, :], 1.0), "iv", 1)
        p.wait(v, "wl")
        for j in range(16):
            for q in range(4):
                c = j + 16 * q
                if q == 0:
                    v.tensor_scalar(srdt[:, j * 128:(j + 1) * 128],
                                    srdf[:, c * 128:(c + 1) * 128],
                                    sowns[:, c:c + 1], None, ALU.mult)
                else:
                    ins = v.scalar_tensor_tensor(srdt[:, j * 128:(j + 1) * 128],
                                                 srdf[:, c * 128:(c + 1) * 128],
                                                 sowns[:, c:c + 1],
                                                 srdt[:, j * 128:(j + 1) * 128],
                                                 ALU.mult, ALU.add)
            p.edge(v, ins)
        p.inc(v.engine_nop(), "rdt", 1)

    block.vector(init_v)

    def dec_s(s):
        p.wait(s, "wl")
        p.edge(s, s.activation(slnd[:, :], slogit[:, :], AF.Sigmoid), "ssy")
        p.inc(s.activation(slnd[:, :], slnd[:, :], AF.Ln), "lnd", 1)

    block.scalar(dec_s)

    def wmat_mm(t):
        p.wait(t, "lnd")
        p.wait(t, "iv")
        p.inc(t.matmul(pa[0:4, 0:4], secp[:, :], slnd[:, :], start=True, stop=True),
              "wm1", 1)

    block.tensor(wmat_mm)

    def wmat_s(s):
        p.wait(s, "wm1")
        p.inc(s.activation(pa[0:4, 4:8], pa[0:4, 0:4], AF.Exp), "wexp", 1)

    block.scalar(wmat_s)

    def wmat_v(v):
        p.wait(v, "wexp")
        p.inc(v.tensor_mul(swmat[:, :], pa[0:4, 4:8], smask[:, :]), "wmat", 1)

    block.vector(wmat_v)
    p.bank_release("pa", "wmat")

    if CST < 2:
        def cdone(g):
            p.wait(g, "wl")
        block.gpsimd(cdone)
        return
    def carry_mm(t):
        p.wait(t, "wmat")
        p.bank_acquire(t, "pb")
        for c in range(32):
            k = c // 8
            ins = t.matmul(pb[:, c:c + 1], sfin[:, c * 128:(c + 1) * 128],
                           swmat[:, k:k + 1], start=True, stop=True)
        p.inc(ins, "car", 1)

    block.tensor(carry_mm)

    def carry_ev(v):
        p.wait(v, "car")
        p.inc(v.tensor_copy(scarb[:, :], pb[:, 0:32]), "carev", 1)

    block.vector(carry_ev)
    p.bank_release("pb", "carev")

    for k in range(KSS):
        def q_mm(t, k=k):
            if k == 0:
                p.wait(t, "carev")
            p.bank_acquire(t, "pc")
            for ci in range(8):
                c = k * 8 + ci
                t.matmul(pc[0:1, 0:512], scarb[:, c:c + 1],
                         swgt[:, c * D:c * D + 512],
                         start=(ci == 0), stop=(ci == 7))
            p.bank_acquire(t, "pd")
            for ci in range(8):
                c = k * 8 + ci
                ins = t.matmul(pd[0:1, 0:512], scarb[:, c:c + 1],
                               swgt[:, c * D + 512:(c + 1) * D],
                               start=(ci == 0), stop=(ci == 7))
            p.inc(ins, "qmm", 1)

        block.tensor(q_mm)

        def q_ev(v, k=k):
            p.wait(v, "qmm")
            v.tensor_copy(sq[:, k * D:k * D + 512], pc[0:1, 0:512])
            p.inc(v.tensor_copy(sq[:, k * D + 512:(k + 1) * D], pd[0:1, 0:512]),
                  "qev", 1)

        block.vector(q_ev)
        p.bank_release("pc", "qev")
        p.bank_release("pd", "qev")

    if CST < 3:
        return
    for sl in range(NSL):
        col0 = sl * SLAB

        def ld_slab(g, sl=sl, col0=col0):
            if sl > 0:
                p.wait(g, "ynv")
            p.inc(g.dma_start(out=sx.rearrange("p (d n) -> p d n", d=NDT),
                              in_=xT[:, col0:col0 + SLAB].rearrange("(d p) n -> p d n", p=128)),
                  "xl", 16)
            p.inc(g.dma_start(out=slg.rearrange("p (d n) -> p d n", d=NDT),
                              in_=longT[:, col0:col0 + SLAB].rearrange("(d p) n -> p d n", p=128)),
                  "xl", 16)

        block.sync(ld_slab)

        def dec_rows(s, sl=sl, col0=col0):
            if sl == 0:
                p.wait(s, "qev")
            else:
                p.wait(s, "corr")
            for k in range(KSS):
                ins = s.activation(sdec[:, k * SLAB:(k + 1) * SLAB],
                                   siop[:, col0:col0 + SLAB], AF.Exp,
                                   scale=slnd[:, k:k + 1])
            p.inc(ins, "drow", 1)

        block.scalar(dec_rows)

        def gbc_mm(t, col0=col0):
            p.bank_acquire(t, "pa")
            t.matmul(pa[:, 0:SLAB], sonesrf[:, :], sglr[:, col0:col0 + SLAB],
                     start=True, stop=True)
            p.bank_acquire(t, "pb")
            p.inc(t.matmul(pb[:, 0:SLAB], sonesrf[:, :], sgmr[:, col0:col0 + SLAB],
                           start=True, stop=True), "gbc", 1)

        block.tensor(gbc_mm)

        def gbc_ev(v):
            p.wait(v, "gbc")
            v.tensor_copy(sglbc[:, :], pa[:, 0:SLAB])
            p.inc(v.tensor_copy(sgmbc[:, :], pb[:, 0:SLAB]), "gbe", 1)

        block.vector(gbc_ev)
        p.bank_release("pa", "gbe")
        p.bank_release("pb", "gbe")

        for i in range(NDT):
            bank = ["pc", "pd"][i % 2]
            pgb = {"pc": pc, "pd": pd}[bank]
            bank2 = ["pe", "pf"][i % 2]
            pgb2 = {"pe": pe, "pf": pf}[bank2]

            def corr_mm(t, i=i, col0=col0, bank=bank, pgb=pgb, bank2=bank2, pgb2=pgb2):
                if i == 0:
                    p.wait(t, "drow")
                p.bank_acquire(t, bank)
                for k in range(KSS):
                    t.matmul(pgb[:, 0:SLAB], sq[:, k * D + i * 128:k * D + (i + 1) * 128],
                             sdec[:, k * SLAB:(k + 1) * SLAB],
                             start=(k == 0), stop=(k == KSS - 1))
                p.bank_acquire(t, bank2)
                if col0 == 0 and i == 0:
                    p.wait(t, "rdt")
                p.inc(t.matmul(pgb2[:, 0:SLAB], smo[:, i * 128:(i + 1) * 128],
                               srdt[:, col0:col0 + SLAB],
                               start=True, stop=True), "corr", 1)

            block.tensor(corr_mm)

            def y_v(v, i=i, bank=bank, pgb=pgb, bank2=bank2, pgb2=pgb2):
                p.wait(v, "corr")
                p.wait(v, "gbe")
                if i == 0:
                    p.wait(v, "xl")
                seg = slice(i * SLAB, (i + 1) * SLAB)
                v.tensor_add(st1[:, :], slg[:, seg], pgb[:, 0:SLAB])
                p.edge(v, v.tensor_mul(st2[:, :], pgb2[:, 0:SLAB], sgmbc[:, :]))
                v.tensor_mul(st1[:, :], st1[:, :], sglbc[:, :])
                p.edge(v, v.engine_nop())
                v.tensor_add(st1[:, :], st1[:, :], st2[:, :])
                p.edge(v, v.engine_nop())
                v.tensor_add(sy[:, seg], st1[:, :], sx[:, seg])
                p.edge(v, v.engine_nop())
                p.inc(v.tensor_add(sw_[:, seg], sy[:, seg], sx[:, seg]), "yv", 1)

            block.vector(y_v)
            p.bank_release(bank, "yv")
            p.bank_release(bank2, "yv")

        def ysq(s):
            p.wait(s, "yv")
            p.inc(s.activation(swb2[:, :], sy[:, :], AF.Square), "ysq", 1)

        block.scalar(ysq)

        def ystat(t):
            p.wait(t, "ysq")
            p.bank_acquire(t, "pa")
            for i in range(NDT):
                ins = t.matmul(pa[0:1, 0:SLAB], sones[:, :],
                               swb2[:, i * SLAB:(i + 1) * SLAB],
                               start=(i == 0), stop=(i == NDT - 1))
            p.inc(ins, "yst", 1)

        block.tensor(ystat)

        def yrow(v):
            p.wait(v, "yst")
            p.inc(v.tensor_scalar(srow[:, 0:SLAB], pa[0:1, 0:SLAB], 1.0 / D, EPS_RMS,
                                  ALU.mult, ALU.add), "yrw", 1)

        block.vector(yrow)
        p.bank_release("pa", "yrw")

        def ysqr(s):
            p.wait(s, "yrw")
            p.edge(s, s.activation(srow[:, 0:SLAB], srow[:, 0:SLAB], AF.Ln), "ssy")
            p.inc(s.activation(srow[:, 0:SLAB], srow[:, 0:SLAB], AF.Exp, scale=-0.5),
                  "yrc", 1)

        block.scalar(ysqr)

        def ybc(t):
            p.wait(t, "yrc")
            p.bank_acquire(t, "pb")
            p.inc(t.matmul(pb[:, 0:SLAB], sonesrf[:, :], srow[:, 0:SLAB],
                           start=True, stop=True), "ybc", 1)

        block.tensor(ybc)

        def yn_v(v):
            p.wait(v, "ybc")
            for i in range(NDT):
                ins = v.tensor_mul(syn[:, i * SLAB:(i + 1) * SLAB],
                                   sy[:, i * SLAB:(i + 1) * SLAB], pb[:, 0:SLAB])
            p.inc(ins, "ynv", 1)

        block.vector(yn_v)
        p.bank_release("pb", "ynv")

        for half in (range(2) if CST >= 4 else []):
            def ld_half(g, half=half, sl=sl):
                if sl == 0 and half == 0:
                    p.wait(g, "qev")
                else:
                    p.wait(g, "dmm")
                p.inc(g.dma_start(
                    out=swgt[:, 0:NDT * 2048].rearrange("p (d n) -> p d n", d=NDT),
                    in_=w_up.rearrange("(d p) n -> p d n", p=128)[:, :, half * 2048:(half + 1) * 2048]),
                    "wh", 16)
                p.inc(g.dma_start(
                    out=swgt[:, NDT * 2048:32 * D].rearrange("p (c n) -> p c n", c=16),
                    in_=w_dn.rearrange("(c p) n -> p c n", p=128)[:, half * 16:(half + 1) * 16, :]),
                    "wh", 16)

            block.gpsimd(ld_half)

            for mg in range(4):
                CUB = ["pc", "pd", "pe", "pf"]
                CUP = {"pc": pc, "pd": pd, "pe": pe, "pf": pf}

                def up_mm(t, mg=mg):
                    if mg == 0:
                        p.wait(t, "wh")
                        p.wait(t, "ynv")
                    for mi in range(4):
                        m = mg * 4 + mi
                        p.bank_acquire(t, CUB[mi])
                        for d in range(NDT):
                            ins = t.matmul(CUP[CUB[mi]][:, 0:SLAB],
                                           swgt[:, d * 2048 + m * 128:d * 2048 + (m + 1) * 128],
                                           syn[:, d * SLAB:(d + 1) * SLAB],
                                           start=(d == 0), stop=(d == NDT - 1))
                    p.inc(ins, "umm", 1)

                block.tensor(up_mm)

                def up_ep(s, mg=mg, half=half):
                    p.wait(s, "umm")
                    p.wait(s, "uev2", p.val("uev2"))
                    for mi in range(4):
                        m = mg * 4 + mi
                        bcol = half * 16 + m
                        s.activation(sct4[:, mi * SLAB:(mi + 1) * SLAB],
                                     CUP[CUB[mi]][:, 0:SLAB], AF.Sigmoid,
                                     bias=sbup[:, bcol:bcol + 1])
                        ins = s.activation(scc4[:, mi * SLAB:(mi + 1) * SLAB],
                                           CUP[CUB[mi]][:, 0:SLAB], AF.Identity,
                                           bias=sbup[:, bcol:bcol + 1])
                    p.inc(ins, "uep", 1)

                block.scalar(up_ep)
                for mi in range(4):
                    p.bank_release(CUB[mi], "uep")

                def up_v(v, mg=mg):
                    p.wait(v, "uep")
                    for mi in range(4):
                        m = mg * 4 + mi
                        ins = v.tensor_mul(sh_[:, m * SLAB:(m + 1) * SLAB],
                                           sct4[:, mi * SLAB:(mi + 1) * SLAB],
                                           scc4[:, mi * SLAB:(mi + 1) * SLAB])
                    p.inc(ins, "uev2", 1)

                block.vector(up_v)

            for ig in range(4):
                def dn_mm(t, ig=ig, half=half):
                    if ig == 0:
                        p.wait(t, "uev2")
                    for ii in range(2):
                        i = ig * 2 + ii
                        bank = ["pa2", "pb2"][ii]
                        pgb = {"pa2": pa, "pb2": pb}[bank]
                        p.bank_acquire(t, bank)
                        if half == 0:
                            t.matmul(pgb[:, 0:SLAB], sbdn[0:1, i * 128:(i + 1) * 128],
                                     sonesr[:, :], start=True, stop=False)
                        for c in range(16):
                            ins = t.matmul(pgb[:, 0:SLAB],
                                           swgt[:, NDT * 2048 + c * D + i * 128:NDT * 2048 + c * D + (i + 1) * 128],
                                           sh_[:, c * SLAB:(c + 1) * SLAB],
                                           start=(half == 1 and c == 0), stop=(c == 15))
                    p.inc(ins, "dmm", 1)

                block.tensor(dn_mm)

                def dn_v(v, ig=ig):
                    p.wait(v, "dmm")
                    for ii in range(2):
                        i = ig * 2 + ii
                        pgb = {0: pa, 1: pb}[ii]
                        seg = slice(i * SLAB, (i + 1) * SLAB)
                        ins = v.tensor_add(sw_[:, seg], sw_[:, seg], pgb[:, 0:SLAB])
                    p.inc(ins, "dnv", 1)

                block.vector(dn_v)
                p.bank_release("pa2", "dnv")
                p.bank_release("pb2", "dnv")
                p.bank_release("pa", "dnv")
                p.bank_release("pb", "dnv")

        if CST < 5:
            continue
        def lnw(s):
            p.wait(s, "dnv")
            s.activation(swb[:, :], sw_[:, :], AF.Copy)
            p.inc(s.activation(swb2[:, :], sw_[:, :], AF.Square), "lnc", 1)

        block.scalar(lnw)

        def lnstat(t):
            p.wait(t, "lnc")
            p.bank_acquire(t, "pa")
            for i in range(NDT):
                t.matmul(pa[0:1, 0:SLAB], sones[:, :], swb[:, i * SLAB:(i + 1) * SLAB],
                         start=(i == 0), stop=(i == NDT - 1))
            p.bank_acquire(t, "pb")
            for i in range(NDT):
                ins = t.matmul(pb[0:1, 0:SLAB], sones[:, :], swb2[:, i * SLAB:(i + 1) * SLAB],
                               start=(i == 0), stop=(i == NDT - 1))
            p.inc(ins, "lns", 1)

        block.tensor(lnstat)

        def lnrow(v):
            p.wait(v, "lns")
            p.edge(v, v.tensor_scalar(srow[:, 0:SLAB], pa[0:1, 0:SLAB], 1.0 / D, None, ALU.mult))
            p.edge(v, v.tensor_scalar(srow[:, SLAB:2 * SLAB], pb[0:1, 0:SLAB], 1.0 / D, None, ALU.mult))
            p.edge(v, v.tensor_mul(srow[:, 2 * SLAB:3 * SLAB], srow[:, 0:SLAB], srow[:, 0:SLAB]))
            p.edge(v, v.tensor_sub(srow[:, SLAB:2 * SLAB], srow[:, SLAB:2 * SLAB],
                                   srow[:, 2 * SLAB:3 * SLAB]))
            p.inc(v.tensor_scalar(srow[:, SLAB:2 * SLAB], srow[:, SLAB:2 * SLAB], 1.0,
                                  EPS_LN, ALU.mult, ALU.add), "lnr", 1)

        block.vector(lnrow)
        p.bank_release("pa", "lnr")
        p.bank_release("pb", "lnr")

        def lnsq(s):
            p.wait(s, "lnr")
            p.edge(s, s.activation(srow[:, SLAB:2 * SLAB], srow[:, SLAB:2 * SLAB],
                                   AF.Ln), "ssy")
            p.inc(s.activation(srow[:, SLAB:2 * SLAB], srow[:, SLAB:2 * SLAB],
                               AF.Exp, scale=-0.5), "lnq", 1)

        block.scalar(lnsq)

        def lnrec(v):
            p.wait(v, "lnq")
            p.inc(v.tensor_mul(srow[:, 2 * SLAB:3 * SLAB], srow[:, 0:SLAB],
                               srow[:, SLAB:2 * SLAB]), "lnm", 1)

        block.vector(lnrec)

        def lnbc(t):
            p.wait(t, "lnm")
            p.bank_acquire(t, "pc")
            t.matmul(pc[:, 0:SLAB], sonesrf[:, :], srow[:, SLAB:2 * SLAB],
                     start=True, stop=True)
            p.bank_acquire(t, "pd")
            p.inc(t.matmul(pd[:, 0:SLAB], sonesrf[:, :], srow[:, 2 * SLAB:3 * SLAB],
                           start=True, stop=True), "lnw2", 1)

        block.tensor(lnbc)

        def x1_v(v):
            p.wait(v, "lnw2")
            for i in range(NDT):
                v.tensor_mul(sx1[:, i * SLAB:(i + 1) * SLAB],
                             sw_[:, i * SLAB:(i + 1) * SLAB], pc[:, 0:SLAB])
            p.edge(v, v.engine_nop())
            for i in range(NDT):
                ins = v.tensor_sub(sx1[:, i * SLAB:(i + 1) * SLAB],
                                   sx1[:, i * SLAB:(i + 1) * SLAB], pd[:, 0:SLAB])
            p.inc(ins, "lnb", 1)

        block.vector(x1_v)
        p.bank_release("pc", "lnb")
        p.bank_release("pd", "lnb")

        def x1_store(g, sl=sl):
            p.wait(g, "lnb")
            p.inc(g.dma_start(
                out=o_x1[:, sl * SLAB:(sl + 1) * SLAB].rearrange("(d p) n -> p d n", p=128),
                in_=sx1.rearrange("p (d n) -> p d n", d=NDT)), "x1st", 16)

        block.gpsimd(x1_store)

        def x1_fence(v):
            p.wait(v, "x1st")

        block.vector(x1_fence)

    def done(g):
        p.wait(g, "x1st")

    block.gpsimd(done)


def build_launch_C():
    p = Prog()
    nc = p.nc
    aps = dict(
        xT=nc.dram_tensor("xT", [D, SH], F32, kind="ExternalInput").ap(),
        longT=nc.dram_tensor("longT", [D, SH], F32, kind="ExternalInput").ap(),
        finAG=nc.dram_tensor("finAG", [128, 128], F32, kind="ExternalInput").ap(),
        gl_row=nc.dram_tensor("gl_row", [1, SH], F32, kind="ExternalInput").ap(),
        gm_row=nc.dram_tensor("gm_row", [1, SH], F32, kind="ExternalInput").ap(),
        readTF=nc.dram_tensor("readTF", [32, T], F32, kind="ExternalInput").ap(),
        owns_bc=nc.dram_tensor("owns_bc", [32, 64], F32, kind="ExternalInput").ap(),
        w_stout=nc.dram_tensor("w_stout", [KSS * D, D], F32, kind="ExternalInput").ap(),
        w_mo=nc.dram_tensor("w_mo", [32, D], F32, kind="ExternalInput").ap(),
        w_up=nc.dram_tensor("w_up", [D, 4 * D], F32, kind="ExternalInput").ap(),
        w_dn=nc.dram_tensor("w_dn", [4 * D, D], F32, kind="ExternalInput").ap(),
        b_up=nc.dram_tensor("b_up", [128, 32], F32, kind="ExternalInput").ap(),
        b_dn=nc.dram_tensor("b_dn", [1, D], F32, kind="ExternalInput").ap(),
        logit_row=nc.dram_tensor("logit_row", [1, KSS], F32, kind="ExternalInput").ap(),
        ecp_row=nc.dram_tensor("ecp_row", [1, 4], F32, kind="ExternalInput").ap(),
        mask44=nc.dram_tensor("mask44", [4, 4], F32, kind="ExternalInput").ap(),
        iotap1=nc.dram_tensor("iotap1", [1, SH], F32, kind="ExternalInput").ap(),
        o_x1=nc.dram_tensor("o_x1", [D, SH], F32, kind="ExternalOutput").ap(),
    )
    p.begin()
    emit_C(p, **aps)
    p.close()
    return p.nc



def emit_D(p, x1T, W1, b1, W2, b2, Wr, br, sel8F, eyeF, o_x2):
    nc = p.nc
    SLAB = 512
    TG = 1024
    NSL = TG // SLAB

    sx1b = p.sb("sx1b", [128, NDT * TG], BF16)
    sacc = p.sb("sacc", [128, NDT * TG], F32)
    swgt2 = [p.sb("swgtA", [128, 16 * D], BF16),
             p.sb("swgtB", [128, 16 * D], BF16)]
    swr = p.sb("swr", [128, NDT * 8], BF16)
    sbr = p.sb("sbr", [1, 8], BF16)
    sb1 = p.sb("sb1", [128, 8 * 32], F32)
    sb2 = p.sb("sb2", [8, D], BF16)
    ssel = p.sb("ssel", [8, 8 * 128], BF16)
    srtm = p.sb("srtm", [128, 8], F32)
    srr = p.sb("srr", [8, TG], BF16)
    srowm = p.sb("srowm", [128, 4], F32)
    sh_ = p.sb("sh_", [128, 16 * SLAB], BF16)
    shtmp = p.sb("shtmp", [128, SLAB], BF16)
    sxs = p.sb("sxs", [128, NDT * SLAB], F32)

    srow = p.sb("srow", [1, 3 * SLAB], F32)
    sones = p.sb("sones", [128, 1], BF16)
    sonesr = p.sb("sonesr", [1, SLAB], BF16)
    sonesrf = p.sb("sonesrf", [1, 128], F32)
    sonesrb = p.sb("sonesrb", [1, 128], BF16)
    shtmp4 = p.sb("shtmp4", [128, 4 * SLAB], BF16)
    seyeb = p.sb("seyeb", [128, 128], BF16)

    pa = p.psum("pa")
    pb = p.psum("pb")
    pc = p.psum("pc")
    pd = p.psum("pd")
    pe = p.psum("pe")
    pf = p.psum("pf")
    pbt8 = p.ent(nc.psum_tensor(p._part + "pbt8", [128, 1024], BF16))

    block = p.block

    def loads(g):
        p.inc(g.dma_start(out=seyeb[:, :], in_=eyeF), "wl", 16)
        p.inc(g.dma_start(out=swr.rearrange("p (d n) -> p d n", d=NDT),
                          in_=Wr.rearrange("(d p) n -> p d n", p=128)), "wl", 16)
        p.inc(g.dma_start(out=sbr[:, :], in_=br), "wl", 16)
        p.inc(g.dma_start(out=sb1[:, :], in_=b1), "wl", 16)
        p.inc(g.dma_start(out=sb2[:, :], in_=b2), "wl", 16)
        p.inc(g.dma_start(out=ssel[:, :], in_=sel8F), "wl", 16)

    block.gpsimd(loads)

    def init_v(v):
        v.memset(sones[:, :], 1.0)
        v.memset(sonesr[:, :], 1.0)
        v.memset(sonesrb[:, :], 1.0)
        p.inc(v.memset(sonesrf[:, :], 1.0), "iv", 1)

    block.vector(init_v)

    for tg in range(2):
        tcol = tg * TG

        def ld_x(g, tg=tg, tcol=tcol):
            if tg > 0:
                p.wait(g, "x2st")
            p.inc(g.dma_start(out=sx1b.rearrange("p (d n) -> p d n", d=NDT),
                              in_=x1T[:, tcol:tcol + TG].rearrange("(d p) n -> p d n", p=128)),
                  "xg", 16)

        block.gpsimd(ld_x)

        for tt in range(TG // 128):
            def r_mm(t, tt=tt, tg=tg):
                if tt == 0:
                    p.wait(t, "xg")
                    p.wait(t, "wl")
                    p.wait(t, "iv")
                p.bank_acquire(t, "pa")
                for d in range(NDT):
                    t.matmul(pa[:, 0:8], sx1b[:, d * TG + tt * 128:d * TG + tt * 128 + 128],
                             swr[:, d * 8:(d + 1) * 8],
                             start=(d == 0), stop=False)
                p.inc(t.matmul(pa[:, 0:8], sonesrb[:, :], sbr[:, :],
                               start=False, stop=True), "rmm", 1)

            block.tensor(r_mm)

            def r_v(v, tt=tt):
                p.wait(v, "rmm")
                p.inc(v.reduce_max(srowm[:, 0:1], pa[:, 0:8], axis=AX.X), "vsy", 1)
                p.wait(v, "vsy")
                v.tensor_scalar(srtm[:, :], pa[:, 0:8], srowm[:, 0:1], None, ALU.is_ge)
                p.edge(v, v.engine_nop())
                v.scalar_tensor_tensor(srtm[:, :], srtm[:, :], -1e9, pa[:, 0:8],
                                       ALU.mult, ALU.add)
                p.edge(v, v.engine_nop())
                p.inc(v.reduce_max(srowm[:, 1:2], srtm[:, :], axis=AX.X), "vsy", 1)
                p.wait(v, "vsy")
                v.tensor_scalar(srtm[:, :], pa[:, 0:8], srowm[:, 1:2], None, ALU.is_ge)
                p.edge(v, v.engine_nop())
                v.tensor_mul(srtm[:, :], srtm[:, :], pa[:, 0:8])
                p.edge(v, v.engine_nop())
                p.inc(v.reduce_max(srowm[:, 2:3], srtm[:, :], axis=AX.X), "vsy", 1)
                p.wait(v, "vsy")
                p.inc(v.tensor_scalar(srowm[:, 2:3], srowm[:, 2:3], -1.0, None, ALU.mult),
                      "rv1", 1)

            block.vector(r_v)

            def r_s(s, tt=tt):
                p.wait(s, "rv1")
                p.inc(s.activation(srtm[:, :], srtm[:, :], AF.Exp, bias=srowm[:, 2:3]),
                      "rexp", 1)

            block.scalar(r_s)

            def r_v2(v, tt=tt):
                p.wait(v, "rexp")
                p.inc(v.reduce_sum(srowm[:, 3:4], srtm[:, :], axis=AX.X), "vsy", 1)
                p.wait(v, "vsy")
                p.inc(v.reciprocal(srowm[:, 3:4], srowm[:, 3:4]), "vsy", 1)
                p.wait(v, "vsy")
                p.inc(v.tensor_scalar(srtm[:, :], srtm[:, :], srowm[:, 3:4], None,
                                      ALU.mult), "rv2", 1)

            block.vector(r_v2)
            p.bank_release("pa", "rv2")

            def r_cp(v, tt=tt):
                p.wait(v, "rv2")
                p.inc(v.tensor_copy(shtmp[:, 0:8], srtm[:, :]), "rcp", 1)

            block.vector(r_cp)

            def r_tr2(t, tt=tt):
                p.wait(t, "rcp")
                p.bank_acquire(t, "pbt")
                p.inc(t.transpose(pbt8[0:8, 0:128], shtmp[:, 0:8], seyeb[:, :]), "rtr", 1)

            block.tensor(r_tr2)

            def r_ev(v, tt=tt):
                p.wait(v, "rtr")
                p.inc(v.tensor_copy(srr[:, tt * 128:(tt + 1) * 128], pbt8[0:8, 0:128]),
                      "rev", 1)

            block.vector(r_ev)
            p.bank_release("pbt", "rev")

        for sl in range(NSL):
            for i in range(NDT):
                bank = ["pc", "pd"][(sl * NDT + i) % 2]
                pgb = {"pc": pc, "pd": pd}[bank]

                def b2_mm(t, sl=sl, i=i, bank=bank, pgb=pgb):
                    if sl == 0 and i == 0:
                        p.wait(t, "rev")
                    p.bank_acquire(t, bank)
                    p.inc(t.matmul(pgb[:, 0:SLAB], sb2[:, i * 128:(i + 1) * 128],
                                   srr[:, sl * SLAB:(sl + 1) * SLAB],
                                   start=True, stop=True), "b2m", 1)

                block.tensor(b2_mm)

                def b2_ev(v, sl=sl, i=i, bank=bank, pgb=pgb):
                    p.wait(v, "b2m")
                    p.inc(v.tensor_copy(sacc[:, sl * NDT * SLAB + i * SLAB:sl * NDT * SLAB + (i + 1) * SLAB],
                                        pgb[:, 0:SLAB]), "b2e", 1)

                block.vector(b2_ev)
                p.bank_release(bank, "b2e")

        dnv_ckpt = {}
        for idx in range(32):
            e, half, q = idx // 4, (idx // 2) % 2, idx % 2
            hid0 = half * 2048 + q * 1024
            c0 = half * 16 + q * 8
            swgt = swgt2[idx % 2]
            dnv_ckpt[idx] = p.val("dnv")

            def ld_w(g, e=e, hid0=hid0, c0=c0, idx=idx, swgt=swgt):
                if idx >= 2:
                    p.wait(g, "dnv", dnv_ckpt[idx - 1])
                else:
                    p.wait(g, "b2e")
                p.inc(g.dma_start(
                    out=swgt[:, 0:NDT * 1024].rearrange("p (d n) -> p d n", d=NDT),
                    in_=W1[e].rearrange("(d p) n -> p d n", p=128)[:, :, hid0:hid0 + 1024]),
                    "wh", 16)
                p.inc(g.dma_start(
                    out=swgt[:, NDT * 1024:16 * D].rearrange("p (c n) -> p c n", c=8),
                    in_=W2[e].rearrange("(c p) n -> p c n", p=128)[:, c0:c0 + 8, :]),
                    "wh", 16)

            block.gpsimd(ld_w)

            for sl in range(NSL):
                def rbc_mm(t, e=e, sl=sl, idx=idx):
                    if sl == 0:
                        p.wait(t, "wh")
                    p.bank_acquire(t, "pe")
                    p.inc(t.matmul(pe[:, 0:SLAB], ssel[:, e * 128:(e + 1) * 128],
                                   srr[:, sl * SLAB:(sl + 1) * SLAB],
                                   start=True, stop=True), "rbc", 1)

                block.tensor(rbc_mm)

                UPB = ["pc", "pd", "pa", "pb"]
                UPP = {"pc": pc, "pd": pd, "pa": pa, "pb": pb}
                for mg in range(2):
                    def up_mm(t, mg=mg, sl=sl, swgt=swgt):
                        for mi in range(4):
                            m = mg * 4 + mi
                            bank = UPB[mi]
                            pgb = UPP[bank]
                            p.bank_acquire(t, bank)
                            for d in range(NDT):
                                ins = t.matmul(pgb[:, 0:SLAB],
                                               swgt[:, d * 1024 + m * 128:d * 1024 + (m + 1) * 128],
                                               sx1b[:, d * TG + sl * SLAB:d * TG + (sl + 1) * SLAB],
                                               start=(d == 0), stop=(d == NDT - 1))
                        p.inc(ins, "umm", 1)

                    block.tensor(up_mm)

                    def up_ep(s, mg=mg, e=e, half=half, q=q):
                        p.wait(s, "umm")
                        p.wait(s, "usc", p.val("usc"))
                        for mi in range(4):
                            m = mg * 4 + mi
                            bcol = e * 32 + half * 16 + q * 8 + m
                            ins = s.activation(shtmp4[:, mi * SLAB:(mi + 1) * SLAB],
                                               UPP[UPB[mi]][:, 0:SLAB], AF.Relu,
                                               bias=sb1[:, bcol:bcol + 1])
                        p.inc(ins, "uep", 1)

                    block.scalar(up_ep)
                    for mi in range(4):
                        p.bank_release(UPB[mi], "uep")

                    def up_scale(v, mg=mg):
                        p.wait(v, "uep")
                        for mi in range(4):
                            m = mg * 4 + mi
                            ins = v.tensor_mul(sh_[:, m * SLAB:(m + 1) * SLAB],
                                               shtmp4[:, mi * SLAB:(mi + 1) * SLAB],
                                               pe[:, 0:SLAB])
                        p.inc(ins, "usc", 1)

                    block.vector(up_scale)

                p.bank_release("pe", "usc")

                for ig in range(4):
                    def dn_mm(t, ig=ig, swgt=swgt):
                        if ig == 0:
                            p.wait(t, "usc")
                        for ii in range(2):
                            i = ig * 2 + ii
                            bank = ["pe2", "pf"][ii]
                            pgb = {"pe2": pe, "pf": pf}[bank]
                            p.bank_acquire(t, bank)
                            for c in range(8):
                                ins = t.matmul(pgb[:, 0:SLAB],
                                               swgt[:, NDT * 1024 + c * D + i * 128:NDT * 1024 + c * D + (i + 1) * 128],
                                               sh_[:, c * SLAB:(c + 1) * SLAB],
                                               start=(c == 0), stop=(c == 7))
                        p.inc(ins, "dmm", 1)

                    block.tensor(dn_mm)

                    def dn_v(v, ig=ig, sl=sl):
                        p.wait(v, "dmm")
                        for ii in range(2):
                            i = ig * 2 + ii
                            pgb = {0: pe, 1: pf}[ii]
                            seg = slice(sl * NDT * SLAB + i * SLAB, sl * NDT * SLAB + (i + 1) * SLAB)
                            ins = v.tensor_add(sacc[:, seg], sacc[:, seg], pgb[:, 0:SLAB])
                        p.inc(ins, "dnv", 1)

                    block.vector(dn_v)
                    p.bank_release("pe2", "dnv")
                    p.bank_release("pf", "dnv")
                    p.bank_release("pe", "dnv")

        for sl in range(NSL):
            col0 = tcol + sl * SLAB

            def ld_xs(g, sl=sl, col0=col0):
                p.wait(g, "lnb", p.val("lnb"))
                p.wait(g, "x2st", p.val("x2st"))
                p.inc(g.dma_start(out=sxs.rearrange("p (d n) -> p d n", d=NDT),
                                  in_=x1T[:, col0:col0 + SLAB].rearrange("(d p) n -> p d n", p=128)),
                      "xsl", 16)

            block.sync(ld_xs)

            def vsum(v, sl=sl):
                p.wait(v, "dnv")
                p.wait(v, "xsl")
                for i in range(NDT):
                    seg = slice(sl * NDT * SLAB + i * SLAB, sl * NDT * SLAB + (i + 1) * SLAB)
                    ins = v.tensor_add(sxs[:, i * SLAB:(i + 1) * SLAB],
                                       sxs[:, i * SLAB:(i + 1) * SLAB], sacc[:, seg])
                p.inc(ins, "vsum", 1)

            block.vector(vsum)

            def lnw(s):
                p.wait(s, "vsum")
                swb = sh_[:, 0:NDT * SLAB]
                swb2 = sh_[:, NDT * SLAB:2 * NDT * SLAB]
                s.activation(swb, sxs[:, :], AF.Copy)
                p.inc(s.activation(swb2, sxs[:, :], AF.Square), "lnc", 1)

            block.scalar(lnw)

            def lnstat(t):
                p.wait(t, "lnc")
                swb = sh_[:, 0:NDT * SLAB]
                swb2 = sh_[:, NDT * SLAB:2 * NDT * SLAB]
                p.bank_acquire(t, "pa")
                for i in range(NDT):
                    t.matmul(pa[0:1, 0:SLAB], sones[:, :], swb[:, i * SLAB:(i + 1) * SLAB],
                             start=(i == 0), stop=(i == NDT - 1))
                p.bank_acquire(t, "pb")
                for i in range(NDT):
                    ins = t.matmul(pb[0:1, 0:SLAB], sones[:, :], swb2[:, i * SLAB:(i + 1) * SLAB],
                                   start=(i == 0), stop=(i == NDT - 1))
                p.inc(ins, "lns", 1)

            block.tensor(lnstat)

            def lnrow(v):
                p.wait(v, "lns")
                p.edge(v, v.tensor_scalar(srow[:, 0:SLAB], pa[0:1, 0:SLAB], 1.0 / D, None, ALU.mult))
                p.edge(v, v.tensor_scalar(srow[:, SLAB:2 * SLAB], pb[0:1, 0:SLAB], 1.0 / D, None, ALU.mult))
                p.edge(v, v.tensor_mul(srow[:, 2 * SLAB:3 * SLAB], srow[:, 0:SLAB], srow[:, 0:SLAB]))
                p.edge(v, v.tensor_sub(srow[:, SLAB:2 * SLAB], srow[:, SLAB:2 * SLAB],
                                       srow[:, 2 * SLAB:3 * SLAB]))
                p.inc(v.tensor_scalar(srow[:, SLAB:2 * SLAB], srow[:, SLAB:2 * SLAB], 1.0,
                                      EPS_LN, ALU.mult, ALU.add), "lnr", 1)

            block.vector(lnrow)
            p.bank_release("pa", "lnr")
            p.bank_release("pb", "lnr")

            def lnsq(s):
                p.wait(s, "lnr")
                p.edge(s, s.activation(srow[:, SLAB:2 * SLAB], srow[:, SLAB:2 * SLAB],
                                       AF.Ln), "ssy")
                p.inc(s.activation(srow[:, SLAB:2 * SLAB], srow[:, SLAB:2 * SLAB],
                                   AF.Exp, scale=-0.5), "lnq", 1)

            block.scalar(lnsq)

            def lnrec(v):
                p.wait(v, "lnq")
                p.inc(v.tensor_mul(srow[:, 2 * SLAB:3 * SLAB], srow[:, 0:SLAB],
                                   srow[:, SLAB:2 * SLAB]), "lnm", 1)

            block.vector(lnrec)

            def lnbc(t):
                p.wait(t, "lnm")
                p.bank_acquire(t, "pc")
                t.matmul(pc[:, 0:SLAB], sonesrf[:, :], srow[:, SLAB:2 * SLAB],
                         start=True, stop=True)
                p.bank_acquire(t, "pd")
                p.inc(t.matmul(pd[:, 0:SLAB], sonesrf[:, :], srow[:, 2 * SLAB:3 * SLAB],
                               start=True, stop=True), "lnw2", 1)

            block.tensor(lnbc)

            def x2_v(v):
                p.wait(v, "lnw2")
                for i in range(NDT):
                    v.tensor_mul(sxs[:, i * SLAB:(i + 1) * SLAB],
                                 sxs[:, i * SLAB:(i + 1) * SLAB], pc[:, 0:SLAB])
                p.edge(v, v.engine_nop())
                for i in range(NDT):
                    ins = v.tensor_sub(sxs[:, i * SLAB:(i + 1) * SLAB],
                                       sxs[:, i * SLAB:(i + 1) * SLAB], pd[:, 0:SLAB])
                p.inc(ins, "lnb", 1)

            block.vector(x2_v)
            p.bank_release("pc", "lnb")
            p.bank_release("pd", "lnb")

            def x2_store(g, col0=col0):
                p.wait(g, "lnb")
                p.inc(g.dma_start(
                    out=o_x2[:, col0:col0 + SLAB].rearrange("(d p) n -> p d n", p=128),
                    in_=sxs.rearrange("p (d n) -> p d n", d=NDT)), "x2st", 16)

            block.gpsimd(x2_store)

    def done(g):
        p.wait(g, "x2st")

    block.gpsimd(done)


def build_launch_D():
    p = Prog()
    nc = p.nc
    aps = dict(
        x1T=nc.dram_tensor("x1T", [D, SH], F32, kind="ExternalInput").ap(),
        W1=nc.dram_tensor("W1", [8, D, 4 * D], F32, kind="ExternalInput").ap(),
        b1=nc.dram_tensor("b1", [128, 8 * 32], F32, kind="ExternalInput").ap(),
        W2=nc.dram_tensor("W2", [8, 4 * D, D], F32, kind="ExternalInput").ap(),
        b2=nc.dram_tensor("b2", [8, D], F32, kind="ExternalInput").ap(),
        Wr=nc.dram_tensor("Wr", [D, 8], F32, kind="ExternalInput").ap(),
        br=nc.dram_tensor("br", [1, 8], F32, kind="ExternalInput").ap(),
        sel8F=nc.dram_tensor("sel8F", [8, 8 * 128], F32, kind="ExternalInput").ap(),
        eyeF=nc.dram_tensor("eyeF", [128, 128], F32, kind="ExternalInput").ap(),
        o_x2=nc.dram_tensor("o_x2", [D, SH], F32, kind="ExternalOutput").ap(),
    )
    p.begin()
    emit_D(p, **aps)
    p.close()
    return p.nc



SCALE3 = float(np.sqrt(3.0 / 64.0))


def emit_E(p, x2T, wkv, wqT, woT, o_x3):
    nc = p.nc
    SLAB = 512
    NSL = SH // SLAB

    e_in = nc.dram_tensor(p._part + "e_in", [64, 64], F32)
    e_out = nc.dram_tensor(p._part + "e_out", [64, 64], F32)

    sx2b = p.sb("sx2b", [128, NDT * SH], BF16)
    swkv = p.sb("swkv", [128, NDT * 128], BF16)
    swq = p.sb("swq", [128, NDT * 64], BF16)
    swo = p.sb("swo", [64, D], BF16)
    skv = p.sb("skv", [128, 128], BF16)
    sqt = p.sb("sqt", [64, SH], BF16)
    sctx = p.sb("sctx", [64, SH], BF16)
    sE = p.sb("sE", [64, 64], F32)
    sEb = p.sb("sEb", [64, 64], BF16)
    sxs = p.sb("sxs", [128, NDT * SLAB], F32)
    swb = p.sb("swb", [128, NDT * SLAB], BF16)
    swb2 = p.sb("swb2", [128, NDT * SLAB], BF16)
    srow = p.sb("srow", [1, 3 * SLAB], F32)
    sones = p.sb("sones", [128, 1], BF16)
    sonesrf = p.sb("sonesrf", [1, 128], F32)
    sx3o = p.sb("sx3o", [128, NDT * SLAB], F32)

    pa = p.psum("pa")
    pb = p.psum("pb")
    pc = p.psum("pc")
    pd = p.psum("pd")
    pe_ = p.psum("pe_")
    pf = p.psum("pf")

    block = p.block

    def loads(g):
        p.inc(g.dma_start(out=sx2b.rearrange("p (d n) -> p d n", d=NDT),
                          in_=x2T.rearrange("(d p) n -> p d n", p=128)), "wl", 16)
        p.inc(g.dma_start(out=swkv.rearrange("p (d n) -> p d n", d=NDT),
                          in_=wkv.rearrange("(d p) n -> p d n", p=128)), "wl", 16)
        p.inc(g.dma_start(out=swq.rearrange("p (d n) -> p d n", d=NDT),
                          in_=wqT.rearrange("(d p) n -> p d n", p=128)), "wl", 16)
        p.inc(g.dma_start(out=swo[:, :], in_=woT), "wl", 16)

    block.gpsimd(loads)

    def init_v(v):
        v.memset(sones[:, :], 1.0)
        p.inc(v.memset(sonesrf[:, :], 1.0), "iv", 1)

    block.vector(init_v)

    for sl in range(NSL):
        def q_mm(t, sl=sl):
            if sl == 0:
                p.wait(t, "wl")
            p.bank_acquire(t, ["pa", "pb"][sl % 2])
            ps = [pa, pb][sl % 2]
            for d in range(NDT):
                ins = t.matmul(ps[0:64, 0:SLAB], swq[:, d * 64:(d + 1) * 64],
                               sx2b[:, d * SH + sl * SLAB:d * SH + (sl + 1) * SLAB],
                               start=(d == 0), stop=(d == NDT - 1))
            p.inc(ins, "qmm", 1)

        block.tensor(q_mm)

        def q_ev(v, sl=sl):
            p.wait(v, "qmm")
            ps = [pa, pb][sl % 2]
            p.inc(v.tensor_copy(sqt[:, sl * SLAB:(sl + 1) * SLAB], ps[0:64, 0:SLAB]),
                  "qev", 1)

        block.vector(q_ev)
        p.bank_release(["pa", "pb"][sl % 2], "qev")

    for tt in range(SH // 128):
        def kv_mm(t, tt=tt):
            p.bank_acquire(t, ["pc", "pd"][tt % 2])
            ps = [pc, pd][tt % 2]
            for d in range(NDT):
                ins = t.matmul(ps[:, 0:128], sx2b[:, d * SH + tt * 128:d * SH + tt * 128 + 128],
                               swkv[:, d * 128:(d + 1) * 128],
                               start=(d == 0), stop=(d == NDT - 1))
            p.inc(ins, "kvm", 1)

        block.tensor(kv_mm)

        def kv_ev(v, tt=tt):
            p.wait(v, "kvm")
            p.inc(v.tensor_copy(skv[:, :], [pc, pd][tt % 2][:, 0:128]), "kve", 1)

        block.vector(kv_ev)
        p.bank_release(["pc", "pd"][tt % 2], "kve")

        def e_mm(t, tt=tt):
            p.wait(t, "kve")
            if tt == 0:
                p.bank_acquire(t, "pe_")
            p.inc(t.matmul(pe_[0:64, 0:64], skv[:, 0:64], skv[:, 64:128],
                           start=(tt == 0), stop=(tt == SH // 128 - 1)), "emm", 1)

        block.tensor(e_mm)

    def e_ev(v):
        p.wait(v, "emm")
        p.inc(v.tensor_copy(sE[:, :], pe_[0:64, 0:64]), "eev", 1)

    block.vector(e_ev)
    p.bank_release("pe_", "eev")

    def e_coll(g):
        p.wait(g, "eev")
        p.inc(g.dma_start(out=e_in.ap(), in_=sE[:, :]), "ecp", 16)
        p.wait(g, "ecp")
        p.inc(g.collective_compute(
            "AllReduce", mybir.AluOpType.add,
            replica_groups=[[0, 1, 2, 3], [4, 5, 6, 7]],
            ins=[e_in.ap().opt()], outs=[e_out.ap().opt()]), "cc", 1)
        p.wait(g, "cc")
        p.inc(g.dma_start(out=sEb[:, :], in_=e_out.ap()), "erd", 16)

    block.gpsimd(e_coll)

    for sl in range(NSL):
        col0 = sl * SLAB

        def ctx_mm(t, sl=sl):
            if sl == 0:
                p.wait(t, "erd")
            p.bank_acquire(t, "pa")
            p.inc(t.matmul(pa[0:64, 0:SLAB], sEb[:, :], sqt[:, sl * SLAB:(sl + 1) * SLAB],
                           start=True, stop=True), "cmm", 1)

        block.tensor(ctx_mm)

        def ctx_ev(s, sl=sl):
            p.wait(s, "cmm")
            p.inc(s.activation(sctx[:, sl * SLAB:(sl + 1) * SLAB], pa[0:64, 0:SLAB],
                               AF.Copy, scale=SCALE3), "cev", 1)

        block.scalar(ctx_ev)
        p.bank_release("pa", "cev")

        def ld_xs(g, sl=sl, col0=col0):
            p.wait(g, "lnb", p.val("lnb"))
            p.inc(g.dma_start(out=sxs.rearrange("p (d n) -> p d n", d=NDT),
                              in_=x2T[:, col0:col0 + SLAB].rearrange("(d p) n -> p d n", p=128)),
                  "xsl", 16)

        block.sync(ld_xs)

        for i in range(NDT):
            bank = ["pc", "pd"][i % 2]
            pgb = {"pc": pc, "pd": pd}[bank]

            def o_mm(t, i=i, sl=sl, bank=bank, pgb=pgb):
                if i == 0:
                    p.wait(t, "cev")
                p.bank_acquire(t, bank)
                p.inc(t.matmul(pgb[:, 0:SLAB], swo[:, i * 128:(i + 1) * 128],
                               sctx[:, sl * SLAB:(sl + 1) * SLAB],
                               start=True, stop=True), "omm", 1)

            block.tensor(o_mm)

            def o_v(v, i=i, bank=bank, pgb=pgb):
                p.wait(v, "omm")
                if i == 0:
                    p.wait(v, "xsl")
                p.inc(v.tensor_add(sxs[:, i * SLAB:(i + 1) * SLAB],
                                   sxs[:, i * SLAB:(i + 1) * SLAB], pgb[:, 0:SLAB]),
                      "ov", 1)

            block.vector(o_v)
            p.bank_release(bank, "ov")

        def lnw(s):
            p.wait(s, "ov")
            s.activation(swb[:, :], sxs[:, :], AF.Copy)
            p.inc(s.activation(swb2[:, :], sxs[:, :], AF.Square), "lnc", 1)

        block.scalar(lnw)

        def lnstat(t):
            p.wait(t, "lnc")
            p.bank_acquire(t, "pa")
            for i in range(NDT):
                t.matmul(pa[0:1, 0:SLAB], sones[:, :], swb[:, i * SLAB:(i + 1) * SLAB],
                         start=(i == 0), stop=(i == NDT - 1))
            p.bank_acquire(t, "pb")
            for i in range(NDT):
                ins = t.matmul(pb[0:1, 0:SLAB], sones[:, :], swb2[:, i * SLAB:(i + 1) * SLAB],
                               start=(i == 0), stop=(i == NDT - 1))
            p.inc(ins, "lns", 1)

        block.tensor(lnstat)

        def lnrow(v):
            p.wait(v, "lns")
            p.edge(v, v.tensor_scalar(srow[:, 0:SLAB], pa[0:1, 0:SLAB], 1.0 / D, None, ALU.mult))
            p.edge(v, v.tensor_scalar(srow[:, SLAB:2 * SLAB], pb[0:1, 0:SLAB], 1.0 / D, None, ALU.mult))
            p.edge(v, v.tensor_mul(srow[:, 2 * SLAB:3 * SLAB], srow[:, 0:SLAB], srow[:, 0:SLAB]))
            p.edge(v, v.tensor_sub(srow[:, SLAB:2 * SLAB], srow[:, SLAB:2 * SLAB],
                                   srow[:, 2 * SLAB:3 * SLAB]))
            p.inc(v.tensor_scalar(srow[:, SLAB:2 * SLAB], srow[:, SLAB:2 * SLAB], 1.0,
                                  EPS_LN, ALU.mult, ALU.add), "lnr", 1)

        block.vector(lnrow)
        p.bank_release("pa", "lnr")
        p.bank_release("pb", "lnr")

        def lnsq(s):
            p.wait(s, "lnr")
            p.edge(s, s.activation(srow[:, SLAB:2 * SLAB], srow[:, SLAB:2 * SLAB],
                                   AF.Ln), "ssy")
            p.inc(s.activation(srow[:, SLAB:2 * SLAB], srow[:, SLAB:2 * SLAB],
                               AF.Exp, scale=-0.5), "lnq", 1)

        block.scalar(lnsq)

        def lnrec(v):
            p.wait(v, "lnq")
            p.inc(v.tensor_mul(srow[:, 2 * SLAB:3 * SLAB], srow[:, 0:SLAB],
                               srow[:, SLAB:2 * SLAB]), "lnm", 1)

        block.vector(lnrec)

        def lnbc(t):
            p.wait(t, "lnm")
            p.bank_acquire(t, "pc")
            t.matmul(pc[:, 0:SLAB], sonesrf[:, :], srow[:, SLAB:2 * SLAB],
                     start=True, stop=True)
            p.bank_acquire(t, "pd")
            p.inc(t.matmul(pd[:, 0:SLAB], sonesrf[:, :], srow[:, 2 * SLAB:3 * SLAB],
                           start=True, stop=True), "lnw2", 1)

        block.tensor(lnbc)

        def x3_v(v):
            p.wait(v, "lnw2")
            p.wait(v, "x3st")
            for i in range(NDT):
                v.tensor_mul(sx3o[:, i * SLAB:(i + 1) * SLAB],
                             sxs[:, i * SLAB:(i + 1) * SLAB], pc[:, 0:SLAB])
            p.edge(v, v.engine_nop())
            for i in range(NDT):
                ins = v.tensor_sub(sx3o[:, i * SLAB:(i + 1) * SLAB],
                                   sx3o[:, i * SLAB:(i + 1) * SLAB], pd[:, 0:SLAB])
            p.inc(ins, "lnb", 1)

        block.vector(x3_v)
        p.bank_release("pc", "lnb")
        p.bank_release("pd", "lnb")

        def x3_store(g, col0=col0):
            p.wait(g, "lnb")
            p.inc(g.dma_start(
                out=o_x3[:, col0:col0 + SLAB].rearrange("(d p) n -> p d n", p=128),
                in_=sx3o.rearrange("p (d n) -> p d n", d=NDT)), "x3st", 16)

        block.gpsimd(x3_store)

    def done(g):
        p.wait(g, "x3st")

    block.gpsimd(done)


def build_launch_E():
    p = Prog()
    nc = p.nc
    aps = dict(
        x2T=nc.dram_tensor("x2T", [D, SH], F32, kind="ExternalInput").ap(),
        wkv=nc.dram_tensor("wkv", [D, 128], F32, kind="ExternalInput").ap(),
        wqT=nc.dram_tensor("wqT", [D, 64], F32, kind="ExternalInput").ap(),
        woT=nc.dram_tensor("woT", [64, D], F32, kind="ExternalInput").ap(),
        o_x3=nc.dram_tensor("o_x3", [D, SH], F32, kind="ExternalOutput").ap(),
    )
    p.begin()
    emit_E(p, **aps)
    p.close()
    return p.nc



def emit_G(p, x3T, wrT, rbias, GU, DN, shGU, shDN, sel16F, eyeF, o_x4):
    nc = p.nc
    SLAB = 512
    NSL = SH // SLAB
    import os as _os
    GDBG = _os.environ.get("GDBG", "") == "1"
    if GDBG:
        o_cmb = nc.dram_tensor("o_cmb", [16, SH], F32, kind="ExternalOutput").ap()
        o_acc = nc.dram_tensor("o_acc", [128, NDT * SH], F32, kind="ExternalOutput").ap()

    sx3b = p.sb("sx3b", [128, NDT * SH], BF16)
    sacc = p.sb("sacc", [128, NDT * SH], F32)
    swr = p.sb("swr", [128, NDT * 16], BF16)
    srb = p.sb("srb", [1, 16], BF16)
    ssel = p.sb("ssel", [16, 16 * 128], BF16)
    seyeb = p.sb("seyeb", [128, 128], BF16)
    sgu2 = [p.sb("sguA", [128, NDT * 1024], BF16),
            p.sb("sguB", [128, NDT * 1024], BF16)]
    sdn2 = [p.sb("sdnA", [128, 4 * D], BF16),
            p.sb("sdnB", [128, 4 * D], BF16)]
    scmb = p.sb("scmb", [128, 16], F32)
    sl2f = p.sb("sl2f", [128, 16], F32)
    scmbb = p.sb("scmbb", [128, 16], BF16)
    scr = p.sb("scr", [16, SH], BF16)
    srowm = p.sb("srowm", [128, 6], F32)
    sh_ = p.sb("sh_", [128, 4 * SLAB], BF16)
    shtmp = p.sb("shtmp", [128, 2 * SLAB], BF16)
    shtmp2 = p.sb("shtmp2", [128, 2 * SLAB], BF16)
    sxs = p.sb("sxs", [128, NDT * SLAB], F32)
    swb = p.sb("swb", [128, NDT * SLAB], BF16)
    swb2 = p.sb("swb2", [128, NDT * SLAB], BF16)
    srow = p.sb("srow", [1, 3 * SLAB], F32)
    sones = p.sb("sones", [128, 1], BF16)
    sonesrb = p.sb("sonesrb", [1, 128], BF16)
    sonesrf = p.sb("sonesrf", [1, 128], F32)

    pa = p.psum("pa")
    pb = p.psum("pb")
    pc = p.psum("pc")
    pd = p.psum("pd")
    pe_ = p.psum("pe_")
    pe_2 = p.psum("pe_2")
    pf = p.psum("pf")
    pbt8 = p.ent(nc.psum_tensor(p._part + "pbt8", [128, 1024], BF16))

    block = p.block

    def loads(g):
        p.inc(g.dma_start(out=sx3b.rearrange("p (d n) -> p d n", d=NDT),
                          in_=x3T.rearrange("(d p) n -> p d n", p=128)), "wl", 16)
        p.inc(g.dma_start(out=swr.rearrange("p (d n) -> p d n", d=NDT),
                          in_=wrT.rearrange("(d p) n -> p d n", p=128)), "wl", 16)
        p.inc(g.dma_start(out=srb[:, :], in_=rbias), "wl", 16)
        p.inc(g.dma_start(out=ssel[:, :], in_=sel16F), "wl", 16)
        p.inc(g.dma_start(out=seyeb[:, :], in_=eyeF), "wl", 16)

    block.gpsimd(loads)

    def init_v(v):
        v.memset(sones[:, :], 1.0)
        v.memset(sonesrb[:, :], 1.0)
        p.inc(v.memset(sonesrf[:, :], 1.0), "iv", 1)

    block.vector(init_v)

    for tt in range(SH // 128):
        def r_mm(t, tt=tt):
            if tt == 0:
                p.wait(t, "wl")
                p.wait(t, "iv")
            p.bank_acquire(t, "pa")
            for d in range(NDT):
                t.matmul(pa[:, 0:16], sx3b[:, d * SH + tt * 128:d * SH + tt * 128 + 128],
                         swr[:, d * 16:(d + 1) * 16],
                         start=(d == 0), stop=False)
            p.inc(t.matmul(pa[:, 0:16], sonesrb[:, :], srb[:, :],
                           start=False, stop=True), "rmm", 1)

        block.tensor(r_mm)

        def r_v(v, tt=tt):
            p.wait(v, "rmm")
            p.inc(v.reduce_max(srowm[:, 0:1], pa[:, 0:16], axis=AX.X), "vsy", 1)
            p.wait(v, "vsy")
            p.edge(v, v.tensor_scalar(scmb[:, :], pa[:, 0:16], srowm[:, 0:1], None, ALU.is_ge))
            p.edge(v, v.scalar_tensor_tensor(sl2f[:, :], scmb[:, :], -1e9, pa[:, 0:16],
                                             ALU.mult, ALU.add))
            p.inc(v.reduce_max(srowm[:, 1:2], sl2f[:, :], axis=AX.X), "vsy", 1)
            p.wait(v, "vsy")
            p.inc(v.tensor_sub(srowm[:, 2:3], srowm[:, 0:1], srowm[:, 1:2]), "rv1", 1)

        block.vector(r_v)

        def r_s(s, tt=tt):
            p.wait(s, "rv1")
            p.inc(s.activation(srowm[:, 3:4], srowm[:, 2:3], AF.Sigmoid), "rsg", 1)

        block.scalar(r_s)

        def r_v2(v, tt=tt):
            p.wait(v, "rsg")
            p.edge(v, v.tensor_scalar(shtmp2[:, 0:16], pa[:, 0:16], srowm[:, 1:2], None, ALU.is_ge))
            p.edge(v, v.tensor_sub(shtmp2[:, 0:16], shtmp2[:, 0:16], scmb[:, :]))
            p.edge(v, v.tensor_scalar(srowm[:, 4:5], srowm[:, 3:4], -1.0, 1.0, ALU.mult, ALU.add))
            v.tensor_scalar(scmb[:, :], scmb[:, :], srowm[:, 3:4], None, ALU.mult)
            p.edge(v, v.tensor_scalar(shtmp2[:, 0:16], shtmp2[:, 0:16], srowm[:, 4:5], None, ALU.mult))
            p.edge(v, v.tensor_add(scmb[:, :], scmb[:, :], shtmp2[:, 0:16]))
            p.inc(v.tensor_copy(scmbb[:, :], scmb[:, :]), "rv2", 1)

        block.vector(r_v2)
        p.bank_release("pa", "rv2")

        def r_tr(t, tt=tt):
            p.wait(t, "rv2")
            p.bank_acquire(t, "pbt")
            p.inc(t.transpose(pbt8[0:16, 0:128], scmbb[:, :], seyeb[:, :]), "rtr", 1)

        block.tensor(r_tr)

        def r_ev(v, tt=tt):
            p.wait(v, "rtr")
            p.inc(v.tensor_copy(scr[:, tt * 128:(tt + 1) * 128], pbt8[0:16, 0:128]),
                  "rev", 1)

        block.vector(r_ev)
        p.bank_release("pbt", "rev")

    GEXP = int(_os.environ.get("GEXP", "16"))
    GSH = _os.environ.get("GSH", "1") == "1"
    dnv_ckpt = {}
    for e in range(GEXP):
        sgu = sgu2[e % 2]
        sdn = sdn2[e % 2]
        dnv_ckpt[e] = p.val("dnv")

        def ld_w(g, e=e, sgu=sgu, sdn=sdn):
            if e >= 2:
                p.wait(g, "dnv", dnv_ckpt[e - 1])
            elif e == 0:
                p.wait(g, "rev")
            p.inc(g.dma_start(out=sgu.rearrange("p (d n) -> p d n", d=NDT),
                              in_=GU[e].rearrange("(d p) n -> p d n", p=128)), "wh", 16)
            p.inc(g.dma_start(out=sdn.rearrange("p (c n) -> p c n", c=4),
                              in_=DN[e].rearrange("(c p) n -> p c n", p=128)), "wh", 16)

        block.gpsimd(ld_w)

        for sl in range(NSL):
            def rbc_mm(t, e=e, sl=sl):
                if sl == 0:
                    p.wait(t, "wh")
                p.bank_acquire(t, "pe_")
                p.inc(t.matmul(pe_[:, 0:SLAB], ssel[:, e * 128:(e + 1) * 128],
                               scr[:, sl * SLAB:(sl + 1) * SLAB],
                               start=True, stop=True), "rbc", 1)

            block.tensor(rbc_mm)

            for mp in range(2):
                GB = [("pc", pc), ("pd", pd)]
                UB = [("pe2", pe_2), ("pf", pf)]

                def gu_mm(t, mp=mp, sl=sl):
                    for mi in range(2):
                        m = mp * 2 + mi
                        p.bank_acquire(t, GB[mi][0])
                        for d in range(NDT):
                            t.matmul(GB[mi][1][:, 0:SLAB],
                                     sgu[:, d * 1024 + m * 128:d * 1024 + (m + 1) * 128],
                                     sx3b[:, d * SH + sl * SLAB:d * SH + (sl + 1) * SLAB],
                                     start=(d == 0), stop=(d == NDT - 1))
                        p.bank_acquire(t, UB[mi][0])
                        for d in range(NDT):
                            ins = t.matmul(UB[mi][1][:, 0:SLAB],
                                           sgu[:, d * 1024 + 512 + m * 128:d * 1024 + 512 + (m + 1) * 128],
                                           sx3b[:, d * SH + sl * SLAB:d * SH + (sl + 1) * SLAB],
                                           start=(d == 0), stop=(d == NDT - 1))
                    p.inc(ins, "gmm", 1)

                block.tensor(gu_mm)

                def gu_ep(s, mp=mp):
                    p.wait(s, "gmm")
                    p.wait(s, "gsc", p.val("gsc"))
                    for mi in range(2):
                        s.activation(shtmp[:, mi * SLAB:(mi + 1) * SLAB],
                                     GB[mi][1][:, 0:SLAB], AF.Silu)
                        ins = s.activation(shtmp2[:, mi * SLAB:(mi + 1) * SLAB],
                                           UB[mi][1][:, 0:SLAB], AF.Copy)
                    p.inc(ins, "gep", 1)

                block.scalar(gu_ep)
                for mi in range(2):
                    p.bank_release(GB[mi][0], "gep")
                    p.bank_release(UB[mi][0], "gep")

                def gu_v(v, mp=mp):
                    p.wait(v, "gep")
                    for mi in range(2):
                        v.tensor_mul(shtmp[:, mi * SLAB:(mi + 1) * SLAB],
                                     shtmp[:, mi * SLAB:(mi + 1) * SLAB],
                                     shtmp2[:, mi * SLAB:(mi + 1) * SLAB])
                    p.edge(v, v.engine_nop())
                    for mi in range(2):
                        m = mp * 2 + mi
                        ins = v.tensor_mul(sh_[:, m * SLAB:(m + 1) * SLAB],
                                           shtmp[:, mi * SLAB:(mi + 1) * SLAB],
                                           pe_[:, 0:SLAB])
                    p.inc(ins, "gsc", 1)

                block.vector(gu_v)

            p.bank_release("pe_", "gsc")

            for i in range(NDT):
                bank = ["pa", "pb"][i % 2]
                pgb = {"pa": pa, "pb": pb}[bank]

                def dn_mm(t, i=i, bank=bank, pgb=pgb):
                    if i == 0:
                        p.wait(t, "gsc")
                    p.bank_acquire(t, bank)
                    for c in range(4):
                        ins = t.matmul(pgb[:, 0:SLAB],
                                       sdn[:, c * D + i * 128:c * D + (i + 1) * 128],
                                       sh_[:, c * SLAB:(c + 1) * SLAB],
                                       start=(c == 0), stop=(c == 3))
                    p.inc(ins, "dmm", 1)

                block.tensor(dn_mm)

                def dn_v(v, e=e, i=i, sl=sl, bank=bank, pgb=pgb):
                    p.wait(v, "dmm")
                    seg = slice(sl * NDT * SLAB + i * SLAB, sl * NDT * SLAB + (i + 1) * SLAB)
                    if e == 0:
                        ins = v.tensor_copy(sacc[:, seg], pgb[:, 0:SLAB])
                    else:
                        ins = v.tensor_add(sacc[:, seg], sacc[:, seg], pgb[:, 0:SLAB])
                    p.inc(ins, "dnv", 1)

                block.vector(dn_v)
                p.bank_release(bank, "dnv")

    GSHN = int(_os.environ.get("GSHN", "2")) if GSH else 0
    for half in range(GSHN):
        sgu = sgu2[half]
        sdn = sdn2[half]

        def ld_sh(g, half=half, sgu=sgu, sdn=sdn):
            if half == 0:
                p.wait(g, "dnv", dnv_ckpt[GEXP - 1])
            p.wait(g, "dmm" if half == 1 else "shd", p.val("dmm" if half == 1 else "shd"))
            p.wait(g, "shd")
            p.inc(g.dma_start(
                out=sgu.rearrange("p (d n) -> p d n", d=NDT)[:, :, 0:512],
                in_=shGU.rearrange("(d p) n -> p d n", p=128)[:, :, half * 512:(half + 1) * 512]),
                "wsh", 16)
            p.inc(g.dma_start(
                out=sgu.rearrange("p (d n) -> p d n", d=NDT)[:, :, 512:1024],
                in_=shGU.rearrange("(d p) n -> p d n", p=128)[:, :, 1024 + half * 512:1024 + (half + 1) * 512]),
                "wsh", 16)
            p.inc(g.dma_start(
                out=sdn.rearrange("p (c n) -> p c n", c=4),
                in_=shDN.rearrange("(c p) n -> p c n", p=128)[:, half * 4:(half + 1) * 4, :]),
                "wsh", 16)

        block.gpsimd(ld_sh)

        for sl in range(NSL):
            for mp in range(2):
                GB = [("pc", pc), ("pd", pd)]
                UB = [("pe2", pe_2), ("pf", pf)]

                def shg_mm(t, mp=mp, sl=sl, half=half):
                    if mp == 0 and sl == 0:
                        p.wait(t, "wsh")
                    for mi in range(2):
                        m = mp * 2 + mi
                        p.bank_acquire(t, GB[mi][0])
                        for d in range(NDT):
                            t.matmul(GB[mi][1][:, 0:SLAB],
                                     sgu[:, d * 1024 + m * 128:d * 1024 + (m + 1) * 128],
                                     sx3b[:, d * SH + sl * SLAB:d * SH + (sl + 1) * SLAB],
                                     start=(d == 0), stop=(d == NDT - 1))
                        p.bank_acquire(t, UB[mi][0])
                        for d in range(NDT):
                            ins = t.matmul(UB[mi][1][:, 0:SLAB],
                                           sgu[:, d * 1024 + 512 + m * 128:d * 1024 + 512 + (m + 1) * 128],
                                           sx3b[:, d * SH + sl * SLAB:d * SH + (sl + 1) * SLAB],
                                           start=(d == 0), stop=(d == NDT - 1))
                    p.inc(ins, "shm", 1)

                block.tensor(shg_mm)

                def shg_ep(s, mp=mp):
                    p.wait(s, "shm")
                    p.wait(s, "shsc", p.val("shsc"))
                    for mi in range(2):
                        s.activation(shtmp[:, mi * SLAB:(mi + 1) * SLAB],
                                     GB[mi][1][:, 0:SLAB], AF.Silu)
                        ins = s.activation(shtmp2[:, mi * SLAB:(mi + 1) * SLAB],
                                           UB[mi][1][:, 0:SLAB], AF.Copy)
                    p.inc(ins, "shep", 1)

                block.scalar(shg_ep)
                for mi in range(2):
                    p.bank_release(GB[mi][0], "shep")
                    p.bank_release(UB[mi][0], "shep")

                def shg_v(v, mp=mp):
                    p.wait(v, "shep")
                    for mi in range(2):
                        m = mp * 2 + mi
                        ins = v.tensor_mul(sh_[:, m * SLAB:(m + 1) * SLAB],
                                           shtmp[:, mi * SLAB:(mi + 1) * SLAB],
                                           shtmp2[:, mi * SLAB:(mi + 1) * SLAB])
                    p.inc(ins, "shsc", 1)

                block.vector(shg_v)

            for i in range(NDT):
                bank = ["pa", "pb"][i % 2]
                pgb = {"pa": pa, "pb": pb}[bank]

                def shd_mm(t, i=i, bank=bank, pgb=pgb):
                    if i == 0:
                        p.wait(t, "shsc")
                    p.bank_acquire(t, bank)
                    for c in range(4):
                        ins = t.matmul(pgb[:, 0:SLAB],
                                       sdn[:, c * D + i * 128:c * D + (i + 1) * 128],
                                       sh_[:, c * SLAB:(c + 1) * SLAB],
                                       start=(c == 0), stop=(c == 3))
                    p.inc(ins, "shd", 1)

                block.tensor(shd_mm)

                def shd_v(v, i=i, sl=sl, bank=bank, pgb=pgb):
                    p.wait(v, "shd")
                    seg = slice(sl * NDT * SLAB + i * SLAB, sl * NDT * SLAB + (i + 1) * SLAB)
                    p.inc(v.tensor_add(sacc[:, seg], sacc[:, seg], pgb[:, 0:SLAB]), "dnv", 1)

                block.vector(shd_v)
                p.bank_release(bank, "dnv")

    if GDBG:
        def dbg_g(g):
            p.wait(g, "dnv")
            p.wait(g, "rev")
            p.inc(g.dma_start(out=o_cmb, in_=scr[:, :]), "gdbg", 16)
            p.inc(g.dma_start(out=o_acc, in_=sacc[:, :]), "gdbg", 16)
        block.gpsimd(dbg_g)

    for sl in range(NSL):
        col0 = sl * SLAB

        def ld_xs(g, sl=sl, col0=col0):
            p.wait(g, "lnb", p.val("lnb"))
            p.wait(g, "x4st", p.val("x4st"))
            p.inc(g.dma_start(out=sxs.rearrange("p (d n) -> p d n", d=NDT),
                              in_=x3T[:, col0:col0 + SLAB].rearrange("(d p) n -> p d n", p=128)),
                  "xsl", 16)

        block.sync(ld_xs)

        def vsum(v, sl=sl):
            p.wait(v, "dnv")
            p.wait(v, "xsl")
            for i in range(NDT):
                seg = slice(sl * NDT * SLAB + i * SLAB, sl * NDT * SLAB + (i + 1) * SLAB)
                ins = v.tensor_add(sxs[:, i * SLAB:(i + 1) * SLAB],
                                   sxs[:, i * SLAB:(i + 1) * SLAB], sacc[:, seg])
            p.inc(ins, "vsum", 1)

        block.vector(vsum)

        def lnw(s):
            p.wait(s, "vsum")
            s.activation(swb[:, :], sxs[:, :], AF.Copy)
            p.inc(s.activation(swb2[:, :], sxs[:, :], AF.Square), "lnc", 1)

        block.scalar(lnw)

        def lnstat(t):
            p.wait(t, "lnc")
            p.bank_acquire(t, "pa")
            for i in range(NDT):
                t.matmul(pa[0:1, 0:SLAB], sones[:, :], swb[:, i * SLAB:(i + 1) * SLAB],
                         start=(i == 0), stop=(i == NDT - 1))
            p.bank_acquire(t, "pb")
            for i in range(NDT):
                ins = t.matmul(pb[0:1, 0:SLAB], sones[:, :], swb2[:, i * SLAB:(i + 1) * SLAB],
                               start=(i == 0), stop=(i == NDT - 1))
            p.inc(ins, "lns", 1)

        block.tensor(lnstat)

        def lnrow(v):
            p.wait(v, "lns")
            p.edge(v, v.tensor_scalar(srow[:, 0:SLAB], pa[0:1, 0:SLAB], 1.0 / D, None, ALU.mult))
            p.edge(v, v.tensor_scalar(srow[:, SLAB:2 * SLAB], pb[0:1, 0:SLAB], 1.0 / D, None, ALU.mult))
            p.edge(v, v.tensor_mul(srow[:, 2 * SLAB:3 * SLAB], srow[:, 0:SLAB], srow[:, 0:SLAB]))
            p.edge(v, v.tensor_sub(srow[:, SLAB:2 * SLAB], srow[:, SLAB:2 * SLAB],
                                   srow[:, 2 * SLAB:3 * SLAB]))
            p.inc(v.tensor_scalar(srow[:, SLAB:2 * SLAB], srow[:, SLAB:2 * SLAB], 1.0,
                                  EPS_LN, ALU.mult, ALU.add), "lnr", 1)

        block.vector(lnrow)
        p.bank_release("pa", "lnr")
        p.bank_release("pb", "lnr")

        def lnsq(s):
            p.wait(s, "lnr")
            p.edge(s, s.activation(srow[:, SLAB:2 * SLAB], srow[:, SLAB:2 * SLAB],
                                   AF.Ln), "ssy")
            p.inc(s.activation(srow[:, SLAB:2 * SLAB], srow[:, SLAB:2 * SLAB],
                               AF.Exp, scale=-0.5), "lnq", 1)

        block.scalar(lnsq)

        def lnrec(v):
            p.wait(v, "lnq")
            p.inc(v.tensor_mul(srow[:, 2 * SLAB:3 * SLAB], srow[:, 0:SLAB],
                               srow[:, SLAB:2 * SLAB]), "lnm", 1)

        block.vector(lnrec)

        def lnbc(t):
            p.wait(t, "lnm")
            p.bank_acquire(t, "pc")
            t.matmul(pc[:, 0:SLAB], sonesrf[:, :], srow[:, SLAB:2 * SLAB],
                     start=True, stop=True)
            p.bank_acquire(t, "pd")
            p.inc(t.matmul(pd[:, 0:SLAB], sonesrf[:, :], srow[:, 2 * SLAB:3 * SLAB],
                           start=True, stop=True), "lnw2", 1)

        block.tensor(lnbc)

        def x4_v(v):
            p.wait(v, "lnw2")
            for i in range(NDT):
                v.tensor_mul(sxs[:, i * SLAB:(i + 1) * SLAB],
                             sxs[:, i * SLAB:(i + 1) * SLAB], pc[:, 0:SLAB])
            p.edge(v, v.engine_nop())
            for i in range(NDT):
                ins = v.tensor_sub(sxs[:, i * SLAB:(i + 1) * SLAB],
                                   sxs[:, i * SLAB:(i + 1) * SLAB], pd[:, 0:SLAB])
            p.inc(ins, "lnb", 1)

        block.vector(x4_v)
        p.bank_release("pc", "lnb")
        p.bank_release("pd", "lnb")

        def x4_store(g, col0=col0):
            p.wait(g, "lnb")
            p.inc(g.dma_start(
                out=o_x4[:, col0:col0 + SLAB].rearrange("(d p) n -> p d n", p=128),
                in_=sxs.rearrange("p (d n) -> p d n", d=NDT)), "x4st", 16)

        block.gpsimd(x4_store)

    def done(g):
        p.wait(g, "x4st")

    block.gpsimd(done)


def build_launch_G():
    p = Prog()
    nc = p.nc
    aps = dict(
        x3T=nc.dram_tensor("x3T", [D, SH], F32, kind="ExternalInput").ap(),
        wrT=nc.dram_tensor("wrT", [D, 16], F32, kind="ExternalInput").ap(),
        rbias=nc.dram_tensor("rbias", [1, 16], F32, kind="ExternalInput").ap(),
        GU=nc.dram_tensor("GU", [16, D, 1024], F32, kind="ExternalInput").ap(),
        DN=nc.dram_tensor("DN", [16, 512, D], F32, kind="ExternalInput").ap(),
        shGU=nc.dram_tensor("shGU", [D, 2048], F32, kind="ExternalInput").ap(),
        shDN=nc.dram_tensor("shDN", [1024, D], F32, kind="ExternalInput").ap(),
        sel16F=nc.dram_tensor("sel16F", [16, 16 * 128], F32, kind="ExternalInput").ap(),
        eyeF=nc.dram_tensor("eyeF", [128, 128], F32, kind="ExternalInput").ap(),
        o_x4=nc.dram_tensor("o_x4", [D, SH], F32, kind="ExternalOutput").ap(),
    )
    p.begin()
    emit_G(p, **aps)
    p.close()
    return p.nc



def build_fused():
    import os as _os
    FPARTS = int(_os.environ.get("FPARTS", "99"))
    p = Prog()
    nc = p.nc

    def ein(name, shape):
        return nc.dram_tensor(name, shape, F32, kind="ExternalInput").ap()

    def dint(name, shape):
        return nc.dram_tensor(name, shape, F32).ap()

    ex = dict(
        xT=ein("xT", [D, SH + 3]),
        xTC=ein("xTC", [D, SH]),
        w_gate=ein("w_gate", [D, D]),
        w_stin=ein("w_stin", [D, KSS * D]),
        w_stout=ein("w_stout", [KSS * D, D]),
        w_z36=ein("w_z36", [D, 36]),
        w_z2=ein("w_z2", [D, 2]),
        b_z2=ein("b_z2", [2, 1]),
        w_qkvg=ein("w_qkvg", [D, 97]),
        wgb_col=ein("wgb_col", [128, 1]),
        conv_w=ein("conv_w", [128, NDT * 4]),
        gate_b=ein("gate_b", [128, NDT]),
        logit_bc=ein("logit_bc", [128, KSS]),
        codesF=ein("codesF", [9, 512]),
        suF=ein("suF", [128, 128]),
        suiF=ein("suiF", [128, 128]),
        thrF=ein("thrF", [128, 128]),
        altF=ein("altF", [128, 128]),
        bndF=ein("bndF", [128, 132]),
        eyeF=ein("eyeF", [128, 128]),
        owns_bc=ein("owns_bc", [32, 64]),
        w_mo=ein("w_mo", [32, D]),
        w_up=ein("w_up", [D, 4 * D]),
        w_dn=ein("w_dn", [4 * D, D]),
        b_up=ein("b_up", [128, 32]),
        b_dn=ein("b_dn", [1, D]),
        logit_row=ein("logit_row", [1, KSS]),
        ecp_row=ein("ecp_row", [1, 4]),
        mask44=ein("mask44", [4, 4]),
        iotap1=ein("iotap1", [1, SH]),
        W1=ein("W1", [8, D, 4 * D]),
        b1=ein("b1", [128, 8 * 32]),
        W2=ein("W2", [8, 4 * D, D]),
        b2=ein("b2", [8, D]),
        Wr=ein("Wr", [D, 8]),
        br=ein("br", [1, 8]),
        sel8F=ein("sel8F", [8, 8 * 128]),
        wkv=ein("wkv", [D, 128]),
        wqT=ein("wqT", [D, 64]),
        woT=ein("woT", [64, D]),
        wrT=ein("wrT", [D, 16]),
        rbias=ein("rbias", [1, 16]),
        GU=ein("GU", [16, D, 1024]),
        DN=ein("DN", [16, 512, D]),
        shGU=ein("shGU", [D, 2048]),
        shDN=ein("shDN", [1024, D]),
        sel16F=ein("sel16F", [16, 16 * 128]),
    )
    o_x4 = nc.dram_tensor("o_x4", [D, SH], F32, kind="ExternalOutput").ap()

    h_signs = dint("h_signs", [36, SH])
    h_qkv = dint("h_qkv", [SH, 97])
    h_zg = dint("h_zg", [128, 16])
    h_fin = dint("h_fin", [32, 128])
    h_long = dint("h_long", [D, SH])
    h_glr = dint("h_glr", [1, SH])
    h_gmr = dint("h_gmr", [1, SH])
    g_signs = dint("g_signs", [144, SH])
    g_qkv = dint("g_qkv", [T, 97])
    g_zg = dint("g_zg", [512, 16])
    g_fin = dint("g_fin", [128, 128])
    h_readT = dint("h_readT", [32, T])
    h_x1 = dint("h_x1", [D, SH])
    h_x2 = dint("h_x2", [D, SH])
    h_x3 = dint("h_x3", [D, SH])

    p.begin()

    p.push_scope("A")
    emit_A(p, xT=ex["xT"], w_gate=ex["w_gate"], w_stin=ex["w_stin"],
           w_stout=ex["w_stout"], w_z36=ex["w_z36"], w_z2=ex["w_z2"],
           b_z2=ex["b_z2"], w_qkvg=ex["w_qkvg"], wgb_col=ex["wgb_col"],
           conv_w=ex["conv_w"], gate_b=ex["gate_b"], logit_bc=ex["logit_bc"],
           o_signs=h_signs, o_glr=h_glr, o_gmr=h_gmr, o_qkvg=h_qkv,
           o_zgpc=h_zg, o_long=h_long, o_finT=h_fin)
    p.pop_scope()

    grp = [[0, 1, 2, 3], [4, 5, 6, 7]]

    def ag(g):
        for i, (src_, dst_) in enumerate([(h_signs, g_signs), (h_qkv, g_qkv),
                                          (h_zg, g_zg), (h_fin, g_fin)]):
            p.inc(g.collective_compute("AllGather", ALU.bypass,
                                       replica_groups=grp,
                                       ins=[src_.opt()], outs=[dst_.opt()]),
                  f"ccg{i}", 1)
        for i in range(4):
            p.wait(g, f"ccg{i}", 1)

    p.block.gpsimd(ag)
    p.barrier()

    if FPARTS >= 2:
        p.push_scope("B")
        emit_B(p, sgath=g_signs, qgath=g_qkv, zgath=g_zg, codesF=ex["codesF"],
               suF=ex["suF"], suiF=ex["suiF"], thrF=ex["thrF"], altF=ex["altF"],
               bndF=ex["bndF"], eyeF=ex["eyeF"], o_readT=h_readT)
        p.pop_scope()
        p.barrier()

    if FPARTS >= 3:
        p.push_scope("C")
        emit_C(p, xT=ex["xTC"], longT=h_long, finAG=g_fin, gl_row=h_glr,
               gm_row=h_gmr, readTF=h_readT, owns_bc=ex["owns_bc"],
               w_stout=ex["w_stout"], w_mo=ex["w_mo"], w_up=ex["w_up"],
               w_dn=ex["w_dn"], b_up=ex["b_up"], b_dn=ex["b_dn"],
               logit_row=ex["logit_row"], ecp_row=ex["ecp_row"],
               mask44=ex["mask44"], iotap1=ex["iotap1"], o_x1=h_x1)
        p.pop_scope()
        p.barrier()

    if FPARTS >= 4:
        p.push_scope("D")
        emit_D(p, x1T=h_x1, W1=ex["W1"], b1=ex["b1"], W2=ex["W2"], b2=ex["b2"],
               Wr=ex["Wr"], br=ex["br"], sel8F=ex["sel8F"], eyeF=ex["eyeF"],
               o_x2=h_x2)
        p.pop_scope()
        p.barrier()

    if FPARTS >= 5:
        p.push_scope("E")
        emit_E(p, x2T=h_x2, wkv=ex["wkv"], wqT=ex["wqT"], woT=ex["woT"], o_x3=h_x3)
        p.pop_scope()
        p.barrier()

    if FPARTS >= 6:
        p.push_scope("G")
        emit_G(p, x3T=h_x3, wrT=ex["wrT"], rbias=ex["rbias"], GU=ex["GU"],
               DN=ex["DN"], shGU=ex["shGU"], shDN=ex["shDN"], sel16F=ex["sel16F"],
               eyeF=ex["eyeF"], o_x4=o_x4)
        p.pop_scope()

    p.close()
    return p.nc



def _host_consts_B():
    C = 128
    f = np.float32
    codes = np.where((np.arange(512)[None, :] >> np.arange(9)[:, None]) & 1 > 0,
                     1.0, -1.0).astype(f)
    su = np.triu(np.ones((C, C), f), 1)
    sui = np.triu(np.ones((C, C), f), 0)
    thr = np.broadcast_to(np.arange(1, 129, dtype=f)[None, :], (C, C)).copy()
    alt = np.broadcast_to(np.where(np.arange(128) % 2 == 0, 1.0, -1.0).astype(f)[None, :],
                          (C, C)).copy()
    bnd = np.broadcast_to((np.arange(132, dtype=f) * QLVL)[None, :], (C, 132)).copy()
    eye = np.eye(C, dtype=f)
    return dict(codesF=codes, suF=su, suiF=sui, thrF=thr, altF=alt, bndF=bnd, eyeF=eye)


def kernel(**inputs):
    f = np.float32
    inputs = {k: np.asarray(v) for k, v in inputs.items()}
    nc = build_fused()

    sel8 = np.zeros((8, 8 * 128), f)
    for e in range(8):
        sel8[e, e * 128:(e + 1) * 128] = 1.0
    sel16 = np.zeros((16, 16 * 128), f)
    for e in range(16):
        sel16[e, e * 128:(e + 1) * 128] = 1.0
    GUw = np.ascontiguousarray(
        np.concatenate([inputs["L1_ffn_gate_proj"], inputs["L1_ffn_up_proj"]], 2), dtype=f)
    shGU = np.ascontiguousarray(
        np.concatenate([inputs["L1_ffn_sh_gate"].T, inputs["L1_ffn_sh_up"].T], 1), dtype=f)
    wkv = np.ascontiguousarray(
        np.concatenate([inputs["L1_attn_wk"].T, inputs["L1_attn_wv"].T], 1), dtype=f)
    consts = _host_consts_B()

    shared = {
        "xTC": None,
        "w_mo": np.ascontiguousarray(inputs["L0_attn_mem_out_w"].T, dtype=f),
        "w_up": np.ascontiguousarray(inputs["L0_attn_mlp_up_w"].T, dtype=f),
        "w_dn": np.ascontiguousarray(inputs["L0_attn_mlp_down_w"].T, dtype=f),
        "b_up": np.ascontiguousarray(inputs["L0_attn_mlp_up_b"].reshape(32, 128).T, dtype=f),
        "b_dn": inputs["L0_attn_mlp_down_b"].reshape(1, D).astype(f),
        "logit_row": inputs["L0_attn_state_decay_logit"].reshape(1, 4).astype(f),
        "iotap1": (np.arange(SH, dtype=f) + 1.0).reshape(1, SH),
        "W1": np.ascontiguousarray(inputs["L0_ffn_W1"], dtype=f),
        "b1": np.ascontiguousarray(
            inputs["L0_ffn_b1"].reshape(8, 32, 128).transpose(2, 0, 1).reshape(128, 256), dtype=f),
        "W2": np.ascontiguousarray(inputs["L0_ffn_W2"], dtype=f),
        "b2": np.ascontiguousarray(inputs["L0_ffn_b2"], dtype=f),
        "Wr": np.ascontiguousarray(inputs["L0_ffn_Wr"], dtype=f),
        "br": inputs["L0_ffn_br"].reshape(1, 8).astype(f),
        "sel8F": sel8,
        "wkv": wkv,
        "wqT": np.ascontiguousarray(inputs["L1_attn_wq"].T, dtype=f),
        "woT": np.ascontiguousarray(inputs["L1_attn_wo"].T, dtype=f),
        "wrT": np.ascontiguousarray(inputs["L1_ffn_router_w"].T, dtype=f),
        "rbias": inputs["L1_ffn_router_bias"].reshape(1, 16).astype(f),
        "GU": GUw,
        "DN": np.ascontiguousarray(inputs["L1_ffn_down_proj"], dtype=f),
        "shGU": shGU,
        "shDN": np.ascontiguousarray(inputs["L1_ffn_sh_down"].T, dtype=f),
        "sel16F": sel16,
    }
    shared.update(consts)

    in_maps = []
    for c in range(NCORE):
        b, pos = c // 4, c % 4
        t0 = pos * SH
        x = inputs["L0_attn_x"][b].astype(f)
        m = dict(host_inputs_A(inputs, c))
        m.update(shared)
        m["xTC"] = np.ascontiguousarray(x[t0:t0 + SH].T)
        ecp = np.zeros((1, 4), f)
        mask = np.zeros((4, 4), f)
        for cp in range(4):
            if cp < pos:
                ecp[0, cp] = SH * (pos - 1 - cp)
                mask[cp, :] = 1.0
        m["ecp_row"] = ecp
        m["mask44"] = mask
        owns = np.zeros((32, 64), f)
        owns[:, pos * 16:(pos + 1) * 16] = 1.0
        m["owns_bc"] = owns
        in_maps.append(m)

    res = run_bass_kernel_spmd(nc, in_maps, core_ids=list(range(NCORE)))
    out = np.empty((B, T, D), f)
    for c in range(NCORE):
        b, pos = c // 4, c % 4
        out[b, pos * SH:(pos + 1) * SH, :] = res.results[c]["o_x4"].T
    return out


# version 10
# speedup vs baseline: 1.0382x; 1.0382x over previous
import numpy as np
import concourse.bass as bass
import concourse.mybir as mybir
from concourse.bass_utils import run_bass_kernel_spmd

F32 = mybir.dt.float32
BF16 = mybir.dt.bfloat16
AF = mybir.ActivationFunctionType
ALU = mybir.AluOpType
AX = mybir.AxisListType

NCORE = 8
B = 2
T = 8192
D = 1024
SH = 2048
KSS = 4
NDT = D // 128
EPS_RMS = 1e-6
EPS_LN = 1e-5

_DEBUG = False


class Prog:

    def __init__(self):
        self.nc = bass.Bass("TRN2", target_bir_lowering=False)
        self._ctx = []
        self._sems = {}
        self._bank_req = {}
        self._scopes = []
        self._part = ""
        self._nbar = 0

    def ent(self, cm):
        if self._scopes:
            self._scopes[-1].append(cm)
        else:
            self._ctx.append(cm)
        return cm.__enter__()

    def sb(self, name, shape, dt=F32):
        return self.ent(self.nc.sbuf_tensor(self._part + name, list(shape), dt))

    def psum(self, name, parts=128):
        return self.ent(self.nc.psum_tensor(self._part + name, [parts, 512], F32))

    def push_scope(self, part):
        self._part = part + "_"
        self._scopes.append([])

    def pop_scope(self):
        for cm in reversed(self._scopes.pop()):
            cm.__exit__(None, None, None)
        self._part = ""

    def barrier(self):
        self._nbar += 1
        pb = self.sem("pbar")
        target = 5 * self._nbar
        blk = self.block
        for section in (blk.tensor, blk.vector, blk.scalar, blk.gpsimd, blk.sync):
            def b(e):
                self.inc(e.nop(), "pbar", 1)
                e.wait_ge(pb[0], target)
            section(b)
        pool = getattr(self, "_sem_pool", None)
        if pool is None:
            pool = self._sem_pool = []
        for nm in [n for n in self._sems if n != "pbar"]:
            h, cnt = self._sems.pop(nm)
            pool.append((h, cnt))
        self._bank_req = {}

    def begin(self):
        self.block = self.ent(self.nc.Block())

    def close(self):
        for c in reversed(self._ctx):
            c.__exit__(None, None, None)
        return self.nc

    def sem(self, name):
        if name not in self._sems:
            if getattr(self, "_sem_pool", None):
                h, base = self._sem_pool.pop()
            else:
                cm = self.nc.semaphore(name)
                self._ctx.append(cm)
                h = cm.__enter__()
                base = 0
            self._sems[name] = [h, base]
        return self._sems[name]

    def inc(self, instr, name, k=1):
        s = self.sem(name)
        instr.then_inc(s[0], k)
        s[1] += k
        return instr

    def val(self, name):
        return self.sem(name)[1]

    def wait(self, eng, name, value=None):
        s = self.sem(name)
        v = s[1] if value is None else value
        if v > 0:
            eng.wait_ge(s[0], v)

    def bank_acquire(self, eng, bank):
        req = self._bank_req.get(bank)
        if req is not None:
            name, cnt = req
            s = self.sem(name)
            eng.wait_ge(s[0], cnt)
            self._bank_req[bank] = None

    def edge(self, eng, instr, name="vsy"):
        self.inc(instr, name, 1)
        self.wait(eng, name)

    def bank_release(self, bank, name):
        self._bank_req[bank] = (name, self.val(name))



def emit_A(p, xT, w_gate, w_stin, w_stout, w_z36, w_z2, b_z2, w_qkvg, wgb_col,
           conv_w, gate_b, logit_bc, o_signs, o_glr, o_gmr, o_qkvg, o_zgpc,
           o_long, o_finT):
    nc = p.nc
    SLAB = 256
    NSL = SH // SLAB
    HCO = SLAB + 3

    sgw = p.sb("sgw", [128, NDT * D], BF16)
    sstin = p.sb("sstin", [128, NDT * KSS * D], BF16)
    sstout = p.sb("sstout", [128, 32 * D], BF16)
    sz36 = p.sb("sz36", [128, NDT * 36], BF16)
    sz2 = p.sb("sz2", [128, NDT * 2], BF16)
    sqkvg = p.sb("sqkvg", [128, NDT * 97], BF16)
    sconv = p.sb("sconv", [128, NDT * 4], F32)
    sgateb = p.sb("sgateb", [128, NDT], F32)
    sbz2 = p.sb("sbz2", [2, 1], F32)
    swgb = p.sb("swgb", [128, 1], F32)
    slogit = p.sb("slogit", [128, KSS], F32)
    sdecay = p.sb("sdecay", [128, KSS], F32)

    sx = p.sb("sx", [128, NDT * HCO], F32)
    sx2 = p.sb("sx2", [128, NDT * HCO], BF16)
    sxn = p.sb("sxn", [128, NDT * HCO], BF16)
    srstd = p.sb("srstd", [1, HCO], F32)
    sh_ = p.sb("sh_", [128, NDT * SLAB], BF16)
    ssig = p.sb("ssig", [128, SLAB], BF16)
    ssil = p.sb("ssil", [128, SLAB], BF16)
    su = p.sb("su", [128, NDT * SLAB], BF16)
    ssgn = p.sb("ssgn", [36, SLAB], F32)
    sglgm = p.sb("sglgm", [2, SLAB], F32)
    sqk = p.sb("sqk", [128, 2 * 97], F32)
    sinp = p.sb("sinp", [128, 2 * SLAB], F32)
    ssc = p.sb("ssc", [128, 2 * SLAB], F32)
    ss = p.sb("ss", [128, 32 * SLAB], BF16)
    scar = p.sb("scar", [128, 32], F32)
    szgpc = p.sb("szgpc", [128, 2 * NSL], F32)
    sfint = p.sb("sfint", [32, 128], F32)
    seyef = p.sb("seyef", [128, 128], F32)
    slong = p.sb("slong", [128, NDT * SLAB], F32)
    sones = p.sb("sones", [128, 1], BF16)
    sonesf = p.sb("sonesf", [1, 128], F32)

    pa = p.psum("pa")
    pb = p.psum("pb")
    pc = p.psum("pc")
    pd = p.psum("pd")
    pl = [p.psum(f"pl{i}") for i in range(4)]

    def plong(i):
        return pl[i // 2][:, (i % 2) * 256:(i % 2) * 256 + 256]

    block = p.block

    def loads(g):
        p.inc(g.dma_start(out=sgw.rearrange("p (d n) -> p d n", d=NDT),
                          in_=w_gate.rearrange("(d p) n -> p d n", p=128)), "wload", 16)
        p.inc(g.dma_start(out=sstin.rearrange("p (d n) -> p d n", d=NDT),
                          in_=w_stin.rearrange("(d p) n -> p d n", p=128)), "wload", 16)
        p.inc(g.dma_start(out=sstout.rearrange("p (c n) -> p c n", c=32),
                          in_=w_stout.rearrange("(c p) n -> p c n", p=128)), "wload", 16)
        p.inc(g.dma_start(out=sz36.rearrange("p (d n) -> p d n", d=NDT),
                          in_=w_z36.rearrange("(d p) n -> p d n", p=128)), "wload", 16)
        p.inc(g.dma_start(out=sz2.rearrange("p (d n) -> p d n", d=NDT),
                          in_=w_z2.rearrange("(d p) n -> p d n", p=128)), "wload", 16)
        p.inc(g.dma_start(out=sqkvg.rearrange("p (d n) -> p d n", d=NDT),
                          in_=w_qkvg.rearrange("(d p) n -> p d n", p=128)), "wload", 16)
        p.inc(g.dma_start(out=sconv[:, :], in_=conv_w), "wload", 16)
        p.inc(g.dma_start(out=sgateb[:, :], in_=gate_b), "wload", 16)
        p.inc(g.dma_start(out=sbz2[:, :], in_=b_z2), "wload", 16)
        p.inc(g.dma_start(out=swgb[:, :], in_=wgb_col), "wload", 16)
        p.inc(g.dma_start(out=slogit[:, :], in_=logit_bc), "wload", 16)

    block.gpsimd(loads)

    def init_g(g):
        p.inc(g.iota(seyef[:, :], pattern=[[1, 128]], base=0, channel_multiplier=-1,
                     allow_small_or_imprecise_dtypes=True), "iog", 1)

    block.gpsimd(init_g)

    def init_v(v):
        v.memset(sones[:, :], 1.0)
        v.memset(sonesf[:, :], 1.0)
        p.wait(v, "iog")
        v.tensor_single_scalar(seyef[:, :], seyef[:, :], 0.0, ALU.is_equal)
        p.inc(v.memset(scar[:, :], 0.0), "init", 1)

    block.vector(init_v)

    def init_s(s):
        p.wait(s, "wload")
        p.inc(s.activation(sdecay[:, :], slogit[:, :], AF.Sigmoid), "init", 1)

    block.scalar(init_s)

    for sl in range(NSL):
        col0 = sl * SLAB

        def ld_x(g, sl=sl, col0=col0):
            if sl > 0:
                p.wait(g, "xnrd")
            p.inc(g.dma_start(
                out=sx.rearrange("p (d n) -> p d n", d=NDT),
                in_=xT[:, col0:col0 + HCO].rearrange("(d p) n -> p d n", p=128)), "xload", 16)

        block.sync(ld_x)

        def sq(s, sl=sl):
            p.wait(s, "xload")
            p.inc(s.activation(sx2[:, :], sx[:, :], AF.Square), "sq", 1)

        block.scalar(sq)

        def statmm(t, sl=sl):
            p.wait(t, "sq")
            p.wait(t, "init", 2)
            p.bank_acquire(t, "pa")
            for i in range(NDT):
                ins = t.matmul(pa[0:1, 0:HCO], sones[:, :],
                               sx2[:, i * HCO:(i + 1) * HCO],
                               start=(i == 0), stop=(i == NDT - 1))
            p.inc(ins, "stat", 1)

        block.tensor(statmm)

        def rstd_row(v):
            p.wait(v, "stat")
            p.inc(v.tensor_scalar(srstd[:, :], pa[0:1, 0:HCO], 1.0 / D, EPS_RMS,
                                  ALU.mult, ALU.add), "statrd", 1)
        block.vector(rstd_row)
        p.bank_release("pa", "statrd")

        def rstd_sqrt(s):
            p.wait(s, "statrd")
            p.edge(s, s.activation(srstd[:, :], srstd[:, :], AF.Ln), "ssy")
            p.inc(s.activation(srstd[:, :], srstd[:, :], AF.Exp, scale=-0.5),
                  "rrec", 1)

        block.scalar(rstd_sqrt)

        def bcmm(t):
            p.wait(t, "rrec")
            p.bank_acquire(t, "pb")
            p.inc(t.matmul(pb[:, 0:HCO], sonesf[:, :], srstd[:, :],
                           start=True, stop=True), "bc", 1)

        block.tensor(bcmm)

        def mk_xn(v):
            p.wait(v, "bc")
            for i in range(NDT):
                ins = v.tensor_mul(sxn[:, i * HCO:(i + 1) * HCO],
                                   sx[:, i * HCO:(i + 1) * HCO],
                                   pb[:, 0:HCO])
            p.inc(ins, "xnrd", 1)

        block.vector(mk_xn)
        p.bank_release("pb", "xnrd")

        def mk_conv(v):
            for i in range(NDT):
                hseg = sh_[:, i * SLAB:(i + 1) * SLAB]
                xnseg = sxn[:, i * HCO:(i + 1) * HCO]
                v.tensor_scalar(hseg, xnseg[:, 0:SLAB], sconv[:, 4 * i:4 * i + 1],
                                None, ALU.mult)
                for j in range(1, 4):
                    ins = v.scalar_tensor_tensor(hseg, xnseg[:, j:SLAB + j],
                                                 sconv[:, 4 * i + j:4 * i + j + 1],
                                                 hseg, ALU.mult, ALU.add)
            p.inc(ins, "conv", 1)

        block.vector(mk_conv)

        for i in range(NDT):
            bank = ["pc", "pd"][i % 2]
            pgb = {"pc": pc, "pd": pd}[bank]

            def gate_mm(t, i=i, bank=bank, pgb=pgb):
                if i == 0:
                    p.wait(t, "xnrd")
                    p.wait(t, "wload")
                p.bank_acquire(t, bank)
                for d in range(NDT):
                    ins = t.matmul(pgb[:, 0:SLAB],
                                   sgw[:, d * D + i * 128:d * D + (i + 1) * 128],
                                   sxn[:, d * HCO + 3:d * HCO + HCO],
                                   start=(d == 0), stop=(d == NDT - 1))
                p.inc(ins, "gmm", 1)

            block.tensor(gate_mm)

            def gate_ep_s(s, i=i, pgb=pgb):
                p.wait(s, "gmm")
                if i == 0:
                    p.wait(s, "conv")
                p.wait(s, "u", p.val("u"))
                s.activation(ssig[:, :], pgb[:, 0:SLAB],
                             AF.Sigmoid, bias=sgateb[:, i:i + 1])
                p.inc(s.activation(ssil[:, :], sh_[:, i * SLAB:(i + 1) * SLAB],
                                   AF.Sigmoid), "gsig", 1)

            block.scalar(gate_ep_s)
            p.bank_release(bank, "gsig")

            def gate_u(v, i=i):
                p.wait(v, "gsig")
                p.edge(v, v.tensor_mul(ssil[:, :], ssil[:, :],
                                       sh_[:, i * SLAB:(i + 1) * SLAB]))
                p.inc(v.tensor_mul(su[:, i * SLAB:(i + 1) * SLAB], ssig[:, :], ssil[:, :]),
                      "u", 1)

            block.vector(gate_u)

        def zmm(t):
            p.wait(t, "u")
            p.bank_acquire(t, "pa")
            p.bank_acquire(t, "pb")
            for d in range(NDT):
                t.matmul(pa[0:36, 0:SLAB], sz36[:, d * 36:(d + 1) * 36],
                         su[:, d * SLAB:(d + 1) * SLAB],
                         start=(d == 0), stop=(d == NDT - 1))
            for d in range(NDT):
                ins = t.matmul(pb[0:2, 0:SLAB], sz2[:, d * 2:(d + 1) * 2],
                               su[:, d * SLAB:(d + 1) * SLAB],
                               start=(d == 0), stop=(d == NDT - 1))
            p.inc(ins, "zmm", 1)

        block.tensor(zmm)

        def z_ep(v):
            p.wait(v, "zmm")
            v.tensor_single_scalar(ssgn[:, :], pa[0:36, 0:SLAB], 0.0, ALU.is_gt)
            p.inc(v.tensor_scalar(ssgn[:, :], ssgn[:, :], 2.0, -1.0, ALU.mult, ALU.add),
                  "sgn", 1)

        block.vector(z_ep)
        p.bank_release("pa", "sgn")

        def z2_ep(s):
            p.wait(s, "zmm")
            p.inc(s.activation(sglgm[:, :], pb[0:2, 0:SLAB], AF.Sigmoid,
                               bias=sbz2[:, :]), "glgm", 1)

        block.scalar(z2_ep)
        p.bank_release("pb", "glgm")

        def z_store(g, col0=col0):
            p.wait(g, "sgn")
            p.wait(g, "glgm")
            p.inc(g.dma_start(out=o_signs[:, col0:col0 + SLAB], in_=ssgn[:, :]),
                  "zst", 16)
            p.inc(g.dma_start(out=o_glr[:, col0:col0 + SLAB], in_=sglgm[0:1, :]),
                  "zst", 16)
            p.inc(g.dma_start(out=o_gmr[:, col0:col0 + SLAB], in_=sglgm[1:2, :]),
                  "zst", 16)

        block.gpsimd(z_store)

        for tt in range(SLAB // 128):
            bank = ["pc", "pd"][tt % 2]
            pgb = {"pc": pc, "pd": pd}[bank]

            def qkv_mm(t, tt=tt, bank=bank, pgb=pgb):
                p.bank_acquire(t, bank)
                for d in range(NDT):
                    ins = t.matmul(pgb[:, 0:97],
                                   su[:, d * SLAB + tt * 128:d * SLAB + tt * 128 + 128],
                                   sqkvg[:, d * 97:(d + 1) * 97],
                                   start=(d == 0), stop=(d == NDT - 1))
                p.inc(ins, "qmm", 1)

            block.tensor(qkv_mm)

            def qkv_ep(v, tt=tt, sl=sl, pgb=pgb):
                p.wait(v, "qmm")
                p.edge(v, v.tensor_copy(sqk[:, tt * 97:(tt + 1) * 97], pgb[:, 0:97]))
                p.edge(v, v.tensor_scalar(sqk[:, tt * 97 + 96:tt * 97 + 97],
                                          sqk[:, tt * 97 + 96:tt * 97 + 97],
                                          swgb[:, :], None, ALU.add))
                p.inc(v.tensor_copy(szgpc[:, sl * 2 + tt:sl * 2 + tt + 1],
                                    sqk[:, tt * 97 + 96:tt * 97 + 97]), "qep", 1)

            block.vector(qkv_ep)
            p.bank_release(bank, "qep")

            def qkv_store(g, tt=tt, col0=col0):
                p.wait(g, "qep")
                p.inc(g.dma_start(out=o_qkvg[col0 + tt * 128:col0 + (tt + 1) * 128, :],
                                  in_=sqk[:, tt * 97:(tt + 1) * 97]), "qst", 16)

            block.gpsimd(qkv_store)

        for c in range(32):
            ib = c % 2
            bank = ["pc", "pd"][c % 2]
            pgb = {"pc": pc, "pd": pd}[bank]

            def inp_mm(t, c=c, bank=bank, pgb=pgb):
                p.bank_acquire(t, bank)
                for d in range(NDT):
                    ins = t.matmul(pgb[:, 0:SLAB],
                                   sstin[:, d * KSS * D + c * 128:d * KSS * D + (c + 1) * 128],
                                   su[:, d * SLAB:(d + 1) * SLAB],
                                   start=(d == 0), stop=(d == NDT - 1))
                p.inc(ins, "imm", 1)

            block.tensor(inp_mm)

            def scan_v(v, c=c, ib=ib, pgb=pgb):
                p.wait(v, "imm")
                p.inc(v.tensor_copy(sinp[:, ib * SLAB:(ib + 1) * SLAB],
                                    pgb[:, 0:SLAB]), "ipev", 1)
                k = c // 8
                p.edge(v, v.tensor_tensor_scan(ssc[:, ib * SLAB:(ib + 1) * SLAB],
                                     sdecay[:, k:k + 1].broadcast_to((128, SLAB)),
                                     sinp[:, ib * SLAB:(ib + 1) * SLAB],
                                     scar[:, c:c + 1], ALU.mult, ALU.add))
                v.tensor_copy(ss[:, c * SLAB:(c + 1) * SLAB],
                              ssc[:, ib * SLAB:(ib + 1) * SLAB])
                p.inc(v.tensor_copy(scar[:, c:c + 1],
                                    ssc[:, (ib + 1) * SLAB - 1:(ib + 1) * SLAB]),
                      "scan", 1)

            block.vector(scan_v)
            p.bank_release(bank, "ipev")

        def long_mm(t, sl=sl):
            p.wait(t, "scan")
            p.bank_acquire(t, "pl")
            for i in range(NDT):
                for c in range(32):
                    ins = t.matmul(plong(i),
                                   sstout[:, c * D + i * 128:c * D + (i + 1) * 128],
                                   ss[:, c * SLAB:(c + 1) * SLAB],
                                   start=(c == 0), stop=(c == 31))
            p.inc(ins, "lmm", 1)

        block.tensor(long_mm)

        def long_ev(v):
            p.wait(v, "lmm")
            for i in range(NDT):
                ins = v.tensor_copy(slong[:, i * SLAB:(i + 1) * SLAB], plong(i))
            p.inc(ins, "lev", 1)

        block.vector(long_ev)
        p.bank_release("pl", "lev")

        def long_store(g, col0=col0):
            p.wait(g, "lev")
            p.inc(g.dma_start(
                out=o_long[:, col0:col0 + SLAB].rearrange("(d p) n -> p d n", p=128),
                in_=slong.rearrange("p (d n) -> p d n", d=NDT)), "lst", 16)

        block.gpsimd(long_store)

        def slab_fence(g):
            p.wait(g, "lst")

        if sl == NSL - 1:
            block.gpsimd(slab_fence)
        else:
            def fence2(v, sl=sl):
                pass
            def store_fence(t, sl=sl):
                p.wait(t, "zst")
                p.wait(t, "qst")
                p.wait(t, "lst")
            block.tensor(store_fence)

    def fin_tr(t):
        p.wait(t, "scan")
        p.bank_acquire(t, "pa")
        p.inc(t.transpose(pa[0:32, 0:128], scar[:, :], seyef[:, :]), "ftr", 1)

    block.tensor(fin_tr)

    def fin_ev(v):
        p.wait(v, "ftr")
        p.inc(v.tensor_copy(sfint[:, :], pa[0:32, 0:128]), "fev", 1)

    block.vector(fin_ev)
    p.bank_release("pa", "fev")

    def fin_store(g):
        p.wait(g, "fev")
        p.inc(g.dma_start(out=o_finT, in_=sfint[:, :]), "fst", 16)
        p.inc(g.dma_start(out=o_zgpc, in_=szgpc[:, :]), "fst", 16)
        p.wait(g, "fst")
        p.wait(g, "zst")
        p.wait(g, "qst")
        p.wait(g, "lst")

    block.gpsimd(fin_store)


def build_launch_A():
    p = Prog()
    nc = p.nc
    aps = dict(
        xT=nc.dram_tensor("xT", [D, SH + 3], F32, kind="ExternalInput").ap(),
        w_gate=nc.dram_tensor("w_gate", [D, D], F32, kind="ExternalInput").ap(),
        w_stin=nc.dram_tensor("w_stin", [D, KSS * D], F32, kind="ExternalInput").ap(),
        w_stout=nc.dram_tensor("w_stout", [KSS * D, D], F32, kind="ExternalInput").ap(),
        w_z36=nc.dram_tensor("w_z36", [D, 36], F32, kind="ExternalInput").ap(),
        w_z2=nc.dram_tensor("w_z2", [D, 2], F32, kind="ExternalInput").ap(),
        b_z2=nc.dram_tensor("b_z2", [2, 1], F32, kind="ExternalInput").ap(),
        w_qkvg=nc.dram_tensor("w_qkvg", [D, 97], F32, kind="ExternalInput").ap(),
        wgb_col=nc.dram_tensor("wgb_col", [128, 1], F32, kind="ExternalInput").ap(),
        conv_w=nc.dram_tensor("conv_w", [128, NDT * 4], F32, kind="ExternalInput").ap(),
        gate_b=nc.dram_tensor("gate_b", [128, NDT], F32, kind="ExternalInput").ap(),
        logit_bc=nc.dram_tensor("logit_bc", [128, KSS], F32, kind="ExternalInput").ap(),
        o_signs=nc.dram_tensor("o_signs", [36, SH], F32, kind="ExternalOutput").ap(),
        o_glr=nc.dram_tensor("o_glr", [1, SH], F32, kind="ExternalOutput").ap(),
        o_gmr=nc.dram_tensor("o_gmr", [1, SH], F32, kind="ExternalOutput").ap(),
        o_qkvg=nc.dram_tensor("o_qkvg", [SH, 97], F32, kind="ExternalOutput").ap(),
        o_zgpc=nc.dram_tensor("o_zgpc", [128, 16], F32, kind="ExternalOutput").ap(),
        o_long=nc.dram_tensor("o_long", [D, SH], F32, kind="ExternalOutput").ap(),
        o_finT=nc.dram_tensor("o_finT", [32, 128], F32, kind="ExternalOutput").ap(),
    )
    p.begin()
    emit_A(p, **aps)
    p.close()
    return p.nc


def host_inputs_A(inputs, core):
    b, pos = core // 4, core % 4
    x = np.asarray(inputs["L0_attn_x"])[b]
    t0 = pos * SH
    xsh = np.zeros((D, SH + 3), np.float32)
    xsh[:, 3:] = np.ascontiguousarray(x[t0:t0 + SH].T)
    if pos > 0:
        xsh[:, 0:3] = x[t0 - 3:t0].T
    z36 = np.concatenate([inputs["L0_attn_read_bits_w"], inputs["L0_attn_write_bits_w"]], 0)
    qkvg = np.concatenate([inputs["L0_attn_qkey_w"], inputs["L0_attn_wkey_w"],
                           inputs["L0_attn_value_w"], inputs["L0_attn_write_gate_w"]], 0)
    z2 = np.concatenate([inputs["L0_attn_gate_long_w"], inputs["L0_attn_gate_mem_w"]], 0)
    f32 = np.float32
    return {
        "xT": xsh,
        "w_gate": np.ascontiguousarray(np.asarray(inputs["L0_attn_gate_w"]).T, dtype=f32),
        "w_stin": np.ascontiguousarray(np.asarray(inputs["L0_attn_state_in_w"]).T, dtype=f32),
        "w_stout": np.ascontiguousarray(np.asarray(inputs["L0_attn_state_out_w"]).T, dtype=f32),
        "w_z36": np.ascontiguousarray(z36.T, dtype=f32),
        "w_z2": np.ascontiguousarray(z2.T, dtype=f32),
        "b_z2": np.stack([np.asarray(inputs["L0_attn_gate_long_b"]),
                          np.asarray(inputs["L0_attn_gate_mem_b"])]).reshape(2, 1).astype(f32),
        "w_qkvg": np.ascontiguousarray(qkvg.T, dtype=f32),
        "wgb_col": np.broadcast_to(np.asarray(inputs["L0_attn_write_gate_b"]).reshape(1, 1),
                                   (128, 1)).astype(f32).copy(),
        "conv_w": np.ascontiguousarray(np.asarray(inputs["L0_attn_conv_w"]).reshape(NDT, 128, 4).transpose(1, 0, 2).reshape(128, NDT * 4), dtype=f32),
        "gate_b": np.ascontiguousarray(np.asarray(inputs["L0_attn_gate_b"]).reshape(NDT, 128).T, dtype=f32),
        "logit_bc": np.broadcast_to(np.asarray(inputs["L0_attn_state_decay_logit"]).reshape(1, KSS),
                                    (128, KSS)).astype(f32).copy(),
    }



QLVL = 80.0
LGCLAMP = 30.0
MSCALE = float(1.0 / np.sqrt(32.0))
EM80 = float(np.exp(-QLVL))


def emit_B(p, sgath, qgath, zgath, codesF, suF, suiF, thrF, altF, bndF, eyeF,
           o_readT, nch_limit=None):
    import os
    BSTAGE = int(os.environ.get("BSTAGE", "99"))
    nc = p.nc
    C = 128
    NCH = T // C
    NCH_RUN = NCH if nch_limit is None else nch_limit
    _IGNORE = """
    codesF = nc.dram_tensor("codesF", [9, 512], F32, kind="ExternalInput").ap()
    suF = nc.dram_tensor("suF", [128, 128], F32, kind="ExternalInput").ap()
    suiF = nc.dram_tensor("suiF", [128, 128], F32, kind="ExternalInput").ap()
    thrF = nc.dram_tensor("thrF", [128, 128], F32, kind="ExternalInput").ap()
    altF = nc.dram_tensor("altF", [128, 128], F32, kind="ExternalInput").ap()
    bndF = nc.dram_tensor("bndF", [128, 132], F32, kind="ExternalInput").ap()
    eyeFX = 0 """
    import os as _os
    BDBG = _os.environ.get("BDBG", "") == "1"
    if BDBG:
        dbg_names = ["d_mtmw", "d_w1h", "d_r1ht", "d_cnt", "d_sms", "d_lpack",
                     "d_lv", "d_rfr", "d_d4", "d_dt", "d_lhsr", "d_aA", "d_aE",
                     "d_read", "d_gath", "d_upd", "d_tbl", "d_cont", "d_att"]
        dbg_shapes = {"d_mtmw": [128, 256], "d_w1h": [128, 512], "d_r1ht": [128, 512],
                      "d_cnt": [128, 1], "d_sms": [128, 4], "d_lpack": [128, 6],
                      "d_lv": [128, 4], "d_rfr": [128, 4], "d_d4": [128, 128],
                      "d_dt": [32, 512], "d_lhsr": [128, 256], "d_aA": [128, 2],
                      "d_aE": [128, 2], "d_read": [128, 128], "d_gath": [128, 128],
                      "d_upd": [128, 528], "d_tbl": [128, 1024], "d_cont": [128, 256],
                      "d_att": [128, 4]}
        dbg = {nm: nc.dram_tensor(nm, dbg_shapes[nm], F32, kind="ExternalOutput").ap()
               for nm in dbg_names}

    s_rs0 = p.sb("s_rs0", [9, T], BF16)
    s_rs1 = p.sb("s_rs1", [9, T], BF16)
    s_ws0 = p.sb("s_ws0", [9, T], BF16)
    s_ws1 = p.sb("s_ws1", [9, T], BF16)
    s_kv = p.sb("s_kv", [128, NCH * 64], BF16)
    s_q = p.sb("s_q", [128, NCH * 32], F32)
    s_zg = p.sb("s_zg", [128, NCH], F32)
    s_g = p.sb("s_g", [128, NCH], F32)
    s_lg = p.sb("s_lg", [128, NCH], F32)
    s_codes = p.sb("s_codes", [9, 512], BF16)
    s_su = p.sb("s_su", [128, 128], BF16)
    s_sui = p.sb("s_sui", [128, 128], BF16)
    s_thr = p.sb("s_thr", [128, 128], F32)
    s_alt = p.sb("s_alt", [128, 128], F32)
    s_bnd = p.sb("s_bnd", [128, 132], F32)
    s_eye = p.sb("s_eye", [128, 128], BF16)
    s_eyef = p.sb("s_eyef", [128, 128], F32)
    s_rot = p.sb("s_rot", [32, 128], F32)
    s_ones1 = p.sb("s_ones1", [128, 1], BF16)
    s_tmp = p.sb("s_tmp", [128, 64], F32)

    s_table = p.sb("s_table", [128, 2 * 4 * 128], F32)
    s_tblb = p.sb("s_tblb", [128, 2 * 4 * 128], BF16)
    s_uexp = p.sb("s_uexp", [128, 8], F32)
    s_newt = p.sb("s_newt", [128, 512], F32)
    s_swp = p.sb("s_swp", [128, 256], F32)

    s_mtmw = p.sb("s_mtmw", [128, 256], BF16)
    s_w1h = p.sb("s_w1h", [128, 512], BF16)
    s_r1ht = p.sb("s_r1ht", [128, 512], BF16)
    s_mskD = p.sb("s_mskD", [128, 128], BF16)
    s_mskA = p.sb("s_mskA", [128, 128], BF16)
    s_mskB = p.sb("s_mskB", [128, 128], BF16)
    s_sms = p.sb("s_sms", [128, 4], BF16)
    s_par = p.sb("s_par", [128, 1], F32)
    s_cdbg = p.sb("s_cdbg", [128, 4], F32)
    s_cmp = p.sb("s_cmp", [128, 128], F32)
    s_rhs2 = p.sb("s_rhs2", [128, 2], BF16)
    s_lpack = p.sb("s_lpack", [128, 6], F32)
    s_neg4 = p.sb("s_neg4", [128, 4], F32)
    s_cum = p.sb("s_cum", [128, 4 * 33], F32)
    s_lv = p.sb("s_lv", [128, 4], F32)
    s_rfr = p.sb("s_rfr", [128, 4], F32)
    s_d4 = p.sb("s_d4", [128, 128], BF16)
    s_dT = p.sb("s_dT", [32, 256], BF16)
    s_dT1 = p.sb("s_dT1", [32, 256], BF16)
    s_exps = p.sb("s_exps", [128, 2], F32)
    s_g1 = p.sb("s_g1", [128, 2], F32)
    s_ecar = p.sb("s_ecar", [128, 2], F32)
    s_aA = p.sb("s_aA", [128, 2], F32)
    s_aE = p.sb("s_aE", [128, 2], F32)
    s_dE = p.sb("s_dE", [128, 2], F32)
    s_rhsr = p.sb("s_rhsr", [128, 128], BF16)
    s_rhsu = p.sb("s_rhsu", [128, 132], BF16)
    s_lhsr = p.sb("s_lhsr", [128, 256], BF16)
    s_upd = p.sb("s_upd", [128, 4 * 132], F32)
    s_cont = p.sb("s_cont", [128, 2 * 128], F32)
    s_contv = p.sb("s_contv", [128, 256], F32)
    s_sc4 = p.sb("s_sc4", [128, 4], F32)
    s_att = p.sb("s_att", [128, 4], F32)
    s_mx = p.sb("s_mx", [128, 1], F32)
    s_den = p.sb("s_den", [128, 1], F32)
    s_ro = p.sb("s_ro", [128, 32], F32)

    pb0 = p.psum("pb0")
    pb1 = p.psum("pb1")
    pb2 = p.psum("pb2")
    pb3 = p.psum("pb3")
    pb4 = p.psum("pb4")
    pb5 = p.psum("pb5")
    pbt = p.ent(nc.psum_tensor(p._part + "pbt", [128, 1024], BF16))
    pbrt = p.psum("pbrt")

    block = p.block

    def loads(g):
        for r in range(4):
            for dst, r0 in [(s_rs0, 0), (s_rs1, 9), (s_ws0, 18), (s_ws1, 27)]:
                p.inc(g.dma_start(out=dst[:, r * SH:(r + 1) * SH],
                                  in_=sgath[r * 36 + r0:r * 36 + r0 + 9, :]), "ld", 16)
            p.inc(g.dma_start(out=s_zg[:, r * 16:(r + 1) * 16],
                              in_=zgath[r * 128:(r + 1) * 128, :]), "ld", 16)
        p.inc(g.dma_start(out=s_kv.rearrange("p (c f) -> p c f", c=NCH),
                          in_=qgath.rearrange("(c p) f -> p c f", p=128)[:, :, 32:96]),
              "ld", 16)
        p.inc(g.dma_start(out=s_q.rearrange("p (c f) -> p c f", c=NCH),
                          in_=qgath.rearrange("(c p) f -> p c f", p=128)[:, :, 0:32]),
              "ld", 16)
        p.inc(g.dma_start(out=s_codes[:, :], in_=codesF), "ld", 16)
        p.inc(g.dma_start(out=s_su[:, :], in_=suF), "ld", 16)
        p.inc(g.dma_start(out=s_sui[:, :], in_=suiF), "ld", 16)
        p.inc(g.dma_start(out=s_thr[:, :], in_=thrF), "ld", 16)
        p.inc(g.dma_start(out=s_alt[:, :], in_=altF), "ld", 16)
        p.inc(g.dma_start(out=s_bnd[:, :], in_=bndF), "ld", 16)
        p.inc(g.dma_start(out=s_eye[:, :], in_=eyeF), "ld", 16)
        p.inc(g.dma_start(out=s_eyef[:, :], in_=eyeF), "ld", 16)

    block.gpsimd(loads)

    def init_g(s):
        p.wait(s, "ld")
        s.activation(s_g[:, :], s_zg[:, :], AF.Sigmoid)
        p.edge(s, s.activation(s_tmp[:, :], s_zg[:, :], AF.Abs), "ssy")
        p.edge(s, s.activation(s_tmp[:, :], s_tmp[:, :], AF.Exp, scale=-1.0), "ssy")
        p.edge(s, s.activation(s_tmp[:, :], s_tmp[:, :], AF.Ln, bias=1.0), "ssy")
        p.inc(s.activation(s_lg[:, :], s_zg[:, :], AF.Relu), "ginit", 1)

    block.scalar(init_g)

    def init_g2(v):
        p.wait(v, "ginit")
        p.edge(v, v.tensor_add(s_lg[:, :], s_lg[:, :], s_tmp[:, :]))
        p.edge(v, v.tensor_scalar(s_lg[:, :], s_lg[:, :], -1.0, None, ALU.mult))
        p.edge(v, v.tensor_scalar(s_lg[:, :], s_lg[:, :], -LGCLAMP, None, ALU.max))
        v.memset(s_ones1[:, :], 1.0)
        v.memset(s_table[:, :], 0.0)
        p.inc(v.memset(s_tblb[:, :], 0.0), "ginit2", 1)

    block.vector(init_g2)

    for c in range(NCH_RUN):
        cc = slice(c * C, (c + 1) * C)
        for h in range(2):
            rs = (s_rs0 if h == 0 else s_rs1)[:, cc]
            ws = (s_ws0 if h == 0 else s_ws1)[:, cc]
            tbl = s_table[:, h * 512:(h + 1) * 512]

            def mm_signs(t, rs=rs, ws=ws, c=c, h=h):
                if c == 0 and h == 0:
                    p.wait(t, "ginit2")
                p.bank_acquire(t, "b0")
                t.matmul(pb0[:, 0:128], ws, rs, start=True, stop=True)
                t.matmul(pb0[:, 128:256], ws, ws, start=True, stop=True)
                p.bank_acquire(t, "b1")
                t.matmul(pb1[:, 0:512], ws, s_codes[:, :], start=True, stop=True)
                p.bank_acquire(t, "b2")
                for rt in range(2):
                    t.matmul(pb2[:, rt * 128:(rt + 1) * 128],
                             s_codes[:, rt * 128:(rt + 1) * 128], rs,
                             start=True, stop=True)
                p.bank_acquire(t, "b3")
                for rt in range(2, 4):
                    ins = t.matmul(pb3[:, (rt - 2) * 128:(rt - 1) * 128],
                                   s_codes[:, rt * 128:(rt + 1) * 128], rs,
                                   start=True, stop=True)
                p.inc(ins, "smm", 1)

            block.tensor(mm_signs)

            def ev_signs(v, c=c, h=h):
                p.wait(v, "smm")
                v.tensor_single_scalar(s_mtmw[:, :], pb0[:, 0:256], 8.5, ALU.is_ge)
                v.tensor_single_scalar(s_w1h[:, :], pb1[:, 0:512], 8.5, ALU.is_ge)
                v.tensor_single_scalar(s_r1ht[:, 0:256], pb2[:, 0:256], 8.5, ALU.is_ge)
                p.inc(v.tensor_single_scalar(s_r1ht[:, 256:512], pb3[:, 0:256], 8.5, ALU.is_ge),
                      "sev", 1)
                p.wait(v, "sev")
                v.tensor_mul(s_mskD[:, :], s_mtmw[:, 0:128], s_su[:, :])
                v.tensor_mul(s_mskA[:, :], s_mtmw[:, 128:256], s_su[:, :])
                p.inc(v.tensor_mul(s_mskB[:, :], s_mtmw[:, 128:256], s_sui[:, :]),
                      "msk", 1)

            block.vector(ev_signs)
            p.bank_release("b0", "sev")
            p.bank_release("b1", "sev")
            p.bank_release("b2", "sev")
            p.bank_release("b3", "sev")

            if BSTAGE < 2:
                continue
            def mm_cnt(t, h=h):
                p.wait(t, "msk")
                p.bank_acquire(t, "b0")
                p.inc(t.matmul(pb0[:, 0:1], s_mskA[:, :], s_ones1[:, :],
                               start=True, stop=True), "cmm", 1)

            block.tensor(mm_cnt)

            def slot_par(v, c=c, h=h):
                p.wait(v, "cmm")
                v.tensor_scalar(s_cmp[:, :], s_thr[:, :], pb0[:, 0:1], None, ALU.is_le)
                p.edge(v, v.tensor_mul(s_cmp[:, :], s_cmp[:, :], s_alt[:, :]))
                p.inc(v.reduce_sum(s_par[:, :], s_cmp[:, :], axis=AX.X), "parred", 1)
                p.wait(v, "parred")
                v.tensor_copy(s_sms[:, 1:2], s_par[:, :])
                v.tensor_scalar(s_sms[:, 0:1], s_par[:, :], -1.0, 1.0, ALU.mult, ALU.add)
                p.edge(v, v.tensor_scalar(s_sms[:, 2:3], s_par[:, :], -2.0, 1.0, ALU.mult, ALU.add))
                p.inc(v.tensor_scalar(s_rhs2[:, 0:2], s_sms[:, 0:2],
                                      s_lg[:, c:c + 1], None, ALU.mult), "sms", 1)

            block.vector(slot_par)
            p.bank_release("b0", "parred")

            if BSTAGE < 3:
                continue
            def mm_L(t, h=h):
                p.wait(t, "sms")
                p.bank_acquire(t, "b0")
                t.matmul(pb0[:, 0:2], s_mskB[:, :], s_rhs2[:, 0:2], start=True, stop=True)
                t.matmul(pb0[:, 2:4], s_mtmw[:, 128:256], s_rhs2[:, 0:2], start=True, stop=True)
                p.inc(t.matmul(pb0[:, 4:6], s_mskD[:, :], s_rhs2[:, 0:2],
                               start=True, stop=True), "lmm2", 1)

            block.tensor(mm_L)

            def levels(v, c=c, h=h):
                p.wait(v, "lmm2")
                p.inc(v.tensor_copy(s_lpack[:, :], pb0[:, 0:6]), "lev2", 1)
                p.wait(v, "lev2")
                v.tensor_scalar(s_neg4[:, 0:2], s_lpack[:, 0:2], -1.0, None, ALU.mult)
                p.edge(v, v.tensor_scalar(s_neg4[:, 2:4], s_lpack[:, 4:6], -1.0, None, ALU.mult))
                for j in range(4):
                    v.tensor_scalar(s_cum[:, j * 33:(j + 1) * 33], s_bnd[:, 0:33],
                                    s_neg4[:, j:j + 1], None, ALU.is_le)
                p.edge(v, v.engine_nop())
                p.inc(v.reduce_sum(s_lv[:, :],
                                   s_cum.rearrange("p (g n) -> p g n", g=4)[:, :, 1:33],
                                   axis=AX.X), "vsy", 1)
                p.wait(v, "vsy")
                v.scalar_tensor_tensor(s_rfr[:, :], s_lv[:, :], -QLVL, s_neg4[:, :],
                                       ALU.mult, ALU.add)
                p.inc(v.tensor_sub(s_d4.rearrange("p (g n) -> p g n", g=4),
                                   s_cum.rearrange("p (g n) -> p g n", g=4)[:, :, 0:32],
                                   s_cum.rearrange("p (g n) -> p g n", g=4)[:, :, 1:33]),
                      "lvd", 1)
                p.wait(v, "lvd")
                p.inc(v.scalar_tensor_tensor(
                    s_d4.rearrange("p (g n) -> p g n", g=4)[:, 2:4, 0:31],
                    s_d4.rearrange("p (g n) -> p g n", g=4)[:, 2:4, 1:32],
                    EM80,
                    s_d4.rearrange("p (g n) -> p g n", g=4)[:, 2:4, 0:31],
                    ALU.mult, ALU.add), "laug", 1)

            block.vector(levels)
            p.bank_release("b0", "lev2")

            if BSTAGE < 4:
                continue
            def exps_s(s, c=c, h=h):
                p.wait(s, "laug")
                s.activation(s_exps[:, :], s_rfr[:, 0:2], AF.Exp)
                s.activation(s_g1[:, :], s_rfr[:, 2:4], AF.Exp, scale=-1.0)
                p.inc(s.activation(s_ecar[:, :], s_lpack[:, 4:6], AF.Exp), "exps", 1)

            block.scalar(exps_s)

            def mk_a(v, c=c, h=h):
                p.wait(v, "exps")
                p.edge(v, v.tensor_scalar(s_aA[:, :], s_exps[:, :], s_g[:, c:c + 1], None, ALU.mult))
                v.tensor_mul(s_aA[:, :], s_aA[:, :], s_sms[:, 0:2])
                v.tensor_sub(s_dE[:, :], s_lpack[:, 2:4], s_lpack[:, 0:2])
                p.inc(v.engine_nop(), "mka", 1)

            block.vector(mk_a)

            def exps_e(s):
                p.wait(s, "mka")
                p.inc(s.activation(s_aE[:, :], s_dE[:, :], AF.Exp), "expe", 1)

            block.scalar(exps_e)

            def mk_rhs(v, c=c, h=h):
                p.wait(v, "expe")
                p.edge(v, v.tensor_scalar(s_aE[:, :], s_aE[:, :], s_g[:, c:c + 1], None, ALU.mult))
                p.edge(v, v.tensor_mul(s_aE[:, :], s_aE[:, :], s_sms[:, 0:2]))
                for si in range(2):
                    v.tensor_scalar(s_rhsr[:, si * 64:(si + 1) * 64],
                                    s_kv[:, c * 64:(c + 1) * 64],
                                    s_aA[:, si:si + 1], None, ALU.mult)
                    v.tensor_scalar(s_rhsu[:, si * 64:(si + 1) * 64],
                                    s_kv[:, c * 64:(c + 1) * 64],
                                    s_aE[:, si:si + 1], None, ALU.mult)
                v.tensor_copy(s_rhsu[:, 128:130], s_rhs2[:, 0:2])
                p.inc(v.tensor_copy(s_rhsu[:, 130:131], s_sms[:, 2:3]), "rhs", 1)

            block.vector(mk_rhs)

            if BSTAGE < 5:
                continue
            def mm_tr(t, h=h):
                p.wait(t, "laug")
                p.bank_acquire(t, "bt")
                t.transpose(pbt[0:64, 0:128], s_d4[:, 0:64], s_eye[:, :])
                p.inc(t.transpose(pbt[0:64, 128:256], s_d4[:, 64:128], s_eye[:, :]),
                      "trmm", 1)

            block.tensor(mm_tr)

            def ev_tr(v):
                p.wait(v, "trmm")
                v.tensor_copy(s_dT[:, :], pbt[0:32, 0:256])
                p.inc(v.tensor_copy(s_dT1[:, :], pbt[32:64, 0:256]), "trev", 1)

            block.vector(ev_tr)
            p.bank_release("bt", "trev")

            if BSTAGE < 42:
                continue
            def mm_E(t, h=h):
                p.wait(t, "trev")
                p.bank_acquire(t, "b0")
                t.matmul(pb0[:, 0:128], s_dT[0:32, 0:128], s_dT[0:32, 128:256],
                         start=True, stop=True)
                p.inc(t.matmul(pb0[:, 128:256], s_dT1[0:32, 0:128], s_dT1[0:32, 128:256],
                               start=True, stop=True), "emm", 1)

            block.tensor(mm_E)

            if BSTAGE < 43:
                continue
            def mk_lhsr(v, h=h):
                p.wait(v, "emm")
                v.tensor_mul(s_lhsr[:, 0:128], s_mskD[:, :], pb0[:, 0:128])
                p.inc(v.tensor_mul(s_lhsr[:, 128:256], s_mskD[:, :], pb0[:, 128:256]),
                      "lhsr", 1)

            block.vector(mk_lhsr)
            p.bank_release("b0", "lhsr")

            if BSTAGE < 6:
                continue
            def mm_read(t, h=h, tbl=tbl):
                p.wait(t, "lhsr")
                p.wait(t, "rhs")
                p.wait(t, "tupd")
                p.bank_acquire(t, "b4")
                t.matmul(pb4[:, 0:64], s_lhsr[:, 0:128], s_rhsr[:, 0:64],
                         start=True, stop=True)
                t.matmul(pb4[:, 64:128], s_lhsr[:, 128:256], s_rhsr[:, 64:128],
                         start=True, stop=True)
                for rt in range(4):
                    ins = t.matmul(pb4[:, 128:256],
                                   s_r1ht[:, rt * 128:(rt + 1) * 128],
                                   s_tblb[:, h * 512 + rt * 128:h * 512 + (rt + 1) * 128],
                                   start=(rt == 0), stop=(rt == 3))
                p.inc(ins, "rmm", 1)
                p.bank_acquire(t, "b5")
                for rt in range(2):
                    t.matmul(pb5[:, rt * 256:rt * 256 + 131],
                             s_w1h[:, rt * 128:(rt + 1) * 128], s_rhsu[:, 0:131],
                             start=True, stop=True)
                p.bank_acquire(t, "b1")
                for rt in range(2, 4):
                    ins = t.matmul(pb1[:, (rt - 2) * 256:(rt - 2) * 256 + 131],
                                   s_w1h[:, rt * 128:(rt + 1) * 128], s_rhsu[:, 0:131],
                                   start=True, stop=True)
                p.inc(ins, "umm", 1)

            block.tensor(mm_read)

            if BSTAGE < 7:
                continue
            def upd_ev(v, h=h):
                p.wait(v, "umm")
                v.tensor_copy(s_upd[:, 0:264].rearrange("p (g n) -> p g n", g=2)[:, :, 0:131],
                              pb5.rearrange("p (g n) -> p g n", g=2)[:, :, 0:131])
                p.inc(v.tensor_copy(s_upd[:, 264:528].rearrange("p (g n) -> p g n", g=2)[:, :, 0:131],
                                    pb1.rearrange("p (g n) -> p g n", g=2)[:, :, 0:131]),
                      "uev", 1)

            block.vector(upd_ev)
            p.bank_release("b5", "uev")
            p.bank_release("b1", "uev")

            def upd_exp(s, h=h):
                p.wait(s, "uev")
                p.inc(s.activation(s_uexp.rearrange("p (g n) -> p g n", g=4),
                                   s_upd.rearrange("p (g n) -> p g n", g=4)[:, :, 128:130],
                                   AF.Exp), "uexp", 1)

            block.scalar(upd_exp)

            def upd_v(v, h=h, tbl=tbl):
                p.wait(v, "uexp")
                tb3 = tbl.rearrange("p (g n) -> p g n", g=4)
                up3 = s_upd.rearrange("p (g n) -> p g n", g=4)
                ue3 = s_uexp.rearrange("p (g n) -> p g n", g=4)
                nt3 = s_newt.rearrange("p (g n) -> p g n", g=4)
                v.tensor_mul(nt3[:, :, 0:64], tb3[:, :, 0:64],
                             ue3[:, :, 0:1].broadcast_to((128, 4, 64)))
                v.tensor_add(nt3[:, :, 0:64], nt3[:, :, 0:64], up3[:, :, 0:64])
                v.tensor_mul(nt3[:, :, 64:128], tb3[:, :, 64:128],
                             ue3[:, :, 1:2].broadcast_to((128, 4, 64)))
                v.tensor_add(nt3[:, :, 64:128], nt3[:, :, 64:128], up3[:, :, 64:128])
                v.tensor_sub(s_swp.rearrange("p (g n) -> p g n", g=4),
                             nt3[:, :, 64:128], nt3[:, :, 0:64])
                v.tensor_mul(s_swp.rearrange("p (g n) -> p g n", g=4),
                             s_swp.rearrange("p (g n) -> p g n", g=4),
                             up3[:, :, 130:131].broadcast_to((128, 4, 64)))
                v.tensor_add(tb3[:, :, 0:64], nt3[:, :, 0:64],
                             s_swp.rearrange("p (g n) -> p g n", g=4))
                v.tensor_sub(tb3[:, :, 64:128], nt3[:, :, 64:128],
                             s_swp.rearrange("p (g n) -> p g n", g=4))
                p.inc(v.tensor_copy(s_tblb[:, h * 512:(h + 1) * 512], tbl), "tupd", 1)

            block.vector(upd_v)

            if BSTAGE < 8:
                continue
            def cont_v(v, h=h, c=c):
                p.wait(v, "rmm")
                for si in range(2):
                    v.tensor_scalar(s_cont[:, si * 64:(si + 1) * 64],
                                    pb4[:, 128 + si * 64:128 + (si + 1) * 64],
                                    s_ecar[:, si:si + 1], None, ALU.mult)
                p.edge(v, v.engine_nop())
                for si in range(2):
                    v.scalar_tensor_tensor(s_contv[:, h * 128 + si * 64:h * 128 + (si + 1) * 64],
                                           pb4[:, si * 64:(si + 1) * 64],
                                           s_g1[:, si:si + 1],
                                           s_cont[:, si * 64:(si + 1) * 64],
                                           ALU.mult, ALU.add)
                p.inc(v.engine_nop(), "cont", 1)

            block.vector(cont_v)
            p.bank_release("b4", "cont")

        if BDBG and c == 0:
            def dbg_dump(g, c=c):
                p.wait(g, "tupd")
                p.wait(g, "cont")
                p.inc(g.dma_start(out=dbg["d_cnt"], in_=s_cdbg[:, 0:1]), "dbgst", 16)
                for nm, src_ in [("d_mtmw", s_mtmw), ("d_w1h", s_w1h), ("d_r1ht", s_r1ht),
                                 ("d_sms", s_sms), ("d_lpack", s_lpack),
                                 ("d_lv", s_lv), ("d_rfr", s_rfr),
                                 ("d_aA", s_aA), ("d_aE", s_aE),
                                 ("d_upd", s_upd), ("d_tbl", s_table), ("d_cont", s_contv)]:
                    p.inc(g.dma_start(out=dbg[nm], in_=src_[:, :]), "dbgst", 16)
                p.inc(g.dma_start(out=dbg["d_d4"], in_=s_d4[:, :]), "dbgst", 16)
                p.inc(g.dma_start(out=dbg["d_dt"][:, 0:256], in_=s_dT[:, :]), "dbgst", 16)
                p.inc(g.dma_start(out=dbg["d_dt"][:, 256:512], in_=s_dT1[:, :]), "dbgst", 16)
                p.inc(g.dma_start(out=dbg["d_lhsr"], in_=s_lhsr[:, :]), "dbgst", 16)
                p.inc(g.dma_start(out=dbg["d_att"], in_=s_neg4[:, :]), "dbgst", 16)
                p.inc(g.dma_start(out=dbg["d_read"][:, 0:2], in_=s_dE[:, :]), "dbgst", 16)
            block.gpsimd(dbg_dump)
        if BSTAGE < 9:
            continue
        def attn_v(v, c=c):
            p.wait(v, "cont")
            v.tensor_mul(s_cmp[:, 0:128].rearrange("p (g n) -> p g n", g=4),
                         s_contv.rearrange("p (g n) -> p g n", g=4)[:, :, 0:32],
                         s_q[:, c * 32:(c + 1) * 32].unsqueeze(1).broadcast_to((128, 4, 32)))
            p.edge(v, v.engine_nop())
            p.inc(v.reduce_sum(s_sc4[:, :], s_cmp[:, 0:128].rearrange("p (g n) -> p g n", g=4),
                               axis=AX.X), "vsy", 1)
            p.wait(v, "vsy")
            p.inc(v.reduce_max(s_mx[:, :], s_sc4[:, :], axis=AX.X), "vsy", 1)
            p.wait(v, "vsy")
            p.inc(v.tensor_scalar(s_mx[:, :], s_mx[:, :], -MSCALE, None, ALU.mult), "att1", 1)

        block.vector(attn_v)

        def attn_s(s, c=c):
            p.wait(s, "att1")
            p.inc(s.activation(s_att[:, :], s_sc4[:, :], AF.Exp,
                               bias=s_mx[:, :], scale=MSCALE), "att2", 1)

        block.scalar(attn_s)

        def attn_v2(v, c=c):
            p.wait(v, "att2")
            p.inc(v.reduce_sum(s_den[:, :], s_att[:, :], axis=AX.X), "vsy", 1)
            p.wait(v, "vsy")
            p.inc(v.reciprocal(s_den[:, :], s_den[:, :]), "vsy", 1)
            p.wait(v, "vsy")
            p.edge(v, v.tensor_scalar(s_att[:, :], s_att[:, :], s_den[:, :], None, ALU.mult))
            p.edge(v, v.tensor_scalar(s_ro[:, :], s_contv[:, 32:64], s_att[:, 0:1], None, ALU.mult))
            for hs in range(1, 4):
                p.edge(v, v.scalar_tensor_tensor(s_ro[:, :],
                                                 s_contv[:, hs * 64 + 32:hs * 64 + 64],
                                                 s_att[:, hs:hs + 1], s_ro[:, :],
                                                 ALU.mult, ALU.add))
            p.inc(v.engine_nop(), "ro", 1)

        block.vector(attn_v2)

        def ro_tr(t, c=c):
            p.wait(t, "ro")
            p.bank_acquire(t, "brt")
            p.inc(t.transpose(pbrt[0:32, 0:128], s_ro[:, :], s_eyef[:, :]), "rtr2", 1)

        block.tensor(ro_tr)

        def ro_ev(v, c=c):
            p.wait(v, "rtr2")
            p.inc(v.tensor_copy(s_rot[:, :], pbrt[0:32, 0:128]), "rev2", 1)

        block.vector(ro_ev)
        p.bank_release("brt", "rev2")

        def ro_store(g, c=c):
            p.wait(g, "rev2")
            p.inc(g.dma_start(out=o_readT[:, c * C:(c + 1) * C], in_=s_rot[:, :]), "rst", 16)

        block.gpsimd(ro_store)

        def ro_fence(v, c=c):
            p.wait(v, "rst")

        block.vector(ro_fence)

    def done(g):
        p.wait(g, "rst")

    block.gpsimd(done)


def build_launch_B(nch_limit=None):
    p = Prog()
    nc = p.nc
    aps = dict(
        sgath=nc.dram_tensor("sgath", [144, SH], F32, kind="ExternalInput").ap(),
        qgath=nc.dram_tensor("qgath", [T, 97], F32, kind="ExternalInput").ap(),
        zgath=nc.dram_tensor("zgath", [512, 16], F32, kind="ExternalInput").ap(),
        codesF=nc.dram_tensor("codesF", [9, 512], F32, kind="ExternalInput").ap(),
        suF=nc.dram_tensor("suF", [128, 128], F32, kind="ExternalInput").ap(),
        suiF=nc.dram_tensor("suiF", [128, 128], F32, kind="ExternalInput").ap(),
        thrF=nc.dram_tensor("thrF", [128, 128], F32, kind="ExternalInput").ap(),
        altF=nc.dram_tensor("altF", [128, 128], F32, kind="ExternalInput").ap(),
        bndF=nc.dram_tensor("bndF", [128, 132], F32, kind="ExternalInput").ap(),
        eyeF=nc.dram_tensor("eyeF", [128, 128], F32, kind="ExternalInput").ap(),
        o_readT=nc.dram_tensor("o_readT", [32, T], F32, kind="ExternalOutput").ap(),
    )
    p.begin()
    emit_B(p, nch_limit=nch_limit, **aps)
    p.close()
    return p.nc



def emit_C(p, xT, longT, finAG, gl_row, gm_row, readTF, owns_bc, w_stout, w_mo,
           w_up, w_dn, b_up, b_dn, logit_row, ecp_row, mask44, iotap1, o_x1):
    import os as _os
    CST = int(_os.environ.get("CST", "99"))
    nc = p.nc
    SLAB = 256
    NSL = SH // SLAB

    swgt = p.sb("swgt", [128, 32 * D], BF16)
    smo = p.sb("smo", [32, D], BF16)
    sbup = p.sb("sbup", [128, 32], F32)
    sbdn = p.sb("sbdn", [1, D], BF16)
    sfin = p.sb("sfin", [4, KSS * D], BF16)
    sglr = p.sb("sglr", [1, SH], F32)
    sgmr = p.sb("sgmr", [1, SH], F32)
    srdt = p.sb("srdt", [32, SH], BF16)
    slogit = p.sb("slogit", [1, KSS], F32)
    slnd = p.sb("slnd", [1, KSS], F32)
    secp = p.sb("secp", [1, 4], F32)
    smask = p.sb("smask", [4, 4], F32)
    swmat = p.sb("swmat", [4, 4], BF16)
    srdf = p.sb("srdf", [32, T], BF16)
    sowns = p.sb("sowns", [32, 64], F32)
    siop = p.sb("siop", [1, SH], F32)
    sq = p.sb("sq", [1, KSS * D], BF16)
    sdec = p.sb("sdec", [1, KSS * SLAB], BF16)
    scarb = p.sb("scarb", [128, 32], BF16)

    sx = p.sb("sx", [128, NDT * SLAB], F32)
    slg = p.sb("slg", [128, NDT * SLAB], F32)
    sy = p.sb("sy", [128, NDT * SLAB], F32)
    sw_ = p.sb("sw_", [128, NDT * SLAB], F32)
    syn = p.sb("syn", [128, NDT * SLAB], BF16)
    sh_ = p.sb("sh_", [128, 16 * SLAB], BF16)
    swb = p.sb("swb", [128, NDT * SLAB], BF16)
    swb2 = p.sb("swb2", [128, NDT * SLAB], BF16)
    st1 = p.sb("st1", [128, SLAB], F32)
    st2 = p.sb("st2", [128, SLAB], F32)
    sct4 = p.sb("sct4", [128, 4 * SLAB], BF16)
    scc4 = p.sb("scc4", [128, 4 * SLAB], BF16)
    sglbc = p.sb("sglbc", [128, SLAB], F32)
    sgmbc = p.sb("sgmbc", [128, SLAB], F32)
    srow = p.sb("srow", [1, 3 * SLAB], F32)
    sones = p.sb("sones", [128, 1], BF16)
    sonesr = p.sb("sonesr", [1, SLAB], BF16)
    sonesrf = p.sb("sonesrf", [1, 128], F32)
    sx1 = p.sb("sx1", [128, NDT * SLAB], F32)

    pa = p.psum("pa")
    pb = p.psum("pb")
    pc = p.psum("pc")
    pd = p.psum("pd")
    pe = p.psum("pe")
    pf = p.psum("pf")

    block = p.block

    def loads(g):
        p.inc(g.dma_start(out=swgt.rearrange("p (c n) -> p c n", c=32),
                          in_=w_stout.rearrange("(c p) n -> p c n", p=128)), "wl", 16)
        p.inc(g.dma_start(out=smo[:, :], in_=w_mo), "wl", 16)
        p.inc(g.dma_start(out=sbup[:, :], in_=b_up), "wl", 16)
        p.inc(g.dma_start(out=sbdn[:, :], in_=b_dn), "wl", 16)
        for r in range(4):
            p.inc(g.dma_start(
                out=sfin[r:r + 1, :],
                in_=finAG[r * 32:(r + 1) * 32, :].rearrange("c pp -> (c pp)").unsqueeze(0)), "wl", 16)
        p.inc(g.dma_start(out=sglr[:, :], in_=gl_row), "wl", 16)
        p.inc(g.dma_start(out=sgmr[:, :], in_=gm_row), "wl", 16)
        p.inc(g.dma_start(out=srdf[:, :], in_=readTF), "wl", 16)
        p.inc(g.dma_start(out=sowns[:, :], in_=owns_bc), "wl", 16)
        p.inc(g.dma_start(out=slogit[:, :], in_=logit_row), "wl", 16)
        p.inc(g.dma_start(out=secp[:, :], in_=ecp_row), "wl", 16)
        p.inc(g.dma_start(out=smask[:, :], in_=mask44), "wl", 16)
        p.inc(g.dma_start(out=siop[:, :], in_=iotap1), "wl", 16)

    block.gpsimd(loads)

    def init_v(v):
        v.memset(sones[:, :], 1.0)
        v.memset(sonesr[:, :], 1.0)
        p.inc(v.memset(sonesrf[:, :], 1.0), "iv", 1)
        p.wait(v, "wl")
        for j in range(16):
            for q in range(4):
                c = j + 16 * q
                if q == 0:
                    v.tensor_scalar(srdt[:, j * 128:(j + 1) * 128],
                                    srdf[:, c * 128:(c + 1) * 128],
                                    sowns[:, c:c + 1], None, ALU.mult)
                else:
                    ins = v.scalar_tensor_tensor(srdt[:, j * 128:(j + 1) * 128],
                                                 srdf[:, c * 128:(c + 1) * 128],
                                                 sowns[:, c:c + 1],
                                                 srdt[:, j * 128:(j + 1) * 128],
                                                 ALU.mult, ALU.add)
            p.edge(v, ins)
        p.inc(v.engine_nop(), "rdt", 1)

    block.vector(init_v)

    def dec_s(s):
        p.wait(s, "wl")
        p.edge(s, s.activation(slnd[:, :], slogit[:, :], AF.Sigmoid), "ssy")
        p.inc(s.activation(slnd[:, :], slnd[:, :], AF.Ln), "lnd", 1)

    block.scalar(dec_s)

    def wmat_mm(t):
        p.wait(t, "lnd")
        p.wait(t, "iv")
        p.inc(t.matmul(pa[0:4, 0:4], secp[:, :], slnd[:, :], start=True, stop=True),
              "wm1", 1)

    block.tensor(wmat_mm)

    def wmat_s(s):
        p.wait(s, "wm1")
        p.inc(s.activation(pa[0:4, 4:8], pa[0:4, 0:4], AF.Exp), "wexp", 1)

    block.scalar(wmat_s)

    def wmat_v(v):
        p.wait(v, "wexp")
        p.inc(v.tensor_mul(swmat[:, :], pa[0:4, 4:8], smask[:, :]), "wmat", 1)

    block.vector(wmat_v)
    p.bank_release("pa", "wmat")

    if CST < 2:
        def cdone(g):
            p.wait(g, "wl")
        block.gpsimd(cdone)
        return
    def carry_mm(t):
        p.wait(t, "wmat")
        p.bank_acquire(t, "pb")
        for c in range(32):
            k = c // 8
            ins = t.matmul(pb[:, c:c + 1], sfin[:, c * 128:(c + 1) * 128],
                           swmat[:, k:k + 1], start=True, stop=True)
        p.inc(ins, "car", 1)

    block.tensor(carry_mm)

    def carry_ev(v):
        p.wait(v, "car")
        p.inc(v.tensor_copy(scarb[:, :], pb[:, 0:32]), "carev", 1)

    block.vector(carry_ev)
    p.bank_release("pb", "carev")

    for k in range(KSS):
        def q_mm(t, k=k):
            if k == 0:
                p.wait(t, "carev")
            p.bank_acquire(t, "pc")
            for ci in range(8):
                c = k * 8 + ci
                t.matmul(pc[0:1, 0:512], scarb[:, c:c + 1],
                         swgt[:, c * D:c * D + 512],
                         start=(ci == 0), stop=(ci == 7))
            p.bank_acquire(t, "pd")
            for ci in range(8):
                c = k * 8 + ci
                ins = t.matmul(pd[0:1, 0:512], scarb[:, c:c + 1],
                               swgt[:, c * D + 512:(c + 1) * D],
                               start=(ci == 0), stop=(ci == 7))
            p.inc(ins, "qmm", 1)

        block.tensor(q_mm)

        def q_ev(v, k=k):
            p.wait(v, "qmm")
            v.tensor_copy(sq[:, k * D:k * D + 512], pc[0:1, 0:512])
            p.inc(v.tensor_copy(sq[:, k * D + 512:(k + 1) * D], pd[0:1, 0:512]),
                  "qev", 1)

        block.vector(q_ev)
        p.bank_release("pc", "qev")
        p.bank_release("pd", "qev")

    if CST < 3:
        return
    for sl in range(NSL):
        col0 = sl * SLAB

        def ld_slab(g, sl=sl, col0=col0):
            if sl > 0:
                p.wait(g, "ynv")
            p.inc(g.dma_start(out=sx.rearrange("p (d n) -> p d n", d=NDT),
                              in_=xT[:, col0:col0 + SLAB].rearrange("(d p) n -> p d n", p=128)),
                  "xl", 16)
            p.inc(g.dma_start(out=slg.rearrange("p (d n) -> p d n", d=NDT),
                              in_=longT[:, col0:col0 + SLAB].rearrange("(d p) n -> p d n", p=128)),
                  "xl", 16)

        block.sync(ld_slab)

        def dec_rows(s, sl=sl, col0=col0):
            if sl == 0:
                p.wait(s, "qev")
            else:
                p.wait(s, "corr")
            for k in range(KSS):
                ins = s.activation(sdec[:, k * SLAB:(k + 1) * SLAB],
                                   siop[:, col0:col0 + SLAB], AF.Exp,
                                   scale=slnd[:, k:k + 1])
            p.inc(ins, "drow", 1)

        block.scalar(dec_rows)

        def gbc_mm(t, col0=col0):
            p.bank_acquire(t, "pa")
            t.matmul(pa[:, 0:SLAB], sonesrf[:, :], sglr[:, col0:col0 + SLAB],
                     start=True, stop=True)
            p.bank_acquire(t, "pb")
            p.inc(t.matmul(pb[:, 0:SLAB], sonesrf[:, :], sgmr[:, col0:col0 + SLAB],
                           start=True, stop=True), "gbc", 1)

        block.tensor(gbc_mm)

        def gbc_ev(v):
            p.wait(v, "gbc")
            v.tensor_copy(sglbc[:, :], pa[:, 0:SLAB])
            p.inc(v.tensor_copy(sgmbc[:, :], pb[:, 0:SLAB]), "gbe", 1)

        block.vector(gbc_ev)
        p.bank_release("pa", "gbe")
        p.bank_release("pb", "gbe")

        for i in range(NDT):
            bank = ["pc", "pd"][i % 2]
            pgb = {"pc": pc, "pd": pd}[bank]
            bank2 = ["pe", "pf"][i % 2]
            pgb2 = {"pe": pe, "pf": pf}[bank2]

            def corr_mm(t, i=i, col0=col0, bank=bank, pgb=pgb, bank2=bank2, pgb2=pgb2):
                if i == 0:
                    p.wait(t, "drow")
                p.bank_acquire(t, bank)
                for k in range(KSS):
                    t.matmul(pgb[:, 0:SLAB], sq[:, k * D + i * 128:k * D + (i + 1) * 128],
                             sdec[:, k * SLAB:(k + 1) * SLAB],
                             start=(k == 0), stop=(k == KSS - 1))
                p.bank_acquire(t, bank2)
                if col0 == 0 and i == 0:
                    p.wait(t, "rdt")
                p.inc(t.matmul(pgb2[:, 0:SLAB], smo[:, i * 128:(i + 1) * 128],
                               srdt[:, col0:col0 + SLAB],
                               start=True, stop=True), "corr", 1)

            block.tensor(corr_mm)

            def y_v(v, i=i, bank=bank, pgb=pgb, bank2=bank2, pgb2=pgb2):
                p.wait(v, "corr")
                p.wait(v, "gbe")
                if i == 0:
                    p.wait(v, "xl")
                seg = slice(i * SLAB, (i + 1) * SLAB)
                v.tensor_add(st1[:, :], slg[:, seg], pgb[:, 0:SLAB])
                p.edge(v, v.tensor_mul(st2[:, :], pgb2[:, 0:SLAB], sgmbc[:, :]))
                v.tensor_mul(st1[:, :], st1[:, :], sglbc[:, :])
                p.edge(v, v.engine_nop())
                v.tensor_add(st1[:, :], st1[:, :], st2[:, :])
                p.edge(v, v.engine_nop())
                v.tensor_add(sy[:, seg], st1[:, :], sx[:, seg])
                p.edge(v, v.engine_nop())
                p.inc(v.tensor_add(sw_[:, seg], sy[:, seg], sx[:, seg]), "yv", 1)

            block.vector(y_v)
            p.bank_release(bank, "yv")
            p.bank_release(bank2, "yv")

        def ysq(s):
            p.wait(s, "yv")
            p.inc(s.activation(swb2[:, :], sy[:, :], AF.Square), "ysq", 1)

        block.scalar(ysq)

        def ystat(t):
            p.wait(t, "ysq")
            p.bank_acquire(t, "pa")
            for i in range(NDT):
                ins = t.matmul(pa[0:1, 0:SLAB], sones[:, :],
                               swb2[:, i * SLAB:(i + 1) * SLAB],
                               start=(i == 0), stop=(i == NDT - 1))
            p.inc(ins, "yst", 1)

        block.tensor(ystat)

        def yrow(v):
            p.wait(v, "yst")
            p.inc(v.tensor_scalar(srow[:, 0:SLAB], pa[0:1, 0:SLAB], 1.0 / D, EPS_RMS,
                                  ALU.mult, ALU.add), "yrw", 1)

        block.vector(yrow)
        p.bank_release("pa", "yrw")

        def ysqr(s):
            p.wait(s, "yrw")
            p.edge(s, s.activation(srow[:, 0:SLAB], srow[:, 0:SLAB], AF.Ln), "ssy")
            p.inc(s.activation(srow[:, 0:SLAB], srow[:, 0:SLAB], AF.Exp, scale=-0.5),
                  "yrc", 1)

        block.scalar(ysqr)

        def ybc(t):
            p.wait(t, "yrc")
            p.bank_acquire(t, "pb")
            p.inc(t.matmul(pb[:, 0:SLAB], sonesrf[:, :], srow[:, 0:SLAB],
                           start=True, stop=True), "ybc", 1)

        block.tensor(ybc)

        def yn_v(v):
            p.wait(v, "ybc")
            for i in range(NDT):
                ins = v.tensor_mul(syn[:, i * SLAB:(i + 1) * SLAB],
                                   sy[:, i * SLAB:(i + 1) * SLAB], pb[:, 0:SLAB])
            p.inc(ins, "ynv", 1)

        block.vector(yn_v)
        p.bank_release("pb", "ynv")

        for half in (range(2) if CST >= 4 else []):
            def ld_half(g, half=half, sl=sl):
                first = sl == 0 and half == 0
                if first:
                    p.wait(g, "qev")
                else:
                    p.wait(g, "umm")
                p.inc(g.dma_start(
                    out=swgt[:, 0:NDT * 2048].rearrange("p (d n) -> p d n", d=NDT),
                    in_=w_up.rearrange("(d p) n -> p d n", p=128)[:, :, half * 2048:(half + 1) * 2048]),
                    "whu", 16)
                if not first:
                    p.wait(g, "dmm")
                p.inc(g.dma_start(
                    out=swgt[:, NDT * 2048:32 * D].rearrange("p (c n) -> p c n", c=16),
                    in_=w_dn.rearrange("(c p) n -> p c n", p=128)[:, half * 16:(half + 1) * 16, :]),
                    "whd", 16)

            block.gpsimd(ld_half)

            for mg in range(4):
                CUB = ["pc", "pd", "pe", "pf"]
                CUP = {"pc": pc, "pd": pd, "pe": pe, "pf": pf}

                def up_mm(t, mg=mg):
                    if mg == 0:
                        p.wait(t, "whu")
                        p.wait(t, "ynv")
                    for mi in range(4):
                        m = mg * 4 + mi
                        p.bank_acquire(t, CUB[mi])
                        for d in range(NDT):
                            ins = t.matmul(CUP[CUB[mi]][:, 0:SLAB],
                                           swgt[:, d * 2048 + m * 128:d * 2048 + (m + 1) * 128],
                                           syn[:, d * SLAB:(d + 1) * SLAB],
                                           start=(d == 0), stop=(d == NDT - 1))
                    p.inc(ins, "umm", 1)

                block.tensor(up_mm)

                def up_ep(s, mg=mg, half=half):
                    p.wait(s, "umm")
                    p.wait(s, "uev2", p.val("uev2"))
                    for mi in range(4):
                        m = mg * 4 + mi
                        bcol = half * 16 + m
                        s.activation(sct4[:, mi * SLAB:(mi + 1) * SLAB],
                                     CUP[CUB[mi]][:, 0:SLAB], AF.Sigmoid,
                                     bias=sbup[:, bcol:bcol + 1])
                        ins = s.activation(scc4[:, mi * SLAB:(mi + 1) * SLAB],
                                           CUP[CUB[mi]][:, 0:SLAB], AF.Identity,
                                           bias=sbup[:, bcol:bcol + 1])
                    p.inc(ins, "uep", 1)

                block.scalar(up_ep)
                for mi in range(4):
                    p.bank_release(CUB[mi], "uep")

                def up_v(v, mg=mg):
                    p.wait(v, "uep")
                    for mi in range(4):
                        m = mg * 4 + mi
                        ins = v.tensor_mul(sh_[:, m * SLAB:(m + 1) * SLAB],
                                           sct4[:, mi * SLAB:(mi + 1) * SLAB],
                                           scc4[:, mi * SLAB:(mi + 1) * SLAB])
                    p.inc(ins, "uev2", 1)

                block.vector(up_v)

            for ig in range(4):
                def dn_mm(t, ig=ig, half=half):
                    if ig == 0:
                        p.wait(t, "uev2")
                        p.wait(t, "whd")
                    for ii in range(2):
                        i = ig * 2 + ii
                        bank = ["pa2", "pb2"][ii]
                        pgb = {"pa2": pa, "pb2": pb}[bank]
                        p.bank_acquire(t, bank)
                        if half == 0:
                            t.matmul(pgb[:, 0:SLAB], sbdn[0:1, i * 128:(i + 1) * 128],
                                     sonesr[:, :], start=True, stop=False)
                        for c in range(16):
                            ins = t.matmul(pgb[:, 0:SLAB],
                                           swgt[:, NDT * 2048 + c * D + i * 128:NDT * 2048 + c * D + (i + 1) * 128],
                                           sh_[:, c * SLAB:(c + 1) * SLAB],
                                           start=(half == 1 and c == 0), stop=(c == 15))
                    p.inc(ins, "dmm", 1)

                block.tensor(dn_mm)

                def dn_v(v, ig=ig):
                    p.wait(v, "dmm")
                    for ii in range(2):
                        i = ig * 2 + ii
                        pgb = {0: pa, 1: pb}[ii]
                        seg = slice(i * SLAB, (i + 1) * SLAB)
                        ins = v.tensor_add(sw_[:, seg], sw_[:, seg], pgb[:, 0:SLAB])
                    p.inc(ins, "dnv", 1)

                block.vector(dn_v)
                p.bank_release("pa2", "dnv")
                p.bank_release("pb2", "dnv")
                p.bank_release("pa", "dnv")
                p.bank_release("pb", "dnv")

        if CST < 5:
            continue
        def lnw(s):
            p.wait(s, "dnv")
            s.activation(swb[:, :], sw_[:, :], AF.Copy)
            p.inc(s.activation(swb2[:, :], sw_[:, :], AF.Square), "lnc", 1)

        block.scalar(lnw)

        def lnstat(t):
            p.wait(t, "lnc")
            p.bank_acquire(t, "pa")
            for i in range(NDT):
                t.matmul(pa[0:1, 0:SLAB], sones[:, :], swb[:, i * SLAB:(i + 1) * SLAB],
                         start=(i == 0), stop=(i == NDT - 1))
            p.bank_acquire(t, "pb")
            for i in range(NDT):
                ins = t.matmul(pb[0:1, 0:SLAB], sones[:, :], swb2[:, i * SLAB:(i + 1) * SLAB],
                               start=(i == 0), stop=(i == NDT - 1))
            p.inc(ins, "lns", 1)

        block.tensor(lnstat)

        def lnrow(v):
            p.wait(v, "lns")
            p.edge(v, v.tensor_scalar(srow[:, 0:SLAB], pa[0:1, 0:SLAB], 1.0 / D, None, ALU.mult))
            p.edge(v, v.tensor_scalar(srow[:, SLAB:2 * SLAB], pb[0:1, 0:SLAB], 1.0 / D, None, ALU.mult))
            p.edge(v, v.tensor_mul(srow[:, 2 * SLAB:3 * SLAB], srow[:, 0:SLAB], srow[:, 0:SLAB]))
            p.edge(v, v.tensor_sub(srow[:, SLAB:2 * SLAB], srow[:, SLAB:2 * SLAB],
                                   srow[:, 2 * SLAB:3 * SLAB]))
            p.inc(v.tensor_scalar(srow[:, SLAB:2 * SLAB], srow[:, SLAB:2 * SLAB], 1.0,
                                  EPS_LN, ALU.mult, ALU.add), "lnr", 1)

        block.vector(lnrow)
        p.bank_release("pa", "lnr")
        p.bank_release("pb", "lnr")

        def lnsq(s):
            p.wait(s, "lnr")
            p.edge(s, s.activation(srow[:, SLAB:2 * SLAB], srow[:, SLAB:2 * SLAB],
                                   AF.Ln), "ssy")
            p.inc(s.activation(srow[:, SLAB:2 * SLAB], srow[:, SLAB:2 * SLAB],
                               AF.Exp, scale=-0.5), "lnq", 1)

        block.scalar(lnsq)

        def lnrec(v):
            p.wait(v, "lnq")
            p.inc(v.tensor_mul(srow[:, 2 * SLAB:3 * SLAB], srow[:, 0:SLAB],
                               srow[:, SLAB:2 * SLAB]), "lnm", 1)

        block.vector(lnrec)

        def lnbc(t):
            p.wait(t, "lnm")
            p.bank_acquire(t, "pc")
            t.matmul(pc[:, 0:SLAB], sonesrf[:, :], srow[:, SLAB:2 * SLAB],
                     start=True, stop=True)
            p.bank_acquire(t, "pd")
            p.inc(t.matmul(pd[:, 0:SLAB], sonesrf[:, :], srow[:, 2 * SLAB:3 * SLAB],
                           start=True, stop=True), "lnw2", 1)

        block.tensor(lnbc)

        def x1_v(v):
            p.wait(v, "lnw2")
            for i in range(NDT):
                v.tensor_mul(sx1[:, i * SLAB:(i + 1) * SLAB],
                             sw_[:, i * SLAB:(i + 1) * SLAB], pc[:, 0:SLAB])
            p.edge(v, v.engine_nop())
            for i in range(NDT):
                ins = v.tensor_sub(sx1[:, i * SLAB:(i + 1) * SLAB],
                                   sx1[:, i * SLAB:(i + 1) * SLAB], pd[:, 0:SLAB])
            p.inc(ins, "lnb", 1)

        block.vector(x1_v)
        p.bank_release("pc", "lnb")
        p.bank_release("pd", "lnb")

        def x1_store(g, sl=sl):
            p.wait(g, "lnb")
            p.inc(g.dma_start(
                out=o_x1[:, sl * SLAB:(sl + 1) * SLAB].rearrange("(d p) n -> p d n", p=128),
                in_=sx1.rearrange("p (d n) -> p d n", d=NDT)), "x1st", 16)

        block.gpsimd(x1_store)

        def x1_fence(v):
            p.wait(v, "x1st")

        block.vector(x1_fence)

    def done(g):
        p.wait(g, "x1st")

    block.gpsimd(done)


def build_launch_C():
    p = Prog()
    nc = p.nc
    aps = dict(
        xT=nc.dram_tensor("xT", [D, SH], F32, kind="ExternalInput").ap(),
        longT=nc.dram_tensor("longT", [D, SH], F32, kind="ExternalInput").ap(),
        finAG=nc.dram_tensor("finAG", [128, 128], F32, kind="ExternalInput").ap(),
        gl_row=nc.dram_tensor("gl_row", [1, SH], F32, kind="ExternalInput").ap(),
        gm_row=nc.dram_tensor("gm_row", [1, SH], F32, kind="ExternalInput").ap(),
        readTF=nc.dram_tensor("readTF", [32, T], F32, kind="ExternalInput").ap(),
        owns_bc=nc.dram_tensor("owns_bc", [32, 64], F32, kind="ExternalInput").ap(),
        w_stout=nc.dram_tensor("w_stout", [KSS * D, D], F32, kind="ExternalInput").ap(),
        w_mo=nc.dram_tensor("w_mo", [32, D], F32, kind="ExternalInput").ap(),
        w_up=nc.dram_tensor("w_up", [D, 4 * D], F32, kind="ExternalInput").ap(),
        w_dn=nc.dram_tensor("w_dn", [4 * D, D], F32, kind="ExternalInput").ap(),
        b_up=nc.dram_tensor("b_up", [128, 32], F32, kind="ExternalInput").ap(),
        b_dn=nc.dram_tensor("b_dn", [1, D], F32, kind="ExternalInput").ap(),
        logit_row=nc.dram_tensor("logit_row", [1, KSS], F32, kind="ExternalInput").ap(),
        ecp_row=nc.dram_tensor("ecp_row", [1, 4], F32, kind="ExternalInput").ap(),
        mask44=nc.dram_tensor("mask44", [4, 4], F32, kind="ExternalInput").ap(),
        iotap1=nc.dram_tensor("iotap1", [1, SH], F32, kind="ExternalInput").ap(),
        o_x1=nc.dram_tensor("o_x1", [D, SH], F32, kind="ExternalOutput").ap(),
    )
    p.begin()
    emit_C(p, **aps)
    p.close()
    return p.nc



def emit_D(p, x1T, W1, b1, W2, b2, Wr, br, sel8F, eyeF, o_x2):
    nc = p.nc
    SLAB = 512
    TG = 1024
    NSL = TG // SLAB

    sx1b = p.sb("sx1b", [128, NDT * TG], BF16)
    sacc = p.sb("sacc", [128, NDT * TG], F32)
    swgt2 = [p.sb("swgtA", [128, 16 * D], BF16),
             p.sb("swgtB", [128, 16 * D], BF16)]
    swr = p.sb("swr", [128, NDT * 8], BF16)
    sbr = p.sb("sbr", [1, 8], BF16)
    sb1 = p.sb("sb1", [128, 8 * 32], F32)
    sb2 = p.sb("sb2", [8, D], BF16)
    ssel = p.sb("ssel", [8, 8 * 128], BF16)
    srtm = p.sb("srtm", [128, 8], F32)
    srr = p.sb("srr", [8, TG], BF16)
    srowm = p.sb("srowm", [128, 4], F32)
    sh_ = p.sb("sh_", [128, 16 * SLAB], BF16)
    shtmp = p.sb("shtmp", [128, SLAB], BF16)
    sxs = p.sb("sxs", [128, NDT * SLAB], F32)

    srow = p.sb("srow", [1, 3 * SLAB], F32)
    sones = p.sb("sones", [128, 1], BF16)
    sonesr = p.sb("sonesr", [1, SLAB], BF16)
    sonesrf = p.sb("sonesrf", [1, 128], F32)
    sonesrb = p.sb("sonesrb", [1, 128], BF16)
    shtmp4 = p.sb("shtmp4", [128, 4 * SLAB], BF16)
    seyeb = p.sb("seyeb", [128, 128], BF16)

    pa = p.psum("pa")
    pb = p.psum("pb")
    pc = p.psum("pc")
    pd = p.psum("pd")
    pe = p.psum("pe")
    pf = p.psum("pf")
    pbt8 = p.ent(nc.psum_tensor(p._part + "pbt8", [128, 1024], BF16))

    block = p.block

    def loads(g):
        p.inc(g.dma_start(out=seyeb[:, :], in_=eyeF), "wl", 16)
        p.inc(g.dma_start(out=swr.rearrange("p (d n) -> p d n", d=NDT),
                          in_=Wr.rearrange("(d p) n -> p d n", p=128)), "wl", 16)
        p.inc(g.dma_start(out=sbr[:, :], in_=br), "wl", 16)
        p.inc(g.dma_start(out=sb1[:, :], in_=b1), "wl", 16)
        p.inc(g.dma_start(out=sb2[:, :], in_=b2), "wl", 16)
        p.inc(g.dma_start(out=ssel[:, :], in_=sel8F), "wl", 16)

    block.gpsimd(loads)

    def init_v(v):
        v.memset(sones[:, :], 1.0)
        v.memset(sonesr[:, :], 1.0)
        v.memset(sonesrb[:, :], 1.0)
        p.inc(v.memset(sonesrf[:, :], 1.0), "iv", 1)

    block.vector(init_v)

    for tg in range(2):
        tcol = tg * TG

        def ld_x(g, tg=tg, tcol=tcol):
            if tg > 0:
                p.wait(g, "x2st")
            p.inc(g.dma_start(out=sx1b.rearrange("p (d n) -> p d n", d=NDT),
                              in_=x1T[:, tcol:tcol + TG].rearrange("(d p) n -> p d n", p=128)),
                  "xg", 16)

        block.gpsimd(ld_x)

        for tt in range(TG // 128):
            def r_mm(t, tt=tt, tg=tg):
                if tt == 0:
                    p.wait(t, "xg")
                    p.wait(t, "wl")
                    p.wait(t, "iv")
                p.bank_acquire(t, "pa")
                for d in range(NDT):
                    t.matmul(pa[:, 0:8], sx1b[:, d * TG + tt * 128:d * TG + tt * 128 + 128],
                             swr[:, d * 8:(d + 1) * 8],
                             start=(d == 0), stop=False)
                p.inc(t.matmul(pa[:, 0:8], sonesrb[:, :], sbr[:, :],
                               start=False, stop=True), "rmm", 1)

            block.tensor(r_mm)

            def r_v(v, tt=tt):
                p.wait(v, "rmm")
                p.inc(v.reduce_max(srowm[:, 0:1], pa[:, 0:8], axis=AX.X), "vsy", 1)
                p.wait(v, "vsy")
                v.tensor_scalar(srtm[:, :], pa[:, 0:8], srowm[:, 0:1], None, ALU.is_ge)
                p.edge(v, v.engine_nop())
                v.scalar_tensor_tensor(srtm[:, :], srtm[:, :], -1e9, pa[:, 0:8],
                                       ALU.mult, ALU.add)
                p.edge(v, v.engine_nop())
                p.inc(v.reduce_max(srowm[:, 1:2], srtm[:, :], axis=AX.X), "vsy", 1)
                p.wait(v, "vsy")
                v.tensor_scalar(srtm[:, :], pa[:, 0:8], srowm[:, 1:2], None, ALU.is_ge)
                p.edge(v, v.engine_nop())
                v.tensor_mul(srtm[:, :], srtm[:, :], pa[:, 0:8])
                p.edge(v, v.engine_nop())
                p.inc(v.reduce_max(srowm[:, 2:3], srtm[:, :], axis=AX.X), "vsy", 1)
                p.wait(v, "vsy")
                p.inc(v.tensor_scalar(srowm[:, 2:3], srowm[:, 2:3], -1.0, None, ALU.mult),
                      "rv1", 1)

            block.vector(r_v)

            def r_s(s, tt=tt):
                p.wait(s, "rv1")
                p.inc(s.activation(srtm[:, :], srtm[:, :], AF.Exp, bias=srowm[:, 2:3]),
                      "rexp", 1)

            block.scalar(r_s)

            def r_v2(v, tt=tt):
                p.wait(v, "rexp")
                p.inc(v.reduce_sum(srowm[:, 3:4], srtm[:, :], axis=AX.X), "vsy", 1)
                p.wait(v, "vsy")
                p.inc(v.reciprocal(srowm[:, 3:4], srowm[:, 3:4]), "vsy", 1)
                p.wait(v, "vsy")
                p.inc(v.tensor_scalar(srtm[:, :], srtm[:, :], srowm[:, 3:4], None,
                                      ALU.mult), "rv2", 1)

            block.vector(r_v2)
            p.bank_release("pa", "rv2")

            def r_cp(v, tt=tt):
                p.wait(v, "rv2")
                p.inc(v.tensor_copy(shtmp[:, 0:8], srtm[:, :]), "rcp", 1)

            block.vector(r_cp)

            def r_tr2(t, tt=tt):
                p.wait(t, "rcp")
                p.bank_acquire(t, "pbt")
                p.inc(t.transpose(pbt8[0:8, 0:128], shtmp[:, 0:8], seyeb[:, :]), "rtr", 1)

            block.tensor(r_tr2)

            def r_ev(v, tt=tt):
                p.wait(v, "rtr")
                p.inc(v.tensor_copy(srr[:, tt * 128:(tt + 1) * 128], pbt8[0:8, 0:128]),
                      "rev", 1)

            block.vector(r_ev)
            p.bank_release("pbt", "rev")

        for sl in range(NSL):
            for i in range(NDT):
                bank = ["pc", "pd"][(sl * NDT + i) % 2]
                pgb = {"pc": pc, "pd": pd}[bank]

                def b2_mm(t, sl=sl, i=i, bank=bank, pgb=pgb):
                    if sl == 0 and i == 0:
                        p.wait(t, "rev")
                    p.bank_acquire(t, bank)
                    p.inc(t.matmul(pgb[:, 0:SLAB], sb2[:, i * 128:(i + 1) * 128],
                                   srr[:, sl * SLAB:(sl + 1) * SLAB],
                                   start=True, stop=True), "b2m", 1)

                block.tensor(b2_mm)

                def b2_ev(v, sl=sl, i=i, bank=bank, pgb=pgb):
                    p.wait(v, "b2m")
                    p.inc(v.tensor_copy(sacc[:, sl * NDT * SLAB + i * SLAB:sl * NDT * SLAB + (i + 1) * SLAB],
                                        pgb[:, 0:SLAB]), "b2e", 1)

                block.vector(b2_ev)
                p.bank_release(bank, "b2e")

        dnv_ckpt = {}
        for idx in range(32):
            e, half, q = idx // 4, (idx // 2) % 2, idx % 2
            hid0 = half * 2048 + q * 1024
            c0 = half * 16 + q * 8
            swgt = swgt2[idx % 2]
            dnv_ckpt[idx] = p.val("dnv")

            def ld_w(g, e=e, hid0=hid0, c0=c0, idx=idx, swgt=swgt):
                if idx >= 2:
                    p.wait(g, "dnv", dnv_ckpt[idx - 1])
                else:
                    p.wait(g, "b2e")
                p.inc(g.dma_start(
                    out=swgt[:, 0:NDT * 1024].rearrange("p (d n) -> p d n", d=NDT),
                    in_=W1[e].rearrange("(d p) n -> p d n", p=128)[:, :, hid0:hid0 + 1024]),
                    "wh", 16)
                p.inc(g.dma_start(
                    out=swgt[:, NDT * 1024:16 * D].rearrange("p (c n) -> p c n", c=8),
                    in_=W2[e].rearrange("(c p) n -> p c n", p=128)[:, c0:c0 + 8, :]),
                    "wh", 16)

            block.gpsimd(ld_w)

            for sl in range(NSL):
                def rbc_mm(t, e=e, sl=sl, idx=idx):
                    if sl == 0:
                        p.wait(t, "wh")
                    p.bank_acquire(t, "pe")
                    p.inc(t.matmul(pe[:, 0:SLAB], ssel[:, e * 128:(e + 1) * 128],
                                   srr[:, sl * SLAB:(sl + 1) * SLAB],
                                   start=True, stop=True), "rbc", 1)

                block.tensor(rbc_mm)

                UPB = ["pc", "pd", "pa", "pb"]
                UPP = {"pc": pc, "pd": pd, "pa": pa, "pb": pb}
                for mg in range(2):
                    def up_mm(t, mg=mg, sl=sl, swgt=swgt):
                        for mi in range(4):
                            m = mg * 4 + mi
                            bank = UPB[mi]
                            pgb = UPP[bank]
                            p.bank_acquire(t, bank)
                            for d in range(NDT):
                                ins = t.matmul(pgb[:, 0:SLAB],
                                               swgt[:, d * 1024 + m * 128:d * 1024 + (m + 1) * 128],
                                               sx1b[:, d * TG + sl * SLAB:d * TG + (sl + 1) * SLAB],
                                               start=(d == 0), stop=(d == NDT - 1))
                        p.inc(ins, "umm", 1)

                    block.tensor(up_mm)

                    def up_ep(s, mg=mg, e=e, half=half, q=q):
                        p.wait(s, "umm")
                        p.wait(s, "usc", p.val("usc"))
                        for mi in range(4):
                            m = mg * 4 + mi
                            bcol = e * 32 + half * 16 + q * 8 + m
                            ins = s.activation(shtmp4[:, mi * SLAB:(mi + 1) * SLAB],
                                               UPP[UPB[mi]][:, 0:SLAB], AF.Relu,
                                               bias=sb1[:, bcol:bcol + 1])
                        p.inc(ins, "uep", 1)

                    block.scalar(up_ep)
                    for mi in range(4):
                        p.bank_release(UPB[mi], "uep")

                    def up_scale(v, mg=mg):
                        p.wait(v, "uep")
                        for mi in range(4):
                            m = mg * 4 + mi
                            ins = v.tensor_mul(sh_[:, m * SLAB:(m + 1) * SLAB],
                                               shtmp4[:, mi * SLAB:(mi + 1) * SLAB],
                                               pe[:, 0:SLAB])
                        p.inc(ins, "usc", 1)

                    block.vector(up_scale)

                p.bank_release("pe", "usc")

                for ig in range(4):
                    def dn_mm(t, ig=ig, swgt=swgt):
                        if ig == 0:
                            p.wait(t, "usc")
                        for ii in range(2):
                            i = ig * 2 + ii
                            bank = ["pe2", "pf"][ii]
                            pgb = {"pe2": pe, "pf": pf}[bank]
                            p.bank_acquire(t, bank)
                            for c in range(8):
                                ins = t.matmul(pgb[:, 0:SLAB],
                                               swgt[:, NDT * 1024 + c * D + i * 128:NDT * 1024 + c * D + (i + 1) * 128],
                                               sh_[:, c * SLAB:(c + 1) * SLAB],
                                               start=(c == 0), stop=(c == 7))
                        p.inc(ins, "dmm", 1)

                    block.tensor(dn_mm)

                    def dn_v(v, ig=ig, sl=sl):
                        p.wait(v, "dmm")
                        for ii in range(2):
                            i = ig * 2 + ii
                            pgb = {0: pe, 1: pf}[ii]
                            seg = slice(sl * NDT * SLAB + i * SLAB, sl * NDT * SLAB + (i + 1) * SLAB)
                            ins = v.tensor_add(sacc[:, seg], sacc[:, seg], pgb[:, 0:SLAB])
                        p.inc(ins, "dnv", 1)

                    block.vector(dn_v)
                    p.bank_release("pe2", "dnv")
                    p.bank_release("pf", "dnv")
                    p.bank_release("pe", "dnv")

        for sl in range(NSL):
            col0 = tcol + sl * SLAB

            def ld_xs(g, sl=sl, col0=col0):
                p.wait(g, "lnb", p.val("lnb"))
                p.wait(g, "x2st", p.val("x2st"))
                p.inc(g.dma_start(out=sxs.rearrange("p (d n) -> p d n", d=NDT),
                                  in_=x1T[:, col0:col0 + SLAB].rearrange("(d p) n -> p d n", p=128)),
                      "xsl", 16)

            block.sync(ld_xs)

            def vsum(v, sl=sl):
                p.wait(v, "dnv")
                p.wait(v, "xsl")
                for i in range(NDT):
                    seg = slice(sl * NDT * SLAB + i * SLAB, sl * NDT * SLAB + (i + 1) * SLAB)
                    ins = v.tensor_add(sxs[:, i * SLAB:(i + 1) * SLAB],
                                       sxs[:, i * SLAB:(i + 1) * SLAB], sacc[:, seg])
                p.inc(ins, "vsum", 1)

            block.vector(vsum)

            def lnw(s):
                p.wait(s, "vsum")
                swb = sh_[:, 0:NDT * SLAB]
                swb2 = sh_[:, NDT * SLAB:2 * NDT * SLAB]
                s.activation(swb, sxs[:, :], AF.Copy)
                p.inc(s.activation(swb2, sxs[:, :], AF.Square), "lnc", 1)

            block.scalar(lnw)

            def lnstat(t):
                p.wait(t, "lnc")
                swb = sh_[:, 0:NDT * SLAB]
                swb2 = sh_[:, NDT * SLAB:2 * NDT * SLAB]
                p.bank_acquire(t, "pa")
                for i in range(NDT):
                    t.matmul(pa[0:1, 0:SLAB], sones[:, :], swb[:, i * SLAB:(i + 1) * SLAB],
                             start=(i == 0), stop=(i == NDT - 1))
                p.bank_acquire(t, "pb")
                for i in range(NDT):
                    ins = t.matmul(pb[0:1, 0:SLAB], sones[:, :], swb2[:, i * SLAB:(i + 1) * SLAB],
                                   start=(i == 0), stop=(i == NDT - 1))
                p.inc(ins, "lns", 1)

            block.tensor(lnstat)

            def lnrow(v):
                p.wait(v, "lns")
                p.edge(v, v.tensor_scalar(srow[:, 0:SLAB], pa[0:1, 0:SLAB], 1.0 / D, None, ALU.mult))
                p.edge(v, v.tensor_scalar(srow[:, SLAB:2 * SLAB], pb[0:1, 0:SLAB], 1.0 / D, None, ALU.mult))
                p.edge(v, v.tensor_mul(srow[:, 2 * SLAB:3 * SLAB], srow[:, 0:SLAB], srow[:, 0:SLAB]))
                p.edge(v, v.tensor_sub(srow[:, SLAB:2 * SLAB], srow[:, SLAB:2 * SLAB],
                                       srow[:, 2 * SLAB:3 * SLAB]))
                p.inc(v.tensor_scalar(srow[:, SLAB:2 * SLAB], srow[:, SLAB:2 * SLAB], 1.0,
                                      EPS_LN, ALU.mult, ALU.add), "lnr", 1)

            block.vector(lnrow)
            p.bank_release("pa", "lnr")
            p.bank_release("pb", "lnr")

            def lnsq(s):
                p.wait(s, "lnr")
                p.edge(s, s.activation(srow[:, SLAB:2 * SLAB], srow[:, SLAB:2 * SLAB],
                                       AF.Ln), "ssy")
                p.inc(s.activation(srow[:, SLAB:2 * SLAB], srow[:, SLAB:2 * SLAB],
                                   AF.Exp, scale=-0.5), "lnq", 1)

            block.scalar(lnsq)

            def lnrec(v):
                p.wait(v, "lnq")
                p.inc(v.tensor_mul(srow[:, 2 * SLAB:3 * SLAB], srow[:, 0:SLAB],
                                   srow[:, SLAB:2 * SLAB]), "lnm", 1)

            block.vector(lnrec)

            def lnbc(t):
                p.wait(t, "lnm")
                p.bank_acquire(t, "pc")
                t.matmul(pc[:, 0:SLAB], sonesrf[:, :], srow[:, SLAB:2 * SLAB],
                         start=True, stop=True)
                p.bank_acquire(t, "pd")
                p.inc(t.matmul(pd[:, 0:SLAB], sonesrf[:, :], srow[:, 2 * SLAB:3 * SLAB],
                               start=True, stop=True), "lnw2", 1)

            block.tensor(lnbc)

            def x2_v(v):
                p.wait(v, "lnw2")
                for i in range(NDT):
                    v.tensor_mul(sxs[:, i * SLAB:(i + 1) * SLAB],
                                 sxs[:, i * SLAB:(i + 1) * SLAB], pc[:, 0:SLAB])
                p.edge(v, v.engine_nop())
                for i in range(NDT):
                    ins = v.tensor_sub(sxs[:, i * SLAB:(i + 1) * SLAB],
                                       sxs[:, i * SLAB:(i + 1) * SLAB], pd[:, 0:SLAB])
                p.inc(ins, "lnb", 1)

            block.vector(x2_v)
            p.bank_release("pc", "lnb")
            p.bank_release("pd", "lnb")

            def x2_store(g, col0=col0):
                p.wait(g, "lnb")
                p.inc(g.dma_start(
                    out=o_x2[:, col0:col0 + SLAB].rearrange("(d p) n -> p d n", p=128),
                    in_=sxs.rearrange("p (d n) -> p d n", d=NDT)), "x2st", 16)

            block.gpsimd(x2_store)

    def done(g):
        p.wait(g, "x2st")

    block.gpsimd(done)


def build_launch_D():
    p = Prog()
    nc = p.nc
    aps = dict(
        x1T=nc.dram_tensor("x1T", [D, SH], F32, kind="ExternalInput").ap(),
        W1=nc.dram_tensor("W1", [8, D, 4 * D], F32, kind="ExternalInput").ap(),
        b1=nc.dram_tensor("b1", [128, 8 * 32], F32, kind="ExternalInput").ap(),
        W2=nc.dram_tensor("W2", [8, 4 * D, D], F32, kind="ExternalInput").ap(),
        b2=nc.dram_tensor("b2", [8, D], F32, kind="ExternalInput").ap(),
        Wr=nc.dram_tensor("Wr", [D, 8], F32, kind="ExternalInput").ap(),
        br=nc.dram_tensor("br", [1, 8], F32, kind="ExternalInput").ap(),
        sel8F=nc.dram_tensor("sel8F", [8, 8 * 128], F32, kind="ExternalInput").ap(),
        eyeF=nc.dram_tensor("eyeF", [128, 128], F32, kind="ExternalInput").ap(),
        o_x2=nc.dram_tensor("o_x2", [D, SH], F32, kind="ExternalOutput").ap(),
    )
    p.begin()
    emit_D(p, **aps)
    p.close()
    return p.nc



SCALE3 = float(np.sqrt(3.0 / 64.0))


def emit_E(p, x2T, wkv, wqT, woT, o_x3):
    nc = p.nc
    SLAB = 512
    NSL = SH // SLAB

    e_in = nc.dram_tensor(p._part + "e_in", [64, 64], F32)
    e_out = nc.dram_tensor(p._part + "e_out", [64, 64], F32)

    sx2b = p.sb("sx2b", [128, NDT * SH], BF16)
    swkv = p.sb("swkv", [128, NDT * 128], BF16)
    swq = p.sb("swq", [128, NDT * 64], BF16)
    swo = p.sb("swo", [64, D], BF16)
    skv = p.sb("skv", [128, 128], BF16)
    sqt = p.sb("sqt", [64, SH], BF16)
    sctx = p.sb("sctx", [64, SH], BF16)
    sE = p.sb("sE", [64, 64], F32)
    sEb = p.sb("sEb", [64, 64], BF16)
    sxs = p.sb("sxs", [128, NDT * SLAB], F32)
    swb = p.sb("swb", [128, NDT * SLAB], BF16)
    swb2 = p.sb("swb2", [128, NDT * SLAB], BF16)
    srow = p.sb("srow", [1, 3 * SLAB], F32)
    sones = p.sb("sones", [128, 1], BF16)
    sonesrf = p.sb("sonesrf", [1, 128], F32)
    sx3o = p.sb("sx3o", [128, NDT * SLAB], F32)

    pa = p.psum("pa")
    pb = p.psum("pb")
    pc = p.psum("pc")
    pd = p.psum("pd")
    pe_ = p.psum("pe_")
    pf = p.psum("pf")

    block = p.block

    def loads(g):
        p.inc(g.dma_start(out=sx2b.rearrange("p (d n) -> p d n", d=NDT),
                          in_=x2T.rearrange("(d p) n -> p d n", p=128)), "wl", 16)
        p.inc(g.dma_start(out=swkv.rearrange("p (d n) -> p d n", d=NDT),
                          in_=wkv.rearrange("(d p) n -> p d n", p=128)), "wl", 16)
        p.inc(g.dma_start(out=swq.rearrange("p (d n) -> p d n", d=NDT),
                          in_=wqT.rearrange("(d p) n -> p d n", p=128)), "wl", 16)
        p.inc(g.dma_start(out=swo[:, :], in_=woT), "wl", 16)

    block.gpsimd(loads)

    def init_v(v):
        v.memset(sones[:, :], 1.0)
        p.inc(v.memset(sonesrf[:, :], 1.0), "iv", 1)

    block.vector(init_v)

    for sl in range(NSL):
        def q_mm(t, sl=sl):
            if sl == 0:
                p.wait(t, "wl")
            p.bank_acquire(t, ["pa", "pb"][sl % 2])
            ps = [pa, pb][sl % 2]
            for d in range(NDT):
                ins = t.matmul(ps[0:64, 0:SLAB], swq[:, d * 64:(d + 1) * 64],
                               sx2b[:, d * SH + sl * SLAB:d * SH + (sl + 1) * SLAB],
                               start=(d == 0), stop=(d == NDT - 1))
            p.inc(ins, "qmm", 1)

        block.tensor(q_mm)

        def q_ev(v, sl=sl):
            p.wait(v, "qmm")
            ps = [pa, pb][sl % 2]
            p.inc(v.tensor_copy(sqt[:, sl * SLAB:(sl + 1) * SLAB], ps[0:64, 0:SLAB]),
                  "qev", 1)

        block.vector(q_ev)
        p.bank_release(["pa", "pb"][sl % 2], "qev")

    for tt in range(SH // 128):
        def kv_mm(t, tt=tt):
            p.bank_acquire(t, ["pc", "pd"][tt % 2])
            ps = [pc, pd][tt % 2]
            for d in range(NDT):
                ins = t.matmul(ps[:, 0:128], sx2b[:, d * SH + tt * 128:d * SH + tt * 128 + 128],
                               swkv[:, d * 128:(d + 1) * 128],
                               start=(d == 0), stop=(d == NDT - 1))
            p.inc(ins, "kvm", 1)

        block.tensor(kv_mm)

        def kv_ev(v, tt=tt):
            p.wait(v, "kvm")
            p.inc(v.tensor_copy(skv[:, :], [pc, pd][tt % 2][:, 0:128]), "kve", 1)

        block.vector(kv_ev)
        p.bank_release(["pc", "pd"][tt % 2], "kve")

        def e_mm(t, tt=tt):
            p.wait(t, "kve")
            if tt == 0:
                p.bank_acquire(t, "pe_")
            p.inc(t.matmul(pe_[0:64, 0:64], skv[:, 0:64], skv[:, 64:128],
                           start=(tt == 0), stop=(tt == SH // 128 - 1)), "emm", 1)

        block.tensor(e_mm)

    def e_ev(v):
        p.wait(v, "emm")
        p.inc(v.tensor_copy(sE[:, :], pe_[0:64, 0:64]), "eev", 1)

    block.vector(e_ev)
    p.bank_release("pe_", "eev")

    def e_coll(g):
        p.wait(g, "eev")
        p.inc(g.dma_start(out=e_in.ap(), in_=sE[:, :]), "ecp", 16)
        p.wait(g, "ecp")
        p.inc(g.collective_compute(
            "AllReduce", mybir.AluOpType.add,
            replica_groups=[[0, 1, 2, 3], [4, 5, 6, 7]],
            ins=[e_in.ap().opt()], outs=[e_out.ap().opt()]), "cc", 1)
        p.wait(g, "cc")
        p.inc(g.dma_start(out=sEb[:, :], in_=e_out.ap()), "erd", 16)

    block.gpsimd(e_coll)

    for sl in range(NSL):
        col0 = sl * SLAB

        def ctx_mm(t, sl=sl):
            if sl == 0:
                p.wait(t, "erd")
            p.bank_acquire(t, "pa")
            p.inc(t.matmul(pa[0:64, 0:SLAB], sEb[:, :], sqt[:, sl * SLAB:(sl + 1) * SLAB],
                           start=True, stop=True), "cmm", 1)

        block.tensor(ctx_mm)

        def ctx_ev(s, sl=sl):
            p.wait(s, "cmm")
            p.inc(s.activation(sctx[:, sl * SLAB:(sl + 1) * SLAB], pa[0:64, 0:SLAB],
                               AF.Copy, scale=SCALE3), "cev", 1)

        block.scalar(ctx_ev)
        p.bank_release("pa", "cev")

        def ld_xs(g, sl=sl, col0=col0):
            p.wait(g, "lnb", p.val("lnb"))
            p.inc(g.dma_start(out=sxs.rearrange("p (d n) -> p d n", d=NDT),
                              in_=x2T[:, col0:col0 + SLAB].rearrange("(d p) n -> p d n", p=128)),
                  "xsl", 16)

        block.sync(ld_xs)

        for i in range(NDT):
            bank = ["pc", "pd"][i % 2]
            pgb = {"pc": pc, "pd": pd}[bank]

            def o_mm(t, i=i, sl=sl, bank=bank, pgb=pgb):
                if i == 0:
                    p.wait(t, "cev")
                p.bank_acquire(t, bank)
                p.inc(t.matmul(pgb[:, 0:SLAB], swo[:, i * 128:(i + 1) * 128],
                               sctx[:, sl * SLAB:(sl + 1) * SLAB],
                               start=True, stop=True), "omm", 1)

            block.tensor(o_mm)

            def o_v(v, i=i, bank=bank, pgb=pgb):
                p.wait(v, "omm")
                if i == 0:
                    p.wait(v, "xsl")
                p.inc(v.tensor_add(sxs[:, i * SLAB:(i + 1) * SLAB],
                                   sxs[:, i * SLAB:(i + 1) * SLAB], pgb[:, 0:SLAB]),
                      "ov", 1)

            block.vector(o_v)
            p.bank_release(bank, "ov")

        def lnw(s):
            p.wait(s, "ov")
            s.activation(swb[:, :], sxs[:, :], AF.Copy)
            p.inc(s.activation(swb2[:, :], sxs[:, :], AF.Square), "lnc", 1)

        block.scalar(lnw)

        def lnstat(t):
            p.wait(t, "lnc")
            p.bank_acquire(t, "pa")
            for i in range(NDT):
                t.matmul(pa[0:1, 0:SLAB], sones[:, :], swb[:, i * SLAB:(i + 1) * SLAB],
                         start=(i == 0), stop=(i == NDT - 1))
            p.bank_acquire(t, "pb")
            for i in range(NDT):
                ins = t.matmul(pb[0:1, 0:SLAB], sones[:, :], swb2[:, i * SLAB:(i + 1) * SLAB],
                               start=(i == 0), stop=(i == NDT - 1))
            p.inc(ins, "lns", 1)

        block.tensor(lnstat)

        def lnrow(v):
            p.wait(v, "lns")
            p.edge(v, v.tensor_scalar(srow[:, 0:SLAB], pa[0:1, 0:SLAB], 1.0 / D, None, ALU.mult))
            p.edge(v, v.tensor_scalar(srow[:, SLAB:2 * SLAB], pb[0:1, 0:SLAB], 1.0 / D, None, ALU.mult))
            p.edge(v, v.tensor_mul(srow[:, 2 * SLAB:3 * SLAB], srow[:, 0:SLAB], srow[:, 0:SLAB]))
            p.edge(v, v.tensor_sub(srow[:, SLAB:2 * SLAB], srow[:, SLAB:2 * SLAB],
                                   srow[:, 2 * SLAB:3 * SLAB]))
            p.inc(v.tensor_scalar(srow[:, SLAB:2 * SLAB], srow[:, SLAB:2 * SLAB], 1.0,
                                  EPS_LN, ALU.mult, ALU.add), "lnr", 1)

        block.vector(lnrow)
        p.bank_release("pa", "lnr")
        p.bank_release("pb", "lnr")

        def lnsq(s):
            p.wait(s, "lnr")
            p.edge(s, s.activation(srow[:, SLAB:2 * SLAB], srow[:, SLAB:2 * SLAB],
                                   AF.Ln), "ssy")
            p.inc(s.activation(srow[:, SLAB:2 * SLAB], srow[:, SLAB:2 * SLAB],
                               AF.Exp, scale=-0.5), "lnq", 1)

        block.scalar(lnsq)

        def lnrec(v):
            p.wait(v, "lnq")
            p.inc(v.tensor_mul(srow[:, 2 * SLAB:3 * SLAB], srow[:, 0:SLAB],
                               srow[:, SLAB:2 * SLAB]), "lnm", 1)

        block.vector(lnrec)

        def lnbc(t):
            p.wait(t, "lnm")
            p.bank_acquire(t, "pc")
            t.matmul(pc[:, 0:SLAB], sonesrf[:, :], srow[:, SLAB:2 * SLAB],
                     start=True, stop=True)
            p.bank_acquire(t, "pd")
            p.inc(t.matmul(pd[:, 0:SLAB], sonesrf[:, :], srow[:, 2 * SLAB:3 * SLAB],
                           start=True, stop=True), "lnw2", 1)

        block.tensor(lnbc)

        def x3_v(v):
            p.wait(v, "lnw2")
            p.wait(v, "x3st")
            for i in range(NDT):
                v.tensor_mul(sx3o[:, i * SLAB:(i + 1) * SLAB],
                             sxs[:, i * SLAB:(i + 1) * SLAB], pc[:, 0:SLAB])
            p.edge(v, v.engine_nop())
            for i in range(NDT):
                ins = v.tensor_sub(sx3o[:, i * SLAB:(i + 1) * SLAB],
                                   sx3o[:, i * SLAB:(i + 1) * SLAB], pd[:, 0:SLAB])
            p.inc(ins, "lnb", 1)

        block.vector(x3_v)
        p.bank_release("pc", "lnb")
        p.bank_release("pd", "lnb")

        def x3_store(g, col0=col0):
            p.wait(g, "lnb")
            p.inc(g.dma_start(
                out=o_x3[:, col0:col0 + SLAB].rearrange("(d p) n -> p d n", p=128),
                in_=sx3o.rearrange("p (d n) -> p d n", d=NDT)), "x3st", 16)

        block.gpsimd(x3_store)

    def done(g):
        p.wait(g, "x3st")

    block.gpsimd(done)


def build_launch_E():
    p = Prog()
    nc = p.nc
    aps = dict(
        x2T=nc.dram_tensor("x2T", [D, SH], F32, kind="ExternalInput").ap(),
        wkv=nc.dram_tensor("wkv", [D, 128], F32, kind="ExternalInput").ap(),
        wqT=nc.dram_tensor("wqT", [D, 64], F32, kind="ExternalInput").ap(),
        woT=nc.dram_tensor("woT", [64, D], F32, kind="ExternalInput").ap(),
        o_x3=nc.dram_tensor("o_x3", [D, SH], F32, kind="ExternalOutput").ap(),
    )
    p.begin()
    emit_E(p, **aps)
    p.close()
    return p.nc



def emit_G(p, x3T, wrT, rbias, GU, DN, shGU, shDN, sel16F, eyeF, o_x4):
    nc = p.nc
    SLAB = 512
    NSL = SH // SLAB
    import os as _os
    GDBG = _os.environ.get("GDBG", "") == "1"
    if GDBG:
        o_cmb = nc.dram_tensor("o_cmb", [16, SH], F32, kind="ExternalOutput").ap()
        o_acc = nc.dram_tensor("o_acc", [128, NDT * SH], F32, kind="ExternalOutput").ap()

    sx3b = p.sb("sx3b", [128, NDT * SH], BF16)
    sacc = p.sb("sacc", [128, NDT * SH], F32)
    swr = p.sb("swr", [128, NDT * 16], BF16)
    srb = p.sb("srb", [1, 16], BF16)
    ssel = p.sb("ssel", [16, 16 * 128], BF16)
    seyeb = p.sb("seyeb", [128, 128], BF16)
    sgu2 = [p.sb("sguA", [128, NDT * 1024], BF16),
            p.sb("sguB", [128, NDT * 1024], BF16)]
    sdn2 = [p.sb("sdnA", [128, 4 * D], BF16),
            p.sb("sdnB", [128, 4 * D], BF16)]
    scmb = p.sb("scmb", [128, 16], F32)
    sl2f = p.sb("sl2f", [128, 16], F32)
    scmbb = p.sb("scmbb", [128, 16], BF16)
    scr = p.sb("scr", [16, SH], BF16)
    srowm = p.sb("srowm", [128, 6], F32)
    sh_ = p.sb("sh_", [128, 4 * SLAB], BF16)
    shtmp = p.sb("shtmp", [128, 2 * SLAB], BF16)
    shtmp2 = p.sb("shtmp2", [128, 2 * SLAB], BF16)
    sxs = p.sb("sxs", [128, NDT * SLAB], F32)
    swb = p.sb("swb", [128, NDT * SLAB], BF16)
    swb2 = p.sb("swb2", [128, NDT * SLAB], BF16)
    srow = p.sb("srow", [1, 3 * SLAB], F32)
    sones = p.sb("sones", [128, 1], BF16)
    sonesrb = p.sb("sonesrb", [1, 128], BF16)
    sonesrf = p.sb("sonesrf", [1, 128], F32)

    pa = p.psum("pa")
    pb = p.psum("pb")
    pc = p.psum("pc")
    pd = p.psum("pd")
    pe_ = p.psum("pe_")
    pe_2 = p.psum("pe_2")
    pf = p.psum("pf")
    pbt8 = p.ent(nc.psum_tensor(p._part + "pbt8", [128, 1024], BF16))

    block = p.block

    def loads(g):
        p.inc(g.dma_start(out=sx3b.rearrange("p (d n) -> p d n", d=NDT),
                          in_=x3T.rearrange("(d p) n -> p d n", p=128)), "wl", 16)
        p.inc(g.dma_start(out=swr.rearrange("p (d n) -> p d n", d=NDT),
                          in_=wrT.rearrange("(d p) n -> p d n", p=128)), "wl", 16)
        p.inc(g.dma_start(out=srb[:, :], in_=rbias), "wl", 16)
        p.inc(g.dma_start(out=ssel[:, :], in_=sel16F), "wl", 16)
        p.inc(g.dma_start(out=seyeb[:, :], in_=eyeF), "wl", 16)

    block.gpsimd(loads)

    def init_v(v):
        v.memset(sones[:, :], 1.0)
        v.memset(sonesrb[:, :], 1.0)
        p.inc(v.memset(sonesrf[:, :], 1.0), "iv", 1)

    block.vector(init_v)

    for tt in range(SH // 128):
        def r_mm(t, tt=tt):
            if tt == 0:
                p.wait(t, "wl")
                p.wait(t, "iv")
            p.bank_acquire(t, "pa")
            for d in range(NDT):
                t.matmul(pa[:, 0:16], sx3b[:, d * SH + tt * 128:d * SH + tt * 128 + 128],
                         swr[:, d * 16:(d + 1) * 16],
                         start=(d == 0), stop=False)
            p.inc(t.matmul(pa[:, 0:16], sonesrb[:, :], srb[:, :],
                           start=False, stop=True), "rmm", 1)

        block.tensor(r_mm)

        def r_v(v, tt=tt):
            p.wait(v, "rmm")
            p.inc(v.reduce_max(srowm[:, 0:1], pa[:, 0:16], axis=AX.X), "vsy", 1)
            p.wait(v, "vsy")
            p.edge(v, v.tensor_scalar(scmb[:, :], pa[:, 0:16], srowm[:, 0:1], None, ALU.is_ge))
            p.edge(v, v.scalar_tensor_tensor(sl2f[:, :], scmb[:, :], -1e9, pa[:, 0:16],
                                             ALU.mult, ALU.add))
            p.inc(v.reduce_max(srowm[:, 1:2], sl2f[:, :], axis=AX.X), "vsy", 1)
            p.wait(v, "vsy")
            p.inc(v.tensor_sub(srowm[:, 2:3], srowm[:, 0:1], srowm[:, 1:2]), "rv1", 1)

        block.vector(r_v)

        def r_s(s, tt=tt):
            p.wait(s, "rv1")
            p.inc(s.activation(srowm[:, 3:4], srowm[:, 2:3], AF.Sigmoid), "rsg", 1)

        block.scalar(r_s)

        def r_v2(v, tt=tt):
            p.wait(v, "rsg")
            p.edge(v, v.tensor_scalar(shtmp2[:, 0:16], pa[:, 0:16], srowm[:, 1:2], None, ALU.is_ge))
            p.edge(v, v.tensor_sub(shtmp2[:, 0:16], shtmp2[:, 0:16], scmb[:, :]))
            p.edge(v, v.tensor_scalar(srowm[:, 4:5], srowm[:, 3:4], -1.0, 1.0, ALU.mult, ALU.add))
            v.tensor_scalar(scmb[:, :], scmb[:, :], srowm[:, 3:4], None, ALU.mult)
            p.edge(v, v.tensor_scalar(shtmp2[:, 0:16], shtmp2[:, 0:16], srowm[:, 4:5], None, ALU.mult))
            p.edge(v, v.tensor_add(scmb[:, :], scmb[:, :], shtmp2[:, 0:16]))
            p.inc(v.tensor_copy(scmbb[:, :], scmb[:, :]), "rv2", 1)

        block.vector(r_v2)
        p.bank_release("pa", "rv2")

        def r_tr(t, tt=tt):
            p.wait(t, "rv2")
            p.bank_acquire(t, "pbt")
            p.inc(t.transpose(pbt8[0:16, 0:128], scmbb[:, :], seyeb[:, :]), "rtr", 1)

        block.tensor(r_tr)

        def r_ev(v, tt=tt):
            p.wait(v, "rtr")
            p.inc(v.tensor_copy(scr[:, tt * 128:(tt + 1) * 128], pbt8[0:16, 0:128]),
                  "rev", 1)

        block.vector(r_ev)
        p.bank_release("pbt", "rev")

    GEXP = int(_os.environ.get("GEXP", "16"))
    GSH = _os.environ.get("GSH", "1") == "1"
    dnv_ckpt = {}
    for e in range(GEXP):
        sgu = sgu2[e % 2]
        sdn = sdn2[e % 2]
        dnv_ckpt[e] = p.val("dnv")

        def ld_w(g, e=e, sgu=sgu, sdn=sdn):
            if e >= 2:
                p.wait(g, "dnv", dnv_ckpt[e - 1])
            elif e == 0:
                p.wait(g, "rev")
            p.inc(g.dma_start(out=sgu.rearrange("p (d n) -> p d n", d=NDT),
                              in_=GU[e].rearrange("(d p) n -> p d n", p=128)), "wh", 16)
            p.inc(g.dma_start(out=sdn.rearrange("p (c n) -> p c n", c=4),
                              in_=DN[e].rearrange("(c p) n -> p c n", p=128)), "wh", 16)

        block.gpsimd(ld_w)

        for sl in range(NSL):
            def rbc_mm(t, e=e, sl=sl):
                if sl == 0:
                    p.wait(t, "wh")
                p.bank_acquire(t, "pe_")
                p.inc(t.matmul(pe_[:, 0:SLAB], ssel[:, e * 128:(e + 1) * 128],
                               scr[:, sl * SLAB:(sl + 1) * SLAB],
                               start=True, stop=True), "rbc", 1)

            block.tensor(rbc_mm)

            for mp in range(2):
                GB = [("pc", pc), ("pd", pd)]
                UB = [("pe2", pe_2), ("pf", pf)]

                def gu_mm(t, mp=mp, sl=sl):
                    for mi in range(2):
                        m = mp * 2 + mi
                        p.bank_acquire(t, GB[mi][0])
                        for d in range(NDT):
                            t.matmul(GB[mi][1][:, 0:SLAB],
                                     sgu[:, d * 1024 + m * 128:d * 1024 + (m + 1) * 128],
                                     sx3b[:, d * SH + sl * SLAB:d * SH + (sl + 1) * SLAB],
                                     start=(d == 0), stop=(d == NDT - 1))
                        p.bank_acquire(t, UB[mi][0])
                        for d in range(NDT):
                            ins = t.matmul(UB[mi][1][:, 0:SLAB],
                                           sgu[:, d * 1024 + 512 + m * 128:d * 1024 + 512 + (m + 1) * 128],
                                           sx3b[:, d * SH + sl * SLAB:d * SH + (sl + 1) * SLAB],
                                           start=(d == 0), stop=(d == NDT - 1))
                    p.inc(ins, "gmm", 1)

                block.tensor(gu_mm)

                def gu_ep(s, mp=mp):
                    p.wait(s, "gmm")
                    p.wait(s, "gsc", p.val("gsc"))
                    for mi in range(2):
                        s.activation(shtmp[:, mi * SLAB:(mi + 1) * SLAB],
                                     GB[mi][1][:, 0:SLAB], AF.Silu)
                        ins = s.activation(shtmp2[:, mi * SLAB:(mi + 1) * SLAB],
                                           UB[mi][1][:, 0:SLAB], AF.Copy)
                    p.inc(ins, "gep", 1)

                block.scalar(gu_ep)
                for mi in range(2):
                    p.bank_release(GB[mi][0], "gep")
                    p.bank_release(UB[mi][0], "gep")

                def gu_v(v, mp=mp):
                    p.wait(v, "gep")
                    for mi in range(2):
                        v.tensor_mul(shtmp[:, mi * SLAB:(mi + 1) * SLAB],
                                     shtmp[:, mi * SLAB:(mi + 1) * SLAB],
                                     shtmp2[:, mi * SLAB:(mi + 1) * SLAB])
                    p.edge(v, v.engine_nop())
                    for mi in range(2):
                        m = mp * 2 + mi
                        ins = v.tensor_mul(sh_[:, m * SLAB:(m + 1) * SLAB],
                                           shtmp[:, mi * SLAB:(mi + 1) * SLAB],
                                           pe_[:, 0:SLAB])
                    p.inc(ins, "gsc", 1)

                block.vector(gu_v)

            p.bank_release("pe_", "gsc")

            for i in range(NDT):
                bank = ["pa", "pb"][i % 2]
                pgb = {"pa": pa, "pb": pb}[bank]

                def dn_mm(t, i=i, bank=bank, pgb=pgb):
                    if i == 0:
                        p.wait(t, "gsc")
                    p.bank_acquire(t, bank)
                    for c in range(4):
                        ins = t.matmul(pgb[:, 0:SLAB],
                                       sdn[:, c * D + i * 128:c * D + (i + 1) * 128],
                                       sh_[:, c * SLAB:(c + 1) * SLAB],
                                       start=(c == 0), stop=(c == 3))
                    p.inc(ins, "dmm", 1)

                block.tensor(dn_mm)

                def dn_v(v, e=e, i=i, sl=sl, bank=bank, pgb=pgb):
                    p.wait(v, "dmm")
                    seg = slice(sl * NDT * SLAB + i * SLAB, sl * NDT * SLAB + (i + 1) * SLAB)
                    if e == 0:
                        ins = v.tensor_copy(sacc[:, seg], pgb[:, 0:SLAB])
                    else:
                        ins = v.tensor_add(sacc[:, seg], sacc[:, seg], pgb[:, 0:SLAB])
                    p.inc(ins, "dnv", 1)

                block.vector(dn_v)
                p.bank_release(bank, "dnv")

    GSHN = int(_os.environ.get("GSHN", "2")) if GSH else 0
    for half in range(GSHN):
        sgu = sgu2[half]
        sdn = sdn2[half]

        def ld_sh(g, half=half, sgu=sgu, sdn=sdn):
            if half == 0:
                p.wait(g, "dnv", dnv_ckpt[GEXP - 1])
            p.wait(g, "dmm" if half == 1 else "shd", p.val("dmm" if half == 1 else "shd"))
            p.wait(g, "shd")
            p.inc(g.dma_start(
                out=sgu.rearrange("p (d n) -> p d n", d=NDT)[:, :, 0:512],
                in_=shGU.rearrange("(d p) n -> p d n", p=128)[:, :, half * 512:(half + 1) * 512]),
                "wsh", 16)
            p.inc(g.dma_start(
                out=sgu.rearrange("p (d n) -> p d n", d=NDT)[:, :, 512:1024],
                in_=shGU.rearrange("(d p) n -> p d n", p=128)[:, :, 1024 + half * 512:1024 + (half + 1) * 512]),
                "wsh", 16)
            p.inc(g.dma_start(
                out=sdn.rearrange("p (c n) -> p c n", c=4),
                in_=shDN.rearrange("(c p) n -> p c n", p=128)[:, half * 4:(half + 1) * 4, :]),
                "wsh", 16)

        block.gpsimd(ld_sh)

        for sl in range(NSL):
            for mp in range(2):
                GB = [("pc", pc), ("pd", pd)]
                UB = [("pe2", pe_2), ("pf", pf)]

                def shg_mm(t, mp=mp, sl=sl, half=half):
                    if mp == 0 and sl == 0:
                        p.wait(t, "wsh")
                    for mi in range(2):
                        m = mp * 2 + mi
                        p.bank_acquire(t, GB[mi][0])
                        for d in range(NDT):
                            t.matmul(GB[mi][1][:, 0:SLAB],
                                     sgu[:, d * 1024 + m * 128:d * 1024 + (m + 1) * 128],
                                     sx3b[:, d * SH + sl * SLAB:d * SH + (sl + 1) * SLAB],
                                     start=(d == 0), stop=(d == NDT - 1))
                        p.bank_acquire(t, UB[mi][0])
                        for d in range(NDT):
                            ins = t.matmul(UB[mi][1][:, 0:SLAB],
                                           sgu[:, d * 1024 + 512 + m * 128:d * 1024 + 512 + (m + 1) * 128],
                                           sx3b[:, d * SH + sl * SLAB:d * SH + (sl + 1) * SLAB],
                                           start=(d == 0), stop=(d == NDT - 1))
                    p.inc(ins, "shm", 1)

                block.tensor(shg_mm)

                def shg_ep(s, mp=mp):
                    p.wait(s, "shm")
                    p.wait(s, "shsc", p.val("shsc"))
                    for mi in range(2):
                        s.activation(shtmp[:, mi * SLAB:(mi + 1) * SLAB],
                                     GB[mi][1][:, 0:SLAB], AF.Silu)
                        ins = s.activation(shtmp2[:, mi * SLAB:(mi + 1) * SLAB],
                                           UB[mi][1][:, 0:SLAB], AF.Copy)
                    p.inc(ins, "shep", 1)

                block.scalar(shg_ep)
                for mi in range(2):
                    p.bank_release(GB[mi][0], "shep")
                    p.bank_release(UB[mi][0], "shep")

                def shg_v(v, mp=mp):
                    p.wait(v, "shep")
                    for mi in range(2):
                        m = mp * 2 + mi
                        ins = v.tensor_mul(sh_[:, m * SLAB:(m + 1) * SLAB],
                                           shtmp[:, mi * SLAB:(mi + 1) * SLAB],
                                           shtmp2[:, mi * SLAB:(mi + 1) * SLAB])
                    p.inc(ins, "shsc", 1)

                block.vector(shg_v)

            for i in range(NDT):
                bank = ["pa", "pb"][i % 2]
                pgb = {"pa": pa, "pb": pb}[bank]

                def shd_mm(t, i=i, bank=bank, pgb=pgb):
                    if i == 0:
                        p.wait(t, "shsc")
                    p.bank_acquire(t, bank)
                    for c in range(4):
                        ins = t.matmul(pgb[:, 0:SLAB],
                                       sdn[:, c * D + i * 128:c * D + (i + 1) * 128],
                                       sh_[:, c * SLAB:(c + 1) * SLAB],
                                       start=(c == 0), stop=(c == 3))
                    p.inc(ins, "shd", 1)

                block.tensor(shd_mm)

                def shd_v(v, i=i, sl=sl, bank=bank, pgb=pgb):
                    p.wait(v, "shd")
                    seg = slice(sl * NDT * SLAB + i * SLAB, sl * NDT * SLAB + (i + 1) * SLAB)
                    p.inc(v.tensor_add(sacc[:, seg], sacc[:, seg], pgb[:, 0:SLAB]), "dnv", 1)

                block.vector(shd_v)
                p.bank_release(bank, "dnv")

    if GDBG:
        def dbg_g(g):
            p.wait(g, "dnv")
            p.wait(g, "rev")
            p.inc(g.dma_start(out=o_cmb, in_=scr[:, :]), "gdbg", 16)
            p.inc(g.dma_start(out=o_acc, in_=sacc[:, :]), "gdbg", 16)
        block.gpsimd(dbg_g)

    for sl in range(NSL):
        col0 = sl * SLAB

        def ld_xs(g, sl=sl, col0=col0):
            p.wait(g, "lnb", p.val("lnb"))
            p.wait(g, "x4st", p.val("x4st"))
            p.inc(g.dma_start(out=sxs.rearrange("p (d n) -> p d n", d=NDT),
                              in_=x3T[:, col0:col0 + SLAB].rearrange("(d p) n -> p d n", p=128)),
                  "xsl", 16)

        block.sync(ld_xs)

        def vsum(v, sl=sl):
            p.wait(v, "dnv")
            p.wait(v, "xsl")
            for i in range(NDT):
                seg = slice(sl * NDT * SLAB + i * SLAB, sl * NDT * SLAB + (i + 1) * SLAB)
                ins = v.tensor_add(sxs[:, i * SLAB:(i + 1) * SLAB],
                                   sxs[:, i * SLAB:(i + 1) * SLAB], sacc[:, seg])
            p.inc(ins, "vsum", 1)

        block.vector(vsum)

        def lnw(s):
            p.wait(s, "vsum")
            s.activation(swb[:, :], sxs[:, :], AF.Copy)
            p.inc(s.activation(swb2[:, :], sxs[:, :], AF.Square), "lnc", 1)

        block.scalar(lnw)

        def lnstat(t):
            p.wait(t, "lnc")
            p.bank_acquire(t, "pa")
            for i in range(NDT):
                t.matmul(pa[0:1, 0:SLAB], sones[:, :], swb[:, i * SLAB:(i + 1) * SLAB],
                         start=(i == 0), stop=(i == NDT - 1))
            p.bank_acquire(t, "pb")
            for i in range(NDT):
                ins = t.matmul(pb[0:1, 0:SLAB], sones[:, :], swb2[:, i * SLAB:(i + 1) * SLAB],
                               start=(i == 0), stop=(i == NDT - 1))
            p.inc(ins, "lns", 1)

        block.tensor(lnstat)

        def lnrow(v):
            p.wait(v, "lns")
            p.edge(v, v.tensor_scalar(srow[:, 0:SLAB], pa[0:1, 0:SLAB], 1.0 / D, None, ALU.mult))
            p.edge(v, v.tensor_scalar(srow[:, SLAB:2 * SLAB], pb[0:1, 0:SLAB], 1.0 / D, None, ALU.mult))
            p.edge(v, v.tensor_mul(srow[:, 2 * SLAB:3 * SLAB], srow[:, 0:SLAB], srow[:, 0:SLAB]))
            p.edge(v, v.tensor_sub(srow[:, SLAB:2 * SLAB], srow[:, SLAB:2 * SLAB],
                                   srow[:, 2 * SLAB:3 * SLAB]))
            p.inc(v.tensor_scalar(srow[:, SLAB:2 * SLAB], srow[:, SLAB:2 * SLAB], 1.0,
                                  EPS_LN, ALU.mult, ALU.add), "lnr", 1)

        block.vector(lnrow)
        p.bank_release("pa", "lnr")
        p.bank_release("pb", "lnr")

        def lnsq(s):
            p.wait(s, "lnr")
            p.edge(s, s.activation(srow[:, SLAB:2 * SLAB], srow[:, SLAB:2 * SLAB],
                                   AF.Ln), "ssy")
            p.inc(s.activation(srow[:, SLAB:2 * SLAB], srow[:, SLAB:2 * SLAB],
                               AF.Exp, scale=-0.5), "lnq", 1)

        block.scalar(lnsq)

        def lnrec(v):
            p.wait(v, "lnq")
            p.inc(v.tensor_mul(srow[:, 2 * SLAB:3 * SLAB], srow[:, 0:SLAB],
                               srow[:, SLAB:2 * SLAB]), "lnm", 1)

        block.vector(lnrec)

        def lnbc(t):
            p.wait(t, "lnm")
            p.bank_acquire(t, "pc")
            t.matmul(pc[:, 0:SLAB], sonesrf[:, :], srow[:, SLAB:2 * SLAB],
                     start=True, stop=True)
            p.bank_acquire(t, "pd")
            p.inc(t.matmul(pd[:, 0:SLAB], sonesrf[:, :], srow[:, 2 * SLAB:3 * SLAB],
                           start=True, stop=True), "lnw2", 1)

        block.tensor(lnbc)

        def x4_v(v):
            p.wait(v, "lnw2")
            for i in range(NDT):
                v.tensor_mul(sxs[:, i * SLAB:(i + 1) * SLAB],
                             sxs[:, i * SLAB:(i + 1) * SLAB], pc[:, 0:SLAB])
            p.edge(v, v.engine_nop())
            for i in range(NDT):
                ins = v.tensor_sub(sxs[:, i * SLAB:(i + 1) * SLAB],
                                   sxs[:, i * SLAB:(i + 1) * SLAB], pd[:, 0:SLAB])
            p.inc(ins, "lnb", 1)

        block.vector(x4_v)
        p.bank_release("pc", "lnb")
        p.bank_release("pd", "lnb")

        def x4_store(g, col0=col0):
            p.wait(g, "lnb")
            p.inc(g.dma_start(
                out=o_x4[:, col0:col0 + SLAB].rearrange("(d p) n -> p d n", p=128),
                in_=sxs.rearrange("p (d n) -> p d n", d=NDT)), "x4st", 16)

        block.gpsimd(x4_store)

    def done(g):
        p.wait(g, "x4st")

    block.gpsimd(done)


def build_launch_G():
    p = Prog()
    nc = p.nc
    aps = dict(
        x3T=nc.dram_tensor("x3T", [D, SH], F32, kind="ExternalInput").ap(),
        wrT=nc.dram_tensor("wrT", [D, 16], F32, kind="ExternalInput").ap(),
        rbias=nc.dram_tensor("rbias", [1, 16], F32, kind="ExternalInput").ap(),
        GU=nc.dram_tensor("GU", [16, D, 1024], F32, kind="ExternalInput").ap(),
        DN=nc.dram_tensor("DN", [16, 512, D], F32, kind="ExternalInput").ap(),
        shGU=nc.dram_tensor("shGU", [D, 2048], F32, kind="ExternalInput").ap(),
        shDN=nc.dram_tensor("shDN", [1024, D], F32, kind="ExternalInput").ap(),
        sel16F=nc.dram_tensor("sel16F", [16, 16 * 128], F32, kind="ExternalInput").ap(),
        eyeF=nc.dram_tensor("eyeF", [128, 128], F32, kind="ExternalInput").ap(),
        o_x4=nc.dram_tensor("o_x4", [D, SH], F32, kind="ExternalOutput").ap(),
    )
    p.begin()
    emit_G(p, **aps)
    p.close()
    return p.nc



def build_fused():
    import os as _os
    FPARTS = int(_os.environ.get("FPARTS", "99"))
    p = Prog()
    nc = p.nc

    def ein(name, shape):
        return nc.dram_tensor(name, shape, F32, kind="ExternalInput").ap()

    def dint(name, shape):
        return nc.dram_tensor(name, shape, F32).ap()

    ex = dict(
        xT=ein("xT", [D, SH + 3]),
        xTC=ein("xTC", [D, SH]),
        w_gate=ein("w_gate", [D, D]),
        w_stin=ein("w_stin", [D, KSS * D]),
        w_stout=ein("w_stout", [KSS * D, D]),
        w_z36=ein("w_z36", [D, 36]),
        w_z2=ein("w_z2", [D, 2]),
        b_z2=ein("b_z2", [2, 1]),
        w_qkvg=ein("w_qkvg", [D, 97]),
        wgb_col=ein("wgb_col", [128, 1]),
        conv_w=ein("conv_w", [128, NDT * 4]),
        gate_b=ein("gate_b", [128, NDT]),
        logit_bc=ein("logit_bc", [128, KSS]),
        codesF=ein("codesF", [9, 512]),
        suF=ein("suF", [128, 128]),
        suiF=ein("suiF", [128, 128]),
        thrF=ein("thrF", [128, 128]),
        altF=ein("altF", [128, 128]),
        bndF=ein("bndF", [128, 132]),
        eyeF=ein("eyeF", [128, 128]),
        owns_bc=ein("owns_bc", [32, 64]),
        w_mo=ein("w_mo", [32, D]),
        w_up=ein("w_up", [D, 4 * D]),
        w_dn=ein("w_dn", [4 * D, D]),
        b_up=ein("b_up", [128, 32]),
        b_dn=ein("b_dn", [1, D]),
        logit_row=ein("logit_row", [1, KSS]),
        ecp_row=ein("ecp_row", [1, 4]),
        mask44=ein("mask44", [4, 4]),
        iotap1=ein("iotap1", [1, SH]),
        W1=ein("W1", [8, D, 4 * D]),
        b1=ein("b1", [128, 8 * 32]),
        W2=ein("W2", [8, 4 * D, D]),
        b2=ein("b2", [8, D]),
        Wr=ein("Wr", [D, 8]),
        br=ein("br", [1, 8]),
        sel8F=ein("sel8F", [8, 8 * 128]),
        wkv=ein("wkv", [D, 128]),
        wqT=ein("wqT", [D, 64]),
        woT=ein("woT", [64, D]),
        wrT=ein("wrT", [D, 16]),
        rbias=ein("rbias", [1, 16]),
        GU=ein("GU", [16, D, 1024]),
        DN=ein("DN", [16, 512, D]),
        shGU=ein("shGU", [D, 2048]),
        shDN=ein("shDN", [1024, D]),
        sel16F=ein("sel16F", [16, 16 * 128]),
    )
    o_x4 = nc.dram_tensor("o_x4", [D, SH], F32, kind="ExternalOutput").ap()

    h_signs = dint("h_signs", [36, SH])
    h_qkv = dint("h_qkv", [SH, 97])
    h_zg = dint("h_zg", [128, 16])
    h_fin = dint("h_fin", [32, 128])
    h_long = dint("h_long", [D, SH])
    h_glr = dint("h_glr", [1, SH])
    h_gmr = dint("h_gmr", [1, SH])
    g_signs = dint("g_signs", [144, SH])
    g_qkv = dint("g_qkv", [T, 97])
    g_zg = dint("g_zg", [512, 16])
    g_fin = dint("g_fin", [128, 128])
    h_readT = dint("h_readT", [32, T])
    h_x1 = dint("h_x1", [D, SH])
    h_x2 = dint("h_x2", [D, SH])
    h_x3 = dint("h_x3", [D, SH])

    p.begin()

    p.push_scope("A")
    emit_A(p, xT=ex["xT"], w_gate=ex["w_gate"], w_stin=ex["w_stin"],
           w_stout=ex["w_stout"], w_z36=ex["w_z36"], w_z2=ex["w_z2"],
           b_z2=ex["b_z2"], w_qkvg=ex["w_qkvg"], wgb_col=ex["wgb_col"],
           conv_w=ex["conv_w"], gate_b=ex["gate_b"], logit_bc=ex["logit_bc"],
           o_signs=h_signs, o_glr=h_glr, o_gmr=h_gmr, o_qkvg=h_qkv,
           o_zgpc=h_zg, o_long=h_long, o_finT=h_fin)
    p.pop_scope()

    grp = [[0, 1, 2, 3], [4, 5, 6, 7]]

    def ag(g):
        for i, (src_, dst_) in enumerate([(h_signs, g_signs), (h_qkv, g_qkv),
                                          (h_zg, g_zg), (h_fin, g_fin)]):
            p.inc(g.collective_compute("AllGather", ALU.bypass,
                                       replica_groups=grp,
                                       ins=[src_.opt()], outs=[dst_.opt()]),
                  f"ccg{i}", 1)
        for i in range(4):
            p.wait(g, f"ccg{i}", 1)

    p.block.gpsimd(ag)
    p.barrier()

    if FPARTS >= 2:
        p.push_scope("B")
        emit_B(p, sgath=g_signs, qgath=g_qkv, zgath=g_zg, codesF=ex["codesF"],
               suF=ex["suF"], suiF=ex["suiF"], thrF=ex["thrF"], altF=ex["altF"],
               bndF=ex["bndF"], eyeF=ex["eyeF"], o_readT=h_readT)
        p.pop_scope()
        p.barrier()

    if FPARTS >= 3:
        p.push_scope("C")
        emit_C(p, xT=ex["xTC"], longT=h_long, finAG=g_fin, gl_row=h_glr,
               gm_row=h_gmr, readTF=h_readT, owns_bc=ex["owns_bc"],
               w_stout=ex["w_stout"], w_mo=ex["w_mo"], w_up=ex["w_up"],
               w_dn=ex["w_dn"], b_up=ex["b_up"], b_dn=ex["b_dn"],
               logit_row=ex["logit_row"], ecp_row=ex["ecp_row"],
               mask44=ex["mask44"], iotap1=ex["iotap1"], o_x1=h_x1)
        p.pop_scope()
        p.barrier()

    if FPARTS >= 4:
        p.push_scope("D")
        emit_D(p, x1T=h_x1, W1=ex["W1"], b1=ex["b1"], W2=ex["W2"], b2=ex["b2"],
               Wr=ex["Wr"], br=ex["br"], sel8F=ex["sel8F"], eyeF=ex["eyeF"],
               o_x2=h_x2)
        p.pop_scope()
        p.barrier()

    if FPARTS >= 5:
        p.push_scope("E")
        emit_E(p, x2T=h_x2, wkv=ex["wkv"], wqT=ex["wqT"], woT=ex["woT"], o_x3=h_x3)
        p.pop_scope()
        p.barrier()

    if FPARTS >= 6:
        p.push_scope("G")
        emit_G(p, x3T=h_x3, wrT=ex["wrT"], rbias=ex["rbias"], GU=ex["GU"],
               DN=ex["DN"], shGU=ex["shGU"], shDN=ex["shDN"], sel16F=ex["sel16F"],
               eyeF=ex["eyeF"], o_x4=o_x4)
        p.pop_scope()

    p.close()
    return p.nc



def _host_consts_B():
    C = 128
    f = np.float32
    codes = np.where((np.arange(512)[None, :] >> np.arange(9)[:, None]) & 1 > 0,
                     1.0, -1.0).astype(f)
    su = np.triu(np.ones((C, C), f), 1)
    sui = np.triu(np.ones((C, C), f), 0)
    thr = np.broadcast_to(np.arange(1, 129, dtype=f)[None, :], (C, C)).copy()
    alt = np.broadcast_to(np.where(np.arange(128) % 2 == 0, 1.0, -1.0).astype(f)[None, :],
                          (C, C)).copy()
    bnd = np.broadcast_to((np.arange(132, dtype=f) * QLVL)[None, :], (C, 132)).copy()
    eye = np.eye(C, dtype=f)
    return dict(codesF=codes, suF=su, suiF=sui, thrF=thr, altF=alt, bndF=bnd, eyeF=eye)


def kernel(**inputs):
    f = np.float32
    inputs = {k: np.asarray(v) for k, v in inputs.items()}
    nc = build_fused()

    sel8 = np.zeros((8, 8 * 128), f)
    for e in range(8):
        sel8[e, e * 128:(e + 1) * 128] = 1.0
    sel16 = np.zeros((16, 16 * 128), f)
    for e in range(16):
        sel16[e, e * 128:(e + 1) * 128] = 1.0
    GUw = np.ascontiguousarray(
        np.concatenate([inputs["L1_ffn_gate_proj"], inputs["L1_ffn_up_proj"]], 2), dtype=f)
    shGU = np.ascontiguousarray(
        np.concatenate([inputs["L1_ffn_sh_gate"].T, inputs["L1_ffn_sh_up"].T], 1), dtype=f)
    wkv = np.ascontiguousarray(
        np.concatenate([inputs["L1_attn_wk"].T, inputs["L1_attn_wv"].T], 1), dtype=f)
    consts = _host_consts_B()

    shared = {
        "xTC": None,
        "w_mo": np.ascontiguousarray(inputs["L0_attn_mem_out_w"].T, dtype=f),
        "w_up": np.ascontiguousarray(inputs["L0_attn_mlp_up_w"].T, dtype=f),
        "w_dn": np.ascontiguousarray(inputs["L0_attn_mlp_down_w"].T, dtype=f),
        "b_up": np.ascontiguousarray(inputs["L0_attn_mlp_up_b"].reshape(32, 128).T, dtype=f),
        "b_dn": inputs["L0_attn_mlp_down_b"].reshape(1, D).astype(f),
        "logit_row": inputs["L0_attn_state_decay_logit"].reshape(1, 4).astype(f),
        "iotap1": (np.arange(SH, dtype=f) + 1.0).reshape(1, SH),
        "W1": np.ascontiguousarray(inputs["L0_ffn_W1"], dtype=f),
        "b1": np.ascontiguousarray(
            inputs["L0_ffn_b1"].reshape(8, 32, 128).transpose(2, 0, 1).reshape(128, 256), dtype=f),
        "W2": np.ascontiguousarray(inputs["L0_ffn_W2"], dtype=f),
        "b2": np.ascontiguousarray(inputs["L0_ffn_b2"], dtype=f),
        "Wr": np.ascontiguousarray(inputs["L0_ffn_Wr"], dtype=f),
        "br": inputs["L0_ffn_br"].reshape(1, 8).astype(f),
        "sel8F": sel8,
        "wkv": wkv,
        "wqT": np.ascontiguousarray(inputs["L1_attn_wq"].T, dtype=f),
        "woT": np.ascontiguousarray(inputs["L1_attn_wo"].T, dtype=f),
        "wrT": np.ascontiguousarray(inputs["L1_ffn_router_w"].T, dtype=f),
        "rbias": inputs["L1_ffn_router_bias"].reshape(1, 16).astype(f),
        "GU": GUw,
        "DN": np.ascontiguousarray(inputs["L1_ffn_down_proj"], dtype=f),
        "shGU": shGU,
        "shDN": np.ascontiguousarray(inputs["L1_ffn_sh_down"].T, dtype=f),
        "sel16F": sel16,
    }
    shared.update(consts)

    in_maps = []
    for c in range(NCORE):
        b, pos = c // 4, c % 4
        t0 = pos * SH
        x = inputs["L0_attn_x"][b].astype(f)
        m = dict(host_inputs_A(inputs, c))
        m.update(shared)
        m["xTC"] = np.ascontiguousarray(x[t0:t0 + SH].T)
        ecp = np.zeros((1, 4), f)
        mask = np.zeros((4, 4), f)
        for cp in range(4):
            if cp < pos:
                ecp[0, cp] = SH * (pos - 1 - cp)
                mask[cp, :] = 1.0
        m["ecp_row"] = ecp
        m["mask44"] = mask
        owns = np.zeros((32, 64), f)
        owns[:, pos * 16:(pos + 1) * 16] = 1.0
        m["owns_bc"] = owns
        in_maps.append(m)

    res = run_bass_kernel_spmd(nc, in_maps, core_ids=list(range(NCORE)))
    out = np.empty((B, T, D), f)
    for c in range(NCORE):
        b, pos = c // 4, c % 4
        out[b, pos * SH:(pos + 1) * SH, :] = res.results[c]["o_x4"].T
    return out
